# version 12
# baseline (speedup 1.0000x reference)
import math
import numpy as np
import ml_dtypes
from contextlib import ExitStack

import os
import concourse.bass as bass
import concourse.mybir as mybir
from concourse.bass_utils import run_bass_kernel_spmd

F32 = mybir.dt.float32
BF16 = mybir.dt.bfloat16
I32 = mybir.dt.int32
AF = mybir.ActivationFunctionType
ALU = mybir.AluOpType
AXX = mybir.AxisListType.X

B, T, C, H, D, DH = 4, 4096, 2048, 16, 128, 64
II = 8192
NCORE = 8
NTOK = 2048
NSLOT = 16
NEG = -1.0e9
EPS = 1.0e-5


class Sched:
    ENGS = ("t", "v", "s", "g", "y")

    def __init__(self, nc, es):
        self.nc = nc
        self.es = es
        self.q = {k: [] for k in self.ENGS}
        self.sem = {}
        self.cnt = {}
        self.bar_mark = {}
        for name in ("T", "V", "S"):
            self._mk(name)

    def _mk(self, name):
        self.sem[name] = self.es.enter_context(self.nc.semaphore(name))
        self.cnt[name] = 0
        self.bar_mark[name] = 0

    def op(self, eng, semname, fn, inc=1):
        self.cnt[semname] += inc
        sem = self.sem[semname]

        def play(E, fn=fn, sem=sem, inc=inc):
            fn(E).then_inc(sem, inc)

        self.q[eng].append(play)
        return self.cnt[semname]

    def opn(self, eng, fn):
        self.q[eng].append(fn)

    def wait(self, eng, semname, val):
        if val <= 0:
            return
        sem = self.sem[semname]
        self.q[eng].append(lambda E, sem=sem, val=val: E.wait_ge(sem, val))

    def dma(self, eng, slot, out, in_):
        if slot not in self.sem:
            self._mk(slot)
        return self.op(eng, slot,
                       lambda E, out=out, in_=in_: E.dma_start(out=out, in_=in_),
                       inc=16)

    def barrier(self):
        todo = [(n, v) for n, v in self.cnt.items() if v > self.bar_mark[n]]
        for e in self.ENGS:
            for sname, val in todo:
                self.wait(e, sname, val)
        for n, v in self.cnt.items():
            self.bar_mark[n] = v

    def flush(self):
        with self.nc.Block() as block:
            def mk(ops):
                def body(E):
                    for f in ops:
                        f(E)
                return body
            if self.q["t"]:
                block.tensor(mk(self.q["t"]))
            if self.q["v"]:
                block.vector(mk(self.q["v"]))
            if self.q["s"]:
                block.scalar(mk(self.q["s"]))
            if self.q["g"]:
                block.gpsimd(mk(self.q["g"]))
            if self.q["y"]:
                block.sync(mk(self.q["y"]))
        self.q = {k: [] for k in self.ENGS}


def build_program(stop=99):
    nc = bass.Bass("TRN2", target_bir_lowering=False)
    dt = nc.dram_tensor
    xbT = dt("xbT", [C, T], F32, kind="ExternalInput").ap()
    xown = dt("xown", [NTOK, C], F32, kind="ExternalInput").ap()
    xownT = dt("xownT", [C, NTOK], F32, kind="ExternalInput").ap()
    wqT = dt("wqT", [C, C], F32, kind="ExternalInput").ap()
    wkT = dt("wkT", [C, C], F32, kind="ExternalInput").ap()
    wvT_in = dt("wvT_in", [C, C], F32, kind="ExternalInput").ap()
    woT_in = dt("woT_in", [C, C], F32, kind="ExternalInput").ap()
    bq_r = dt("bq_r", [128, H], F32, kind="ExternalInput").ap()
    bk_r = dt("bk_r", [128, H], F32, kind="ExternalInput").ap()
    bv_bc_in = dt("bv_bc", [128, C], F32, kind="ExternalInput").ap()
    bo_bc_in = dt("bo_bc", [128, C], F32, kind="ExternalInput").ap()
    gwT = dt("gwT", [C, II], I32, kind="ExternalInput").ap()
    uwT = dt("uwT", [C, II], I32, kind="ExternalInput").ap()
    dwM = dt("dwM", [128, 64, C], I32, kind="ExternalInput").ap()
    gsc_in = dt("gsc", [128, 64], F32, kind="ExternalInput").ap()
    usc_in = dt("usc", [128, 64], F32, kind="ExternalInput").ap()
    dsbc_in = dt("dsbc", [128, C], F32, kind="ExternalInput").ap()
    ckA_in = dt("ckA", [128, T], F32, kind="ExternalInput").ap()
    ckB_in = dt("ckB", [128, T], F32, kind="ExternalInput").ap()
    cqA_in = dt("cqA", [128, NTOK], F32, kind="ExternalInput").ap()
    cqB_in = dt("cqB", [128, NTOK], F32, kind="ExternalInput").ap()
    m1_in = dt("m1_in", [128, 64], F32, kind="ExternalInput").ap()
    m2_in = dt("m2_in", [128, 64], F32, kind="ExternalInput").ap()
    ident_in = dt("ident_in", [128, 128], F32, kind="ExternalInput").ap()
    masks_in = dt("masks_in", [128, NSLOT * 256], F32, kind="ExternalInput").ap()
    masksT_in = dt("masksT_in", [128, NSLOT * 256], F32, kind="ExternalInput").ap()
    out = dt("out", [NTOK, C], F32, kind="ExternalOutput").ap()

    k_sp = dt("k_sp", [H, 128, T], BF16, kind="Internal").ap()
    q_sp = dt("q_sp", [H, 128, NTOK], BF16, kind="Internal").ap()
    v_sp = dt("v_sp", [T, C], BF16, kind="Internal").ap()
    av_sp = dt("av_sp", [H, 128, NTOK], BF16, kind="Internal").ap()
    h_sp = dt("h_sp", [4, 128, 64 * 512], BF16, kind="Internal").ap()
    dbg_sp = dt("dbg_sp", [128, 8192], F32, kind="Internal").ap()
    y1tm_sp = dt("y1tm_sp", [16, 128, C], BF16, kind="Internal").ap()
    x_bf = dt("x_bf", [C, T], BF16, kind="Internal").ap()
    y2_sp = dt("y2_sp", [16, 128, C], BF16, kind="Internal").ap()

    eso = ExitStack()
    EO = eso.enter_context
    sc = Sched(nc, eso)

    ident = EO(nc.sbuf_tensor("ident", [128, 128], BF16))
    identf = EO(nc.sbuf_tensor("identf", [128, 128], F32))
    m1 = EO(nc.sbuf_tensor("m1", [128, 64], BF16))
    m2 = EO(nc.sbuf_tensor("m2", [128, 64], BF16))
    allones = EO(nc.sbuf_tensor("allones", [128, 128], BF16))
    allzeros = EO(nc.sbuf_tensor("allzeros", [128, 128], BF16))
    bqr_sb = EO(nc.sbuf_tensor("bqr_sb", [128, H], F32))
    bkr_sb = EO(nc.sbuf_tensor("bkr_sb", [128, H], F32))
    gsc_sb = EO(nc.sbuf_tensor("gsc_sb", [128, 64], F32))
    usc_sb = EO(nc.sbuf_tensor("usc_sb", [128, 64], F32))
    gbias_sb = EO(nc.sbuf_tensor("gbias_sb", [128, 64], F32))
    ubias_sb = EO(nc.sbuf_tensor("ubias_sb", [128, 64], F32))
    eps_sb = EO(nc.sbuf_tensor("eps_sb", [128, 1], F32))

    for dst, src in ((ident, ident_in), (m1, m1_in), (m2, m2_in)):
        sc.dma("g", "c0g", dst[:], src)
    sc.dma("y", "c0y", identf[:], ident_in)
    for dst, src in ((bqr_sb, bq_r), (bkr_sb, bk_r), (gsc_sb, gsc_in),
                     (usc_sb, usc_in)):
        sc.dma("y", "c0y", dst[:], src)
    sc.wait("v", "c0g", sc.cnt["c0g"])
    sc.wait("v", "c0y", sc.cnt["c0y"])
    sc.opn("v", lambda E: E.memset(allones[:], 1.0))
    sc.opn("v", lambda E: E.memset(allzeros[:], 0.0))
    for dst, src in ((gbias_sb, gsc_sb), (ubias_sb, usc_sb)):
        sc.opn("v", lambda E, d=dst, s=src: E.tensor_scalar_mul(d[:], s[:], -128.0))
    sc.op("v", "V", lambda E: E.memset(eps_sb[:], EPS))
    sc.barrier()
    sc.flush()

    def phase_proj_rope(wT_in, b_sb, cA_in, cB_in, xT_in, ntok, dst_sp, tag):
        es = ExitStack()
        with es:
            wT = es.enter_context(nc.sbuf_tensor(f"wT1b{tag}", [128, 16, C], BF16))
            xq = [es.enter_context(nc.sbuf_tensor(f"xq{tag}{i}", [128, 16, 1024], BF16))
                  for i in range(2)]
            cA = es.enter_context(nc.sbuf_tensor(f"cA1b{tag}", [128, 1024], F32))
            cB = es.enter_context(nc.sbuf_tensor(f"cB1b{tag}", [128, 1024], F32))
            p1st = es.enter_context(nc.sbuf_tensor(f"p1st{tag}", [128, 1024], BF16))
            p2st = es.enter_context(nc.sbuf_tensor(f"p2st{tag}", [128, 1024], BF16))
            kst = [es.enter_context(nc.sbuf_tensor(f"kst{tag}{i}", [128, 1024], BF16))
                   for i in range(2)]
            pk = [es.enter_context(nc.psum_tensor(f"pk{tag}{i}", [128, 1024], F32))
                  for i in range(2)]
            krp = [es.enter_context(nc.psum_tensor(f"krp{tag}{i}", [128, 1024], F32))
                   for i in range(2)]
            for f in range(16):
                wtw = sc.dma("g", f"wt{tag}", wT[:, f, :],
                             wT_in[f * 128:(f + 1) * 128, :])
            nqt = ntok // 1024
            pk_free = [0, 0]
            rope_free = 0
            kr_free = [0, 0]
            krref = 0
            kst_free = [0, 0]
            xq_free = [0, 0]
            for qt in range(nqt):
                xbuf = qt % 2
                sc.wait("g", "T", xq_free[xbuf])
                if tag == 'k':
                    sc.wait("g", "xbf", sc.cnt.get("xbf", 0))
                sc.wait("y", "V", rope_free)
                sc.dma("y", f"tab{tag}", cA[:], cA_in[:, qt * 1024:(qt + 1) * 1024])
                tabw = sc.dma("y", f"tab{tag}", cB[:],
                              cB_in[:, qt * 1024:(qt + 1) * 1024])
                xw = 0
                for f in range(16):
                    xw = sc.dma("g", f"xq{tag}{xbuf}", xq[xbuf][:, f, :],
                                xT_in[f * 128:(f + 1) * 128,
                                      qt * 1024:(qt + 1) * 1024])
                if tag == 'k':
                    sc.wait("y", f"xq{tag}{xbuf}", xw)
                    for f in range(16):
                        sc.dma("y", "xbf", x_bf[f * 128:(f + 1) * 128,
                                               qt * 1024:(qt + 1) * 1024],
                               xq[xbuf][:, f, :])
                for hh in range(H):
                    pb = hh % 2
                    sc.wait("t", "V", pk_free[pb])
                    if hh == 0:
                        sc.wait("t", f"xq{tag}{xbuf}", xw)
                        if qt == 0:
                            sc.wait("t", f"wt{tag}", wtw)
                    for f in range(16):
                        for cpos in (0, 512):
                            last = (f == 15 and cpos == 512)
                            fn = lambda E, pb=pb, f=f, c=cpos, xb2=xbuf, hh=hh: \
                                E.matmul(pk[pb][:, c:c + 512],
                                         wT[:, f, hh * 128:(hh + 1) * 128],
                                         xq[xb2][:, f, c:c + 512],
                                         start=(f == 0), stop=(f == 15))
                            if last:
                                mmref = sc.op("t", "T", fn)
                            else:
                                sc.opn("t", fn)
                    if hh == 15:
                        xq_free[xbuf] = mmref
                    sc.wait("v", "T", mmref)
                    if hh == 0:
                        sc.wait("v", f"tab{tag}", tabw)
                    sc.wait("v", "T", krref)
                    sc.opn("v", lambda E, pb=pb, hh=hh:
                           E.scalar_tensor_tensor(p1st[:], pk[pb][:],
                                                  b_sb[:, hh:hh + 1], cA[:],
                                                  ALU.add, ALU.mult))
                    pk_free[pb] = sc.op(
                        "v", "V", lambda E, pb=pb, hh=hh:
                        E.scalar_tensor_tensor(p2st[:], pk[pb][:],
                                               b_sb[:, hh:hh + 1], cB[:],
                                               ALU.add, ALU.mult))
                    rope_free = pk_free[pb]
                    kb = hh % 2
                    sc.wait("t", "V", rope_free)
                    sc.wait("t", "S", kr_free[kb])
                    for cpos in (0, 512):
                        sc.opn("t", lambda E, c=cpos, kb=kb:
                               E.matmul(krp[kb][0:64, c:c + 512], m1[:],
                                        p1st[:, c:c + 512], start=True, stop=True))
                    sc.opn("t", lambda E, kb=kb:
                           E.matmul(krp[kb][64:128, 0:512], m2[:],
                                    p2st[:, 0:512], start=True, stop=True))
                    krref = sc.op("t", "T", lambda E, kb=kb:
                                  E.matmul(krp[kb][64:128, 512:1024], m2[:],
                                           p2st[:, 512:1024], start=True, stop=True))
                    sc.wait("s", "T", krref)
                    sc.wait("s", f"kst{tag}{kb}", kst_free[kb])
                    kr_free[kb] = sc.op("s", "S", lambda E, kb=kb:
                                        E.activation(kst[kb][:], krp[kb][:],
                                                     AF.Identity))
                    sref = sc.cnt["S"]
                    sc.wait("y", "S", sref)
                    kst_free[kb] = sc.dma(
                        "y", f"kst{tag}{kb}",
                        dst_sp[hh, :, qt * 1024:(qt + 1) * 1024], kst[kb][:])
            sc.barrier()
            sc.flush()

    if stop >= 3:
        phase_proj_rope(wkT, bkr_sb, ckA_in, ckB_in, xbT, T, k_sp, 'k')
    if stop >= 4:
        phase_proj_rope(wqT, bqr_sb, cqA_in, cqB_in, xownT, NTOK, q_sp, 'q')

    def phase_vproj():
        es = ExitStack()
        with es:
            wvT = es.enter_context(nc.sbuf_tensor("wvT", [128, 16, C], BF16))
            bv_bc = es.enter_context(nc.sbuf_tensor("bv_bc_sb", [128, C], F32))
            bvw = sc.dma("y", "bv", bv_bc[:], bv_bc_in)
            for f in range(16):
                wtw = sc.dma("g", "wtv", wvT[:, f, :],
                             wvT_in[f * 128:(f + 1) * 128, :])
            xst = [es.enter_context(nc.sbuf_tensor(f"xst{i}", [128, 16, 512], BF16))
                   for i in range(2)]
            vst = [es.enter_context(nc.sbuf_tensor(f"vst{i}", [128, 1024], BF16))
                   for i in range(2)]
            pv = [es.enter_context(nc.psum_tensor(f"pv{i}", [128, 1024], F32))
                  for i in range(2)]
            pv_free = [0, 0]
            vst_free = [0, 0]
            xst_free = [0, 0]
            for grp in range(8):
                xbuf = grp % 2
                sc.wait("y", "T", xst_free[xbuf])
                for f in range(16):
                    xw = sc.dma("y", f"xv{xbuf}", xst[xbuf][:, f, :],
                                x_bf[f * 128:(f + 1) * 128,
                                     grp * 512:(grp + 1) * 512])
                lastmm = 0
                for tb in range(4):
                    for dh in range(2):
                        it = tb * 2 + dh
                        pb = it % 2
                        sc.wait("t", "V", pv_free[pb])
                        if tb == 0 and dh == 0:
                            sc.wait("t", f"xv{xbuf}", xw)
                            if grp == 0:
                                sc.wait("t", "wtv", wtw)
                        for f in range(16):
                            for cpos in (0, 512):
                                last = (f == 15 and cpos == 512)
                                fn = lambda E, pb=pb, f=f, c=cpos, xb2=xbuf, \
                                    tb=tb, dh=dh: \
                                    E.matmul(pv[pb][:, c:c + 512],
                                             xst[xb2][:, f, tb * 128:(tb + 1) * 128],
                                             wvT[:, f, dh * 1024 + c:dh * 1024 + c + 512],
                                             start=(f == 0), stop=(f == 15))
                                if last:
                                    mmref = sc.op("t", "T", fn)
                                else:
                                    sc.opn("t", fn)
                        lastmm = mmref
                        vb = it % 2
                        sc.wait("v", "T", mmref)
                        if grp == 0 and it == 0:
                            sc.wait("v", "bv", bvw)
                        sc.wait("v", f"vsp{vb}", vst_free[vb])
                        pv_free[pb] = sc.op(
                            "v", "V", lambda E, pb=pb, vb=vb, dh=dh:
                            E.tensor_add(vst[vb][:], pv[pb][:],
                                         bv_bc[:, dh * 1024:(dh + 1) * 1024]))
                        vref = sc.cnt["V"]
                        sc.wait("y", "V", vref)
                        row0 = (grp * 4 + tb) * 128
                        vst_free[vb] = sc.dma(
                            "y", f"vsp{vb}",
                            v_sp[row0:row0 + 128, dh * 1024:(dh + 1) * 1024],
                            vst[vb][:])
                xst_free[xbuf] = lastmm
            sc.barrier()
            sc.flush()

    if stop >= 5:
        phase_vproj()

    def phase_attn():
        KH = int(os.environ.get("KH", "16"))
        es = ExitStack()
        with es:
            Kh = [es.enter_context(nc.sbuf_tensor(f"Kh{i}", [128, T], BF16))
                  for i in range(2)]
            Qh = [es.enter_context(nc.sbuf_tensor(f"Qh{i}", [128, NTOK], BF16))
                  for i in range(2)]
            v2 = [es.enter_context(nc.sbuf_tensor(f"v2_{i}", [128, 32, 256], BF16))
                  for i in range(2)]
            masksT = es.enter_context(nc.sbuf_tensor("masksT", [128, NSLOT * 256], BF16))
            mkw = sc.dma("g", "mks", masksT[:], masksT_in)
            ptq = [es.enter_context(nc.sbuf_tensor(f"ptq{i}", [128, 1024], BF16))
                   for i in range(3)]
            sums_sb = [es.enter_context(nc.sbuf_tensor(f"sums_sb{i}", [128, 1024], F32))
                       for i in range(2)]
            lnt = es.enter_context(nc.sbuf_tensor("lnt", [128, 1024], F32))
            avout = [es.enter_context(nc.sbuf_tensor(f"avout{i}", [128, 1024], BF16))
                     for i in range(2)]
            sps = [es.enter_context(nc.psum_tensor(f"sps{i}", [128, 1024], F32))
                   for i in range(2)]
            avp = es.enter_context(nc.psum_tensor("avp", [128, 1024], F32))
            smtp = es.enter_context(nc.psum_tensor("smtp", [128, 1024], F32))
            kq_free = [0, 0]
            v2_free = [0, 0]
            spsB_free = [0, 0]
            ptq_free = [0, 0, 0]
            avp_free = 0
            smtp_evac = 0
            lnt_free = 0
            sums_free = [0, 0]
            avout_free = [0, 0]
            spsrot = [0]
            vw_cur = [0]

            def stage_STAV(h, band, hb, kqw):
                nonlocal avp_free, smtp_evac, lnt_free
                bpar = band % 2
                tmax = 16 * band + 16
                hoff = (h % 2) * 128
                vb2 = (h // 2) % 2
                qbase = band * 1024
                erefs = {}

                def st_tile(t):
                    vbase = max(0, t // 2 - band * 8)
                    pb2 = spsrot[0] % 2
                    spsrot[0] += 1
                    sc.wait("t", "S", spsB_free[pb2])
                    c0 = vbase * 128
                    segs = [(c0, 512), (512, 1024)] if c0 < 512 else [(c0, 1024)]
                    jd = t // 2
                    jjd = jd - band * 8 if band * 8 <= jd < band * 8 + 8 else None
                    ref = None
                    for idx, (ca, cb) in enumerate(segs):
                        mask_here = jjd is not None and ca <= jjd * 128 < cb
                        last = (idx == len(segs) - 1)
                        smm = lambda E, pb2=pb2, ca=ca, cb=cb, t=t, hb=hb, \
                            stp=not mask_here: \
                            E.matmul(sps[pb2][:, ca:cb],
                                     Kh[hb][:, t * 128:(t + 1) * 128],
                                     Qh[hb][:, qbase + ca:qbase + cb],
                                     start=True, stop=stp)
                        if last and not mask_here:
                            ref = sc.op("t", "T", smm)
                        else:
                            sc.opn("t", smm)
                        if mask_here:
                            u = t % 2
                            fn = lambda E, pb2=pb2, jjd=jjd, jd=jd, u=u: \
                                E.matmul(sps[pb2][:, jjd * 128:(jjd + 1) * 128],
                                         ident[:],
                                         masksT[:, jd * 256 + u * 128:
                                                jd * 256 + (u + 1) * 128],
                                         start=False, stop=True)
                            if last:
                                ref = sc.op("t", "T", fn)
                            else:
                                sc.opn("t", fn)
                    return ref, pb2, vbase, c0

                def consume(t):
                    eref, rb, vbase, c0 = erefs[t]
                    sc.wait("t", "S", eref)
                    if t == 0:
                        sc.wait("t", "S", smtp_evac)
                    segs = [(c0, 512), (512, 1024)] if c0 < 512 else [(c0, 1024)]
                    for (ca, cb) in segs:
                        sc.opn("t", lambda E, ca=ca, cb=cb, rb=rb, t=t:
                               E.matmul(smtp[:, ca:cb], allones[:],
                                        ptq[rb][:, ca:cb],
                                        start=(t == 0), stop=False))
                    if t == 0:
                        sc.wait("t", "V", avp_free)
                    for si, (ca, cb) in enumerate(segs):
                        fn = lambda E, ca=ca, cb=cb, rb=rb, t=t, vb2=vb2, \
                            hoff=hoff: \
                            E.matmul(avp[:, ca:cb],
                                     v2[vb2][:, t, hoff:hoff + 128],
                                     ptq[rb][:, ca:cb],
                                     start=(t == 0), stop=False)
                        if si == len(segs) - 1:
                            ptq_free[rb] = sc.op("t", "T", fn)
                        else:
                            sc.opn("t", fn)

                if band == 0:
                    sc.wait("t", f"kq{hb}", kqw)
                    if h == 0:
                        sc.wait("t", "mks", mkw)
                    if h % 2 == 0:
                        sc.wait("t", f"v2_{vb2}", vw_cur[0])
                for t in range(tmax):
                    mmref, pb2, vbase, c0 = st_tile(t)
                    rb = t % 3
                    sc.wait("s", "T", mmref)
                    sc.wait("s", "T", ptq_free[rb])
                    spsB_free[pb2] = sc.op(
                        "s", "S", lambda E, pb2=pb2, rb=rb, c0=c0:
                        E.activation(ptq[rb][:, c0:1024], sps[pb2][:, c0:1024],
                                     AF.Exp, bias=0.0, scale=1.0))
                    erefs[t] = (spsB_free[pb2], rb, vbase, c0)
                    if t >= 1:
                        consume(t - 1)
                consume(tmax - 1)
                sc.opn("t", lambda E:
                       E.matmul(avp[:, 0:512], allzeros[:], ptq[0][:, 0:512],
                                start=False, stop=True))
                sc.opn("t", lambda E:
                       E.matmul(avp[:, 512:1024], allzeros[:], ptq[0][:, 512:1024],
                                start=False, stop=True))
                sc.opn("t", lambda E:
                       E.matmul(smtp[:, 0:512], allzeros[:], ptq[0][:, 0:512],
                                start=False, stop=True))
                avref = sc.op("t", "T", lambda E:
                              E.matmul(smtp[:, 512:1024], allzeros[:],
                                       ptq[0][:, 512:1024], start=False, stop=True))
                v2_free[vb2] = avref
                sc.wait("s", "T", avref)
                sc.wait("s", "S", lnt_free)
                smtp_evac = sc.op("s", "S", lambda E:
                                  E.activation(lnt[:], smtp[:], AF.Ln))
                sc.wait("s", "S", smtp_evac)
                sc.wait("s", "V", sums_free[bpar])
                rcref = sc.op("s", "S", lambda E, bpar=bpar:
                              E.activation(sums_sb[bpar][:], lnt[:], AF.Exp,
                                           scale=-1.0))
                lnt_free = rcref
                ab = bpar
                sc.wait("v", "S", rcref)
                sc.wait("v", "T", avref)
                sc.wait("v", f"avo{ab}", avout_free[ab])
                avp_free = sc.op("v", "V", lambda E, ab=ab, bpar=bpar:
                                 E.tensor_tensor(avout[ab][:], avp[:],
                                                 sums_sb[bpar][:], ALU.mult))
                sums_free[bpar] = avp_free
                vref = sc.cnt["V"]
                sc.wait("y", "V", vref)
                avout_free[ab] = sc.dma(
                    "y", f"avo{ab}", av_sp[h, :, band * 1024:(band + 1) * 1024],
                    avout[ab][:])

            def head_loads(h):
                hb2 = h % 2
                sc.wait("y", "T", kq_free[hb2])
                sc.dma("y", f"kq{hb2}", Kh[hb2][:], k_sp[h])
                kqn = sc.dma("y", f"kq{hb2}", Qh[hb2][:], q_sp[h])
                if h % 2 == 0:
                    vb2 = (h // 2) % 2
                    sc.wait("y", "T", v2_free[vb2])
                    vw_cur[0] = sc.dma("y", f"v2_{vb2}", v2[vb2][:],
                                       v_sp.rearrange("(tt p) d -> p tt d", p=128)
                                       [:, :, h * 128:(h + 2) * 128])
                return kqn

            kqw = head_loads(0)
            for h in range(KH):
                hb = h % 2
                stage_STAV(h, 0, hb, kqw)
                kqw_next = head_loads(h + 1) if h + 1 < KH else 0
                stage_STAV(h, 1, hb, kqw)
                kq_free[hb] = sc.cnt["T"]
                kqw = kqw_next
            sc.barrier()
            sc.flush()

    if stop >= 6:
        phase_attn()

    y1nfm_cm = nc.sbuf_tensor("y1nfm", [128, 16, NTOK], BF16)
    y1nfm = y1nfm_cm.__enter__()

    def phase_outproj():
        es = ExitStack()
        with es:
            woT = es.enter_context(nc.sbuf_tensor("woT", [128, 16, C], BF16))
            for f in range(16):
                wtw = sc.dma("g", "wto", woT[:, f, :],
                             woT_in[f * 128:(f + 1) * 128, :])
            bo_bc = es.enter_context(nc.sbuf_tensor("bo_bc_sb", [128, C], F32))
            bow = sc.dma("y", "bo", bo_bc[:], bo_bc_in)
            avs = [es.enter_context(nc.sbuf_tensor(f"avs{i}", [128, 16, 512], BF16))
                   for i in range(2)]
            xot = [es.enter_context(nc.sbuf_tensor(f"xot{i}", [128, C], F32))
                   for i in range(2)]
            y1f = es.enter_context(nc.sbuf_tensor("y1f", [128, C], F32))
            y1nst = es.enter_context(nc.sbuf_tensor("y1nst", [128, C], BF16))
            bn6 = es.enter_context(nc.sbuf_tensor("bn6", [128, 24], F32))
            mv2 = es.enter_context(nc.sbuf_tensor("mv2", [128, 2], F32))
            sd1 = es.enter_context(nc.sbuf_tensor("sd1", [128, 1], F32))
            rstd = es.enter_context(nc.sbuf_tensor("rstd", [128, 1], F32))
            nmr = es.enter_context(nc.sbuf_tensor("nmr", [128, 1], F32))
            py1 = [es.enter_context(nc.psum_tensor(f"py1_{i}", [128, 1024], F32))
                   for i in range(2)]
            ttp = [es.enter_context(nc.psum_tensor(f"ttp3{i}", [128, 1024], BF16))
                   for i in range(2)]
            py_free = [0, 0]
            ttp_free = [0, 0]
            y1f_free = 0
            y1n_free = 0
            xo_free = [0, 0]
            avs_free = [0, 0]
            for tb in range(16):
                g4 = tb // 4
                if tb % 4 == 0:
                    abuf = g4 % 2
                    sc.wait("y", "T", avs_free[abuf])
                    for d in range(16):
                        avw = sc.dma("y", f"avs{abuf}", avs[abuf][:, d, :],
                                     av_sp[d, :, g4 * 512:(g4 + 1) * 512])
                xbuf = tb % 2
                sc.wait("y", "V", xo_free[xbuf])
                xw = sc.dma("y", f"xo{xbuf}", xot[xbuf][:],
                            xown[tb * 128:(tb + 1) * 128, :])
                abuf = g4 % 2
                toff = (tb % 4) * 128
                lastmm = [0, 0]
                for fh in range(2):
                    pb = fh
                    sc.wait("t", "V", py_free[pb])
                    if tb % 4 == 0 and fh == 0:
                        if tb == 0:
                            sc.wait("t", "wto", wtw)
                        sc.wait("t", f"avs{abuf}", avw)
                    for d in range(16):
                        for cpos in (0, 512):
                            last = (d == 15 and cpos == 512)
                            fn = lambda E, pb=pb, d=d, c=cpos, fh=fh, ab=abuf, \
                                toff=toff: \
                                E.matmul(py1[pb][:, c:c + 512],
                                         avs[ab][:, d, toff:toff + 128],
                                         woT[:, d, fh * 1024 + c:fh * 1024 + c + 512],
                                         start=(d == 0), stop=(d == 15))
                            if last:
                                lastmm[fh] = sc.op("t", "T", fn)
                            else:
                                sc.opn("t", fn)
                if tb % 4 == 3:
                    avs_free[abuf] = lastmm[1]
                for fh in range(2):
                    pb = fh
                    sc.wait("v", "T", lastmm[fh])
                    if tb == 0 and fh == 0:
                        sc.wait("v", "bo", bow)
                    sc.wait("v", f"xo{xbuf}", xw)
                    sc.wait("v", "S", y1f_free)
                    a1ref = sc.op("v", "V", lambda E, pb=pb, fh=fh:
                                  E.tensor_add(y1f[:, fh * 1024:(fh + 1) * 1024],
                                               py1[pb][:],
                                               bo_bc[:, fh * 1024:(fh + 1) * 1024]))
                    sc.wait("v", "V", a1ref)
                    py_free[pb] = sc.op(
                        "v", "V", lambda E, fh=fh, xbuf=xbuf:
                        E.tensor_add(y1f[:, fh * 1024:(fh + 1) * 1024],
                                     y1f[:, fh * 1024:(fh + 1) * 1024],
                                     xot[xbuf][:, fh * 1024:(fh + 1) * 1024]))
                xo_free[xbuf] = py_free[1]
                sc.wait("v", "V", py_free[1])
                for kb in range(3):
                    sc.opn("v", lambda E, kb=kb:
                           E.bn_stats(bn6[:, kb * 6:(kb + 1) * 6],
                                      y1f[:, kb * 512:(kb + 1) * 512]))
                bsref = sc.op("v", "V", lambda E:
                              E.bn_stats(bn6[:, 18:24], y1f[:, 1536:2048]))
                sc.wait("v", "V", bsref)
                bnref = sc.op("v", "V", lambda E: E.bn_aggr(mv2[:], bn6[:]))
                sc.wait("s", "V", bnref)
                sdref = sc.op("s", "S", lambda E:
                              E.activation(sd1[:], mv2[:, 1:2], AF.Sqrt, bias=eps_sb[:]))
                sc.wait("v", "S", sdref)
                rcref = sc.op("v", "V", lambda E: E.reciprocal(rstd[:], sd1[:]))
                sc.wait("v", "V", rcref)
                vref = sc.op("v", "V", lambda E:
                             E.tensor_scalar(nmr[:], mv2[:, 0:1], rstd[:], -1.0,
                                             ALU.mult, ALU.mult))
                sc.wait("s", "V", vref)
                sc.wait("s", "T", y1n_free)
                sc.wait("s", "y1sp", sc.cnt.get("y1sp", 0))
                y1f_free = sc.op("s", "S", lambda E:
                                 E.activation(y1nst[:], y1f[:], AF.Identity,
                                              bias=nmr[:], scale=rstd[:]))
                sc.wait("y", "S", y1f_free)
                sc.dma("y", "y1sp", y1tm_sp[tb], y1nst[:])
                sc.wait("t", "S", y1f_free)
                for fg in range(2):
                    pb4 = fg
                    sc.wait("t", "V", ttp_free[pb4])
                    for f8 in range(7):
                        f = fg * 8 + f8
                        sc.opn("t", lambda E, pb4=pb4, f8=f8, f=f:
                               E.transpose(ttp[pb4][:, f8 * 128:(f8 + 1) * 128],
                                           y1nst[:, f * 128:(f + 1) * 128],
                                           ident[:]))
                    f = fg * 8 + 7
                    tref = sc.op("t", "T", lambda E, pb4=pb4, f=f:
                                 E.transpose(ttp[pb4][:, 7 * 128:8 * 128],
                                             y1nst[:, f * 128:(f + 1) * 128],
                                             ident[:]))
                    y1n_free = tref
                    sc.wait("v", "T", tref)
                    ttp_free[pb4] = sc.op(
                        "v", "V", lambda E, pb4=pb4, fg=fg, tb=tb:
                        E.tensor_copy(
                            y1nfm[:, fg * 8:(fg + 1) * 8, tb * 128:(tb + 1) * 128],
                            ttp[pb4][:].rearrange("p (a b) -> p a b", a=8)))
            sc.barrier()
            sc.flush()

    if stop >= 7:
        phase_outproj()


    def phase_gateup():
        es = ExitStack()
        with es:
            wg_sb = [es.enter_context(nc.sbuf_tensor(f"wg_sb{i}", [128, 16, 512], BF16))
                     for i in range(2)]
            wu_sb = [es.enter_context(nc.sbuf_tensor(f"wu_sb{i}", [128, 16, 512], BF16))
                     for i in range(2)]
            tg = es.enter_context(nc.sbuf_tensor("tg", [128, 1024], BF16))
            tu = es.enter_context(nc.sbuf_tensor("tu", [128, 1024], BF16))
            gsl = es.enter_context(nc.sbuf_tensor("gsl", [128, 1024], BF16))
            hst = [es.enter_context(nc.sbuf_tensor(f"hst{i}", [128, 1024], BF16))
                   for i in range(2)]
            sy_bc = es.enter_context(nc.sbuf_tensor("sy_bc", [128, C], F32))
            gps = es.enter_context(nc.psum_tensor("gps", [128, 1024], F32))
            ups = es.enter_context(nc.psum_tensor("ups", [128, 1024], F32))
            syp = es.enter_context(nc.psum_tensor("syp", [128, 1024], F32))
            syref = 0
            for c4 in range(2):
                sc.wait("t", "V", syref)
                for f in range(16):
                    for cpos in (0, 512):
                        last = (f == 15 and cpos == 512)
                        fn = lambda E, f=f, c=cpos, c4=c4: \
                            E.matmul(syp[:, c:c + 512], allones[:],
                                     y1nfm[:, f, c4 * 1024 + c:c4 * 1024 + c + 512],
                                     start=(f == 0), stop=(f == 15))
                        if last:
                            smr = sc.op("t", "T", fn)
                        else:
                            sc.opn("t", fn)
                sc.wait("v", "T", smr)
                syref = sc.op("v", "V", lambda E, c4=c4:
                              E.tensor_copy(sy_bc[:, c4 * 1024:(c4 + 1) * 1024],
                                            syp[:]))
            wfree = [0, 0]
            gps_free = 0
            ups_free = 0
            tg_free = 0
            tu_free = 0
            gsl_free = 0
            hst_free = [0, 0]
            for icg in range(16):
                buf = icg % 2
                sc.wait("g", "T", wfree[buf])
                for f in range(16):
                    sc.dma("g", f"gu{buf}", wg_sb[buf][:, f, :],
                           gwT[f * 128:(f + 1) * 128, icg * 512:(icg + 1) * 512])
                    ww = sc.dma("g", f"gu{buf}", wu_sb[buf][:, f, :],
                                uwT[f * 128:(f + 1) * 128, icg * 512:(icg + 1) * 512])
                for ici in range(4):
                    ic = icg * 4 + ici
                    for th in range(2):
                        sc.wait("t", "V", gps_free)
                        if ici == 0 and th == 0:
                            sc.wait("t", f"gu{buf}", ww)
                            if icg == 0:
                                sc.wait("t", "V", syref)
                        for f in range(16):
                            for cpos in (0, 512):
                                last = (f == 15 and cpos == 512)
                                fn = lambda E, f=f, c=cpos, th=th, buf=buf, ici=ici: \
                                    E.matmul(gps[:, c:c + 512],
                                             wg_sb[buf][:, f, ici * 128:(ici + 1) * 128],
                                             y1nfm[:, f, th * 1024 + c:th * 1024 + c + 512],
                                             start=(f == 0), stop=(f == 15))
                                if last:
                                    gmref = sc.op("t", "T", fn)
                                else:
                                    sc.opn("t", fn)
                        sc.wait("v", "T", gmref)
                        sc.wait("v", "S", tg_free)
                        gps_free = sc.op(
                            "v", "V", lambda E, th=th, ic=ic:
                            E.scalar_tensor_tensor(
                                tg[:], sy_bc[:, th * 1024:(th + 1) * 1024],
                                -128.0, gps[:], ALU.mult, ALU.add))
                        sc.wait("s", "V", gps_free)
                        sc.wait("s", "V", gsl_free)
                        tg_free = sc.op("s", "S", lambda E, ic=ic:
                                        E.activation(gsl[:], tg[:], AF.Silu,
                                                     scale=gsc_sb[:, ic:ic + 1]))
                        sc.wait("t", "V", ups_free)
                        for f in range(16):
                            for cpos in (0, 512):
                                last = (f == 15 and cpos == 512)
                                fn = lambda E, f=f, c=cpos, th=th, buf=buf, ici=ici: \
                                    E.matmul(ups[:, c:c + 512],
                                             wu_sb[buf][:, f, ici * 128:(ici + 1) * 128],
                                             y1nfm[:, f, th * 1024 + c:th * 1024 + c + 512],
                                             start=(f == 0), stop=(f == 15))
                                if last:
                                    umref = sc.op("t", "T", fn)
                                else:
                                    sc.opn("t", fn)
                        if ici == 3 and th == 1:
                            wfree[buf] = umref
                        hb = th % 2
                        sc.wait("v", "T", umref)
                        sc.wait("v", "V", tu_free)
                        ups_free = sc.op(
                            "v", "V", lambda E, th=th, ic=ic:
                            E.scalar_tensor_tensor(
                                tu[:], sy_bc[:, th * 1024:(th + 1) * 1024],
                                -128.0, ups[:], ALU.mult, ALU.add))
                        sc.wait("v", "V", ups_free)
                        sc.wait("v", "S", tg_free)
                        sc.wait("v", f"hsp{hb}", hst_free[hb])
                        tu_free = sc.op(
                            "v", "V", lambda E, hb=hb, ic=ic:
                            E.scalar_tensor_tensor(hst[hb][:], tu[:],
                                                   usc_sb[:, ic:ic + 1], gsl[:],
                                                   ALU.mult, ALU.mult))
                        gsl_free = tu_free
                        vref = sc.cnt["V"]
                        sc.wait("y", "V", vref)
                        sc.dma("y", f"hsp{hb}",
                               h_sp[2 * th, :, ic * 512:(ic + 1) * 512],
                               hst[hb][:, 0:512])
                        hst_free[hb] = sc.dma(
                            "y", f"hsp{hb}",
                            h_sp[2 * th + 1, :, ic * 512:(ic + 1) * 512],
                            hst[hb][:, 512:1024])
            sc.barrier()
            sc.flush()

    if stop >= 9:
        phase_gateup()

    def phase_down():
        es = ExitStack()
        with es:
            wdm = [es.enter_context(nc.sbuf_tensor(f"wdm{i}", [128, 64, 512], BF16))
                   for i in range(2)]
            hsb = [es.enter_context(nc.sbuf_tensor(f"hsb{i}", [128, 32, 128], BF16))
                   for i in range(3)]
            y1t = [es.enter_context(nc.sbuf_tensor(f"y1t{i}", [128, 512], BF16))
                   for i in range(2)]
            dsbc = es.enter_context(nc.sbuf_tensor("dsbc_sb", [128, C], F32))
            y2row = es.enter_context(nc.sbuf_tensor("y2row", [128, C], BF16))
            yfsl = es.enter_context(nc.sbuf_tensor("yfsl", [128, 512], F32))
            yfb = [es.enter_context(nc.sbuf_tensor(f"yfb{i}", [128, 512], BF16))
                   for i in range(2)]
            shcol = es.enter_context(nc.sbuf_tensor("shcol", [128, 16], F32))
            bnacc = es.enter_context(nc.sbuf_tensor("bnacc", [128, 16 * 24], F32))
            mv2 = es.enter_context(nc.sbuf_tensor("mv2b", [128, 2], F32))
            sd1 = es.enter_context(nc.sbuf_tensor("sd1b", [128, 1], F32))
            rstd = es.enter_context(nc.sbuf_tensor("rstdb", [128, 1], F32))
            nmr = es.enter_context(nc.sbuf_tensor("nmrb", [128, 1], F32))
            outst = [es.enter_context(nc.sbuf_tensor(f"outst{i}", [128, 1024], F32))
                     for i in range(2)]
            y2p = [es.enter_context(nc.psum_tensor(f"y2p{i}", [128, 512], F32))
                   for i in range(3)]
            shp = es.enter_context(nc.psum_tensor("shp", [128, 16], F32))
            dsw = sc.dma("y", "dsb", dsbc[:], dsbc_in)

            wdm_free = [0, 0]
            wload = [0, 0]
            hsb_free = [0, 0, 0]
            hload = [0, 0, 0]
            y1t_free = [0, 0]
            y2p_freeV = [0, 0, 0]
            yfb_free = [0, 0]
            y2row_free = 0
            outst_free = [0, 0]
            yfsl_free = 0

            def h_load(ci):
                r = ci % 32
                tb_, hh = r // 2, r % 2
                g4, off = tb_ // 4, (tb_ % 4) * 128
                sl = ci % 3
                sc.wait("y", "T", hsb_free[sl])
                hload[sl] = sc.dma(
                    "y", f"hsb{sl}", hsb[sl][:],
                    h_sp[g4].rearrange("p (ic c) -> p ic c", ic=64)
                    [:, hh * 32:(hh + 1) * 32, off:off + 128])
                return hload[sl]

            wl0a = sc.dma("g", "wdl0h", wdm[0][:, 0:32, :], dwM[:, 0:32, 0:512])
            wl0b = sc.dma("g", "wdl0", wdm[0][:, 32:64, :], dwM[:, 32:64, 0:512])
            wload[0] = [("wdl0h", wl0a), ("wdl0", wl0b)]
            h_load(0)
            h_load(1)
            for cg in range(4):
                wb = cg % 2
                if cg + 1 < 4:
                    nwb = (cg + 1) % 2
                    sc.wait("g", "T", wdm_free[nwb])
                    wn = sc.dma("g", f"wdl{nwb}", wdm[nwb][:],
                                dwM[:, :, (cg + 1) * 512:(cg + 2) * 512])
                    wload[nwb] = [(f"wdl{nwb}", wn), (f"wdl{nwb}", wn)]
                for tb in range(16):
                    ci0 = cg * 32 + tb * 2
                    if ci0 + 2 < 128:
                        h_load(ci0 + 2)
                    pb = (cg * 16 + tb) % 3
                    rbt = tb % 2
                    sc.wait("y", "V", y1t_free[rbt])
                    ylw = sc.dma("y", f"y1t{rbt}", y1t[rbt][:],
                                 y1tm_sp[tb][:, cg * 512:(cg + 1) * 512])
                    sc.wait("t", "V", y2p_freeV[pb])
                    shmm = 0
                    mmref = 0
                    for hh in range(2):
                        sl = (ci0 + hh) % 3
                        sc.wait("t", f"hsb{sl}", hload[sl])
                        if tb == 0:
                            sc.wait("t", wload[wb][hh][0], wload[wb][hh][1])
                        for i2 in range(32):
                            ic = hh * 32 + i2
                            lastm = (ic == 63)
                            fn = lambda E, pb=pb, sl=sl, i2=i2, ic=ic, wb=wb: \
                                E.matmul(y2p[pb][:],
                                         hsb[sl][:, i2, :],
                                         wdm[wb][:, ic, :],
                                         start=(ic == 0), stop=(ic == 63))
                            if lastm or (cg != 0 and i2 == 31):
                                mmref = sc.op("t", "T", fn)
                            else:
                                sc.opn("t", fn)
                            if cg == 0:
                                fs = lambda E, sl=sl, i2=i2, ic=ic, tb=tb: \
                                    E.matmul(shp[:, tb:tb + 1],
                                             hsb[sl][:, i2, :],
                                             allones[:, 0:1],
                                             start=(ic == 0), stop=(ic == 63))
                                if i2 == 31:
                                    shmm = sc.op("t", "T", fs)
                                else:
                                    sc.opn("t", fs)
                        hsb_free[sl] = sc.cnt["T"]
                        if hh == 0 and ci0 + 3 < 128:
                            h_load(ci0 + 3)
                    if tb == 15:
                        wdm_free[wb] = mmref if cg != 0 else shmm
                    if cg == 0:
                        sc.wait("v", "T", shmm)
                        shv = sc.op("v", "V", lambda E, tb=tb:
                                    E.tensor_scalar_mul(shcol[:, tb:tb + 1],
                                                        shp[:, tb:tb + 1], -128.0))
                        sc.wait("v", "V", shv)
                    sc.wait("v", "T", mmref)
                    sc.wait("v", f"y1t{rbt}", ylw)
                    if cg == 0 and tb == 0:
                        sc.wait("v", "dsb", dsw)
                    sc.wait("v", "V", yfsl_free)
                    v1 = sc.op("v", "V", lambda E, pb=pb, cg=cg, tb=tb:
                               E.scalar_tensor_tensor(
                                   yfsl[:], y2p[pb][:], shcol[:, tb:tb + 1],
                                   dsbc[:, cg * 512:(cg + 1) * 512],
                                   ALU.add, ALU.mult))
                    y2p_freeV[pb] = v1
                    sc.wait("v", "V", v1)
                    sc.wait("v", f"yspl{rbt}", yfb_free[rbt])
                    v2 = sc.op("v", "V", lambda E, rbt=rbt:
                               E.tensor_add(yfb[rbt][:], yfsl[:], y1t[rbt][:]))
                    yfsl_free = v2
                    y1t_free[rbt] = v2
                    sc.wait("v", "V", v2)
                    v3 = sc.op("v", "V", lambda E, rbt=rbt, tb=tb, cg=cg:
                               E.bn_stats(bnacc[:, tb * 24 + cg * 6:
                                                 tb * 24 + cg * 6 + 6],
                                          yfb[rbt][:]))
                    sc.wait("y", "V", v3)
                    yfb_free[rbt] = sc.dma("y", f"yspl{rbt}",
                                           y2_sp[tb][:, cg * 512:(cg + 1) * 512],
                                           yfb[rbt][:])
                    if cg == 3:
                        sc.wait("v", "V", v3)
                        bnref = sc.op("v", "V", lambda E, tb=tb:
                                      E.bn_aggr(mv2[:], bnacc[:, tb * 24:
                                                               (tb + 1) * 24]))
                        sc.wait("s", "V", bnref)
                        sdref = sc.op("s", "S", lambda E:
                                      E.activation(sd1[:], mv2[:, 1:2], AF.Sqrt,
                                                   bias=eps_sb[:]))
                        sc.wait("v", "S", sdref)
                        rcref = sc.op("v", "V", lambda E:
                                      E.reciprocal(rstd[:], sd1[:]))
                        sc.wait("v", "V", rcref)
                        vref = sc.op("v", "V", lambda E:
                                     E.tensor_scalar(nmr[:], mv2[:, 0:1], rstd[:],
                                                     -1.0, ALU.mult, ALU.mult))
                        sc.wait("y", "S", y2row_free)
                        sc.wait("y", "yspl0", sc.cnt.get("yspl0", 0))
                        sc.wait("y", "yspl1", sc.cnt.get("yspl1", 0))
                        y2rw = sc.dma("y", "y2r", y2row[:], y2_sp[tb])
                        sc.wait("s", "y2r", y2rw)
                        sc.wait("s", "V", vref)
                        row0 = tb * 128
                        for half in range(2):
                            sc.wait("s", f"outd{half}", outst_free[half])
                            apref = sc.op(
                                "s", "S", lambda E, half=half:
                                E.activation(outst[half][:],
                                             y2row[:, half * 1024:
                                                   (half + 1) * 1024],
                                             AF.Identity, bias=nmr[:],
                                             scale=rstd[:]))
                            if half == 1:
                                y2row_free = apref
                            sc.wait("y", "S", apref)
                            outst_free[half] = sc.dma(
                                "y", f"outd{half}",
                                out[row0:row0 + 128,
                                    half * 1024:(half + 1) * 1024],
                                outst[half][:])
            sc.wait("y", "outd0", sc.cnt.get("outd0", 0))
            sc.wait("y", "outd1", sc.cnt.get("outd1", 0))
            sc.barrier()
            sc.flush()

    y1nfm_cm.__exit__(None, None, None)
    if stop >= 10:
        phase_down()
    eso.close()
    return nc


_CACHE = {}
_LEAK = []


def _host_tables(par):
    th = np.float32(10000.0) ** (-(np.arange(DH, dtype=np.float32) / np.float32(DH)))
    pos_all = np.arange(T, dtype=np.float32)
    ang = pos_all[:, None] * th[None, :]
    cosk, sink = np.cos(ang).astype(np.float32), np.sin(ang).astype(np.float32)
    ckA = np.concatenate([cosk.T, sink.T], axis=0)
    ckB = np.concatenate([sink.T, cosk.T], axis=0)
    own_blocks = [2 * j + par for j in range(NSLOT)]
    pos_own = np.concatenate([np.arange(g * 128, (g + 1) * 128) for g in own_blocks])
    ango = pos_own[:, None].astype(np.float32) * th[None, :]
    cosq, sinq = np.cos(ango).astype(np.float32), np.sin(ango).astype(np.float32)
    s = np.float32(1.0 / math.sqrt(D))
    cqA = np.concatenate([cosq.T, sinq.T], axis=0) * s
    cqB = np.concatenate([sinq.T, cosq.T], axis=0) * s
    M = np.zeros((NSLOT, 128, 256), dtype=np.float32)
    kk = np.arange(256)[None, :]
    qq = np.arange(128)[:, None]
    for j in range(NSLOT):
        lim = qq + (0 if par == 0 else 128)
        M[j] = np.where(kk <= lim, 0.0, NEG)
    masks = M.transpose(1, 0, 2).reshape(128, NSLOT * 256)
    MT = np.zeros((NSLOT, 2, 128, 128), dtype=np.float32)
    for j in range(NSLOT):
        for u in range(2):
            MT[j, u] = M[j][:, u * 128:(u + 1) * 128].T
    masksT = MT.transpose(2, 0, 1, 3).reshape(128, NSLOT * 256)
    return ckA, ckB, cqA, cqB, masks, masksT, own_blocks


def kernel(L0_attn_x, L0_attn_w_qkv, L0_attn_b_qkv, L0_attn_w_out, L0_attn_b_out,
           L0_ffn_gate_w, L0_ffn_gate_scale, L0_ffn_up_w, L0_ffn_up_scale,
           L0_ffn_down_w, L0_ffn_down_scale):
    x = np.asarray(L0_attn_x, dtype=np.float32)
    wqkv = np.asarray(L0_attn_w_qkv, dtype=np.float32)
    bqkv = np.asarray(L0_attn_b_qkv, dtype=np.float32)
    wout = np.ascontiguousarray(np.asarray(L0_attn_w_out, dtype=np.float32))
    bout = np.asarray(L0_attn_b_out, dtype=np.float32)
    gwn = np.ascontiguousarray(np.asarray(L0_ffn_gate_w, dtype=np.int32))
    gscale = np.asarray(L0_ffn_gate_scale, dtype=np.float32)
    uwn = np.ascontiguousarray(np.asarray(L0_ffn_up_w, dtype=np.int32))
    uscale = np.asarray(L0_ffn_up_scale, dtype=np.float32)
    dwn = np.ascontiguousarray(np.asarray(L0_ffn_down_w, dtype=np.int32))
    dscale = np.asarray(L0_ffn_down_scale, dtype=np.float32)

    pair = np.concatenate([np.arange(0, 128, 2), np.arange(1, 128, 2)])
    perm = (np.arange(H)[:, None] * 128 + pair[None, :]).reshape(-1)
    wqT_h = np.ascontiguousarray(wqkv[0:C][perm].T)
    wkT_h = np.ascontiguousarray(wqkv[C:2 * C][perm].T)
    wvT_h = np.ascontiguousarray(wqkv[2 * C:3 * C].T)
    woT_h = np.ascontiguousarray(wout.T)
    bq_r = np.ascontiguousarray(bqkv[0:C][perm].reshape(H, 128).T)
    bk_r = np.ascontiguousarray(bqkv[C:2 * C][perm].reshape(H, 128).T)
    bv_bc = np.ascontiguousarray(np.broadcast_to(bqkv[2 * C:3 * C], (128, C)))
    bo_bc = np.ascontiguousarray(np.broadcast_to(bout, (128, C)))
    gwT_h = np.ascontiguousarray(gwn.T)
    uwT_h = np.ascontiguousarray(uwn.T)
    dwM_h = np.ascontiguousarray(dwn.T.reshape(64, 128, C).transpose(1, 0, 2))
    gsc = np.ascontiguousarray(gscale.reshape(64, 128).T)
    usc = np.ascontiguousarray(uscale.reshape(64, 128).T)
    dsbc_h = np.ascontiguousarray(np.broadcast_to(dscale, (128, C)))
    eye = np.eye(64, dtype=np.float32)
    m1 = np.concatenate([eye, -eye], axis=0)
    m2 = np.concatenate([eye, eye], axis=0)
    identity = np.eye(128, dtype=np.float32)

    _stop = int(os.environ.get("KSTOP", "99"))
    if "nc" not in _CACHE:
        _CACHE["nc"] = build_program(_stop)
    nc = _CACHE["nc"]

    tabs = {par: _host_tables(par) for par in range(2)}
    in_maps = []
    own_rows = {}
    for c in range(NCORE):
        b, par = c // 2, c % 2
        ckA, ckB, cqA, cqB, masks, masksT, own_blocks = tabs[par]
        rows = np.concatenate([np.arange(g * 128, (g + 1) * 128) for g in own_blocks])
        own_rows[c] = rows
        xo = np.ascontiguousarray(x[b][rows])
        in_maps.append({
            "xbT": np.ascontiguousarray(x[b].T),
            "xown": xo,
            "xownT": np.ascontiguousarray(xo.T),
            "wqT": wqT_h, "wkT": wkT_h, "wvT_in": wvT_h, "woT_in": woT_h,
            "bq_r": bq_r, "bk_r": bk_r, "bv_bc": bv_bc, "bo_bc": bo_bc,
            "gwT": gwT_h, "uwT": uwT_h, "dwM": dwM_h,
            "gsc": gsc, "usc": usc, "dsbc": dsbc_h,
            "ckA": np.ascontiguousarray(ckA), "ckB": np.ascontiguousarray(ckB),
            "cqA": np.ascontiguousarray(cqA), "cqB": np.ascontiguousarray(cqB),
            "m1_in": m1, "m2_in": m2, "ident_in": identity,
            "masks_in": np.ascontiguousarray(masks),
            "masksT_in": np.ascontiguousarray(masksT),
        })

    res = run_bass_kernel_spmd(nc, in_maps, core_ids=list(range(NCORE)))
    outf = np.empty((B, T, C), dtype=np.float32)
    for c in range(NCORE):
        b = c // 2
        outf[b][own_rows[c]] = np.asarray(res.results[c]["out"])
    return outf


# version 13
# speedup vs baseline: 1.1107x; 1.1107x over previous
import math
import numpy as np
import ml_dtypes
from contextlib import ExitStack

import os
import concourse.bass as bass
import concourse.mybir as mybir
from concourse.bass_utils import run_bass_kernel_spmd

F32 = mybir.dt.float32
BF16 = mybir.dt.bfloat16
I32 = mybir.dt.int32
AF = mybir.ActivationFunctionType
ALU = mybir.AluOpType
AXX = mybir.AxisListType.X

B, T, C, H, D, DH = 4, 4096, 2048, 16, 128, 64
II = 8192
NCORE = 8
NTOK = 2048
NSLOT = 16
NEG = -1.0e9
EPS = 1.0e-5


class Sched:
    ENGS = ("t", "v", "s", "g", "y")

    def __init__(self, nc, es):
        self.nc = nc
        self.es = es
        self.q = {k: [] for k in self.ENGS}
        self.sem = {}
        self.cnt = {}
        self.bar_mark = {}
        for name in ("T", "V", "S"):
            self._mk(name)

    def _mk(self, name):
        self.sem[name] = self.es.enter_context(self.nc.semaphore(name))
        self.cnt[name] = 0
        self.bar_mark[name] = 0

    def op(self, eng, semname, fn, inc=1):
        self.cnt[semname] += inc
        sem = self.sem[semname]

        def play(E, fn=fn, sem=sem, inc=inc):
            fn(E).then_inc(sem, inc)

        self.q[eng].append(play)
        return self.cnt[semname]

    def opn(self, eng, fn):
        self.q[eng].append(fn)

    def wait(self, eng, semname, val):
        if val <= 0:
            return
        sem = self.sem[semname]
        self.q[eng].append(lambda E, sem=sem, val=val: E.wait_ge(sem, val))

    def dma(self, eng, slot, out, in_):
        if slot not in self.sem:
            self._mk(slot)
        return self.op(eng, slot,
                       lambda E, out=out, in_=in_: E.dma_start(out=out, in_=in_),
                       inc=16)

    def barrier(self):
        todo = [(n, v) for n, v in self.cnt.items() if v > self.bar_mark[n]]
        for e in self.ENGS:
            for sname, val in todo:
                self.wait(e, sname, val)
        for n, v in self.cnt.items():
            self.bar_mark[n] = v

    def flush(self):
        with self.nc.Block() as block:
            def mk(ops):
                def body(E):
                    for f in ops:
                        f(E)
                return body
            if self.q["t"]:
                block.tensor(mk(self.q["t"]))
            if self.q["v"]:
                block.vector(mk(self.q["v"]))
            if self.q["s"]:
                block.scalar(mk(self.q["s"]))
            if self.q["g"]:
                block.gpsimd(mk(self.q["g"]))
            if self.q["y"]:
                block.sync(mk(self.q["y"]))
        self.q = {k: [] for k in self.ENGS}


def build_program(stop=99):
    nc = bass.Bass("TRN2", target_bir_lowering=False)
    dt = nc.dram_tensor
    xbT = dt("xbT", [C, T], F32, kind="ExternalInput").ap()
    xown = dt("xown", [NTOK, C], F32, kind="ExternalInput").ap()
    xownT = dt("xownT", [C, NTOK], F32, kind="ExternalInput").ap()
    wqT = dt("wqT", [C, C], F32, kind="ExternalInput").ap()
    wkT = dt("wkT", [C, C], F32, kind="ExternalInput").ap()
    wvT_in = dt("wvT_in", [C, C], F32, kind="ExternalInput").ap()
    woT_in = dt("woT_in", [C, C], F32, kind="ExternalInput").ap()
    bq_r = dt("bq_r", [128, H], F32, kind="ExternalInput").ap()
    bk_r = dt("bk_r", [128, H], F32, kind="ExternalInput").ap()
    bv_bc_in = dt("bv_bc", [128, C], F32, kind="ExternalInput").ap()
    bo_bc_in = dt("bo_bc", [128, C], F32, kind="ExternalInput").ap()
    gwT = dt("gwT", [C, II], I32, kind="ExternalInput").ap()
    uwT = dt("uwT", [C, II], I32, kind="ExternalInput").ap()
    dwM = dt("dwM", [128, 64, C], I32, kind="ExternalInput").ap()
    gsc_in = dt("gsc", [128, 64], F32, kind="ExternalInput").ap()
    usc_in = dt("usc", [128, 64], F32, kind="ExternalInput").ap()
    dsbc_in = dt("dsbc", [128, C], F32, kind="ExternalInput").ap()
    ckA_in = dt("ckA", [128, T], F32, kind="ExternalInput").ap()
    ckB_in = dt("ckB", [128, T], F32, kind="ExternalInput").ap()
    cqA_in = dt("cqA", [128, NTOK], F32, kind="ExternalInput").ap()
    cqB_in = dt("cqB", [128, NTOK], F32, kind="ExternalInput").ap()
    m1_in = dt("m1_in", [128, 64], F32, kind="ExternalInput").ap()
    m2_in = dt("m2_in", [128, 64], F32, kind="ExternalInput").ap()
    ident_in = dt("ident_in", [128, 128], F32, kind="ExternalInput").ap()
    masks_in = dt("masks_in", [128, NSLOT * 256], F32, kind="ExternalInput").ap()
    masksT_in = dt("masksT_in", [128, NSLOT * 256], F32, kind="ExternalInput").ap()
    out = dt("out", [NTOK, C], F32, kind="ExternalOutput").ap()

    k_sp = dt("k_sp", [H, 128, T], BF16, kind="Internal").ap()
    q_sp = dt("q_sp", [H, 128, NTOK], BF16, kind="Internal").ap()
    v_sp = dt("v_sp", [T, C], BF16, kind="Internal").ap()
    av_sp = dt("av_sp", [H, 128, NTOK], BF16, kind="Internal").ap()
    h_sp = dt("h_sp", [4, 128, 64 * 512], BF16, kind="Internal").ap()
    dbg_sp = dt("dbg_sp", [128, 8192], F32, kind="Internal").ap()
    y1tm_sp = dt("y1tm_sp", [16, 128, C], BF16, kind="Internal").ap()
    x_bf = dt("x_bf", [C, T], BF16, kind="Internal").ap()
    y2_sp = dt("y2_sp", [16, 128, C], BF16, kind="Internal").ap()

    eso = ExitStack()
    EO = eso.enter_context
    sc = Sched(nc, eso)

    ident = EO(nc.sbuf_tensor("ident", [128, 128], BF16))
    identf = EO(nc.sbuf_tensor("identf", [128, 128], F32))
    m1 = EO(nc.sbuf_tensor("m1", [128, 64], BF16))
    m2 = EO(nc.sbuf_tensor("m2", [128, 64], BF16))
    allones = EO(nc.sbuf_tensor("allones", [128, 128], BF16))
    allzeros = EO(nc.sbuf_tensor("allzeros", [128, 128], BF16))
    bqr_sb = EO(nc.sbuf_tensor("bqr_sb", [128, H], F32))
    bkr_sb = EO(nc.sbuf_tensor("bkr_sb", [128, H], F32))
    gsc_sb = EO(nc.sbuf_tensor("gsc_sb", [128, 64], F32))
    usc_sb = EO(nc.sbuf_tensor("usc_sb", [128, 64], F32))
    gbias_sb = EO(nc.sbuf_tensor("gbias_sb", [128, 64], F32))
    ubias_sb = EO(nc.sbuf_tensor("ubias_sb", [128, 64], F32))
    eps_sb = EO(nc.sbuf_tensor("eps_sb", [128, 1], F32))

    for dst, src in ((ident, ident_in), (m1, m1_in), (m2, m2_in)):
        sc.dma("g", "c0g", dst[:], src)
    sc.dma("y", "c0y", identf[:], ident_in)
    for dst, src in ((bqr_sb, bq_r), (bkr_sb, bk_r), (gsc_sb, gsc_in),
                     (usc_sb, usc_in)):
        sc.dma("y", "c0y", dst[:], src)
    sc.wait("v", "c0g", sc.cnt["c0g"])
    sc.wait("v", "c0y", sc.cnt["c0y"])
    sc.opn("v", lambda E: E.memset(allones[:], 1.0))
    sc.opn("v", lambda E: E.memset(allzeros[:], 0.0))
    for dst, src in ((gbias_sb, gsc_sb), (ubias_sb, usc_sb)):
        sc.opn("v", lambda E, d=dst, s=src: E.tensor_scalar_mul(d[:], s[:], -128.0))
    sc.op("v", "V", lambda E: E.memset(eps_sb[:], EPS))
    sc.barrier()
    sc.flush()

    def phase_proj_rope(wT_in, b_sb, cA_in, cB_in, xT_in, ntok, dst_sp, tag):
        es = ExitStack()
        with es:
            wT = es.enter_context(nc.sbuf_tensor(f"wT1b{tag}", [128, 16, C], BF16))
            xq = [es.enter_context(nc.sbuf_tensor(f"xq{tag}{i}", [128, 16, 1024], BF16))
                  for i in range(2)]
            cA = es.enter_context(nc.sbuf_tensor(f"cA1b{tag}", [128, 1024], F32))
            cB = es.enter_context(nc.sbuf_tensor(f"cB1b{tag}", [128, 1024], F32))
            p1st = es.enter_context(nc.sbuf_tensor(f"p1st{tag}", [128, 1024], BF16))
            p2st = es.enter_context(nc.sbuf_tensor(f"p2st{tag}", [128, 1024], BF16))
            kst = [es.enter_context(nc.sbuf_tensor(f"kst{tag}{i}", [128, 1024], BF16))
                   for i in range(2)]
            pk = [es.enter_context(nc.psum_tensor(f"pk{tag}{i}", [128, 1024], F32))
                  for i in range(2)]
            krp = [es.enter_context(nc.psum_tensor(f"krp{tag}{i}", [128, 1024], F32))
                   for i in range(2)]
            for f in range(16):
                wtw = sc.dma("g", f"wt{tag}", wT[:, f, :],
                             wT_in[f * 128:(f + 1) * 128, :])
            nqt = ntok // 1024
            pk_free = [0, 0]
            rope_free = 0
            kr_free = [0, 0]
            krref = 0
            kst_free = [0, 0]
            xq_free = [0, 0]
            for qt in range(nqt):
                xbuf = qt % 2
                sc.wait("g", "T", xq_free[xbuf])
                if tag == 'k':
                    sc.wait("g", "xbf", sc.cnt.get("xbf", 0))
                sc.wait("y", "V", rope_free)
                sc.dma("y", f"tab{tag}", cA[:], cA_in[:, qt * 1024:(qt + 1) * 1024])
                tabw = sc.dma("y", f"tab{tag}", cB[:],
                              cB_in[:, qt * 1024:(qt + 1) * 1024])
                xw = 0
                for f in range(16):
                    xw = sc.dma("g", f"xq{tag}{xbuf}", xq[xbuf][:, f, :],
                                xT_in[f * 128:(f + 1) * 128,
                                      qt * 1024:(qt + 1) * 1024])
                if tag == 'k':
                    sc.wait("y", f"xq{tag}{xbuf}", xw)
                    for f in range(16):
                        sc.dma("y", "xbf", x_bf[f * 128:(f + 1) * 128,
                                               qt * 1024:(qt + 1) * 1024],
                               xq[xbuf][:, f, :])
                for hh in range(H):
                    pb = hh % 2
                    sc.wait("t", "V", pk_free[pb])
                    if hh == 0:
                        sc.wait("t", f"xq{tag}{xbuf}", xw)
                        if qt == 0:
                            sc.wait("t", f"wt{tag}", wtw)
                    for f in range(16):
                        for cpos in (0, 512):
                            last = (f == 15 and cpos == 512)
                            fn = lambda E, pb=pb, f=f, c=cpos, xb2=xbuf, hh=hh: \
                                E.matmul(pk[pb][:, c:c + 512],
                                         wT[:, f, hh * 128:(hh + 1) * 128],
                                         xq[xb2][:, f, c:c + 512],
                                         start=(f == 0), stop=(f == 15))
                            if last:
                                mmref = sc.op("t", "T", fn)
                            else:
                                sc.opn("t", fn)
                    if hh == 15:
                        xq_free[xbuf] = mmref
                    sc.wait("v", "T", mmref)
                    if hh == 0:
                        sc.wait("v", f"tab{tag}", tabw)
                    sc.wait("v", "T", krref)
                    sc.opn("v", lambda E, pb=pb, hh=hh:
                           E.scalar_tensor_tensor(p1st[:], pk[pb][:],
                                                  b_sb[:, hh:hh + 1], cA[:],
                                                  ALU.add, ALU.mult))
                    pk_free[pb] = sc.op(
                        "v", "V", lambda E, pb=pb, hh=hh:
                        E.scalar_tensor_tensor(p2st[:], pk[pb][:],
                                               b_sb[:, hh:hh + 1], cB[:],
                                               ALU.add, ALU.mult))
                    rope_free = pk_free[pb]
                    kb = hh % 2
                    sc.wait("t", "V", rope_free)
                    sc.wait("t", "S", kr_free[kb])
                    for cpos in (0, 512):
                        sc.opn("t", lambda E, c=cpos, kb=kb:
                               E.matmul(krp[kb][0:64, c:c + 512], m1[:],
                                        p1st[:, c:c + 512], start=True, stop=True))
                    sc.opn("t", lambda E, kb=kb:
                           E.matmul(krp[kb][64:128, 0:512], m2[:],
                                    p2st[:, 0:512], start=True, stop=True))
                    krref = sc.op("t", "T", lambda E, kb=kb:
                                  E.matmul(krp[kb][64:128, 512:1024], m2[:],
                                           p2st[:, 512:1024], start=True, stop=True))
                    sc.wait("s", "T", krref)
                    sc.wait("s", f"kst{tag}{kb}", kst_free[kb])
                    kr_free[kb] = sc.op("s", "S", lambda E, kb=kb:
                                        E.activation(kst[kb][:], krp[kb][:],
                                                     AF.Identity))
                    sref = sc.cnt["S"]
                    sc.wait("y", "S", sref)
                    kst_free[kb] = sc.dma(
                        "y", f"kst{tag}{kb}",
                        dst_sp[hh, :, qt * 1024:(qt + 1) * 1024], kst[kb][:])
            sc.barrier()
            sc.flush()

    if stop >= 3:
        phase_proj_rope(wkT, bkr_sb, ckA_in, ckB_in, xbT, T, k_sp, 'k')
    if stop >= 4:
        phase_proj_rope(wqT, bqr_sb, cqA_in, cqB_in, xownT, NTOK, q_sp, 'q')

    def phase_vproj():
        es = ExitStack()
        with es:
            wvT = es.enter_context(nc.sbuf_tensor("wvT", [128, 16, C], BF16))
            bv_bc = es.enter_context(nc.sbuf_tensor("bv_bc_sb", [128, C], F32))
            bvw = sc.dma("y", "bv", bv_bc[:], bv_bc_in)
            for f in range(16):
                wtw = sc.dma("g", "wtv", wvT[:, f, :],
                             wvT_in[f * 128:(f + 1) * 128, :])
            xst = [es.enter_context(nc.sbuf_tensor(f"xst{i}", [128, 16, 512], BF16))
                   for i in range(2)]
            vst = [es.enter_context(nc.sbuf_tensor(f"vst{i}", [128, 1024], BF16))
                   for i in range(2)]
            pv = [es.enter_context(nc.psum_tensor(f"pv{i}", [128, 1024], F32))
                  for i in range(2)]
            pv_free = [0, 0]
            vst_free = [0, 0]
            xst_free = [0, 0]
            xst_w = [0, 0]

            def xload(g):
                xb = g % 2
                sc.wait("y", "T", xst_free[xb])
                for f in range(16):
                    xst_w[xb] = sc.dma("y", f"xv{xb}", xst[xb][:, f, :],
                                       x_bf[f * 128:(f + 1) * 128,
                                            g * 512:(g + 1) * 512])

            xload(0)
            for grp in range(8):
                xbuf = grp % 2
                if grp + 1 < 8:
                    xload(grp + 1)
                xw = xst_w[xbuf]
                lastmm = 0
                for tb in range(4):
                    for dh in range(2):
                        it = tb * 2 + dh
                        pb = it % 2
                        sc.wait("t", "V", pv_free[pb])
                        if tb == 0 and dh == 0:
                            sc.wait("t", f"xv{xbuf}", xw)
                            if grp == 0:
                                sc.wait("t", "wtv", wtw)
                        for f in range(16):
                            for cpos in (0, 512):
                                last = (f == 15 and cpos == 512)
                                fn = lambda E, pb=pb, f=f, c=cpos, xb2=xbuf, \
                                    tb=tb, dh=dh: \
                                    E.matmul(pv[pb][:, c:c + 512],
                                             xst[xb2][:, f, tb * 128:(tb + 1) * 128],
                                             wvT[:, f, dh * 1024 + c:dh * 1024 + c + 512],
                                             start=(f == 0), stop=(f == 15))
                                if last:
                                    mmref = sc.op("t", "T", fn)
                                else:
                                    sc.opn("t", fn)
                        lastmm = mmref
                        vb = it % 2
                        sc.wait("v", "T", mmref)
                        if grp == 0 and it == 0:
                            sc.wait("v", "bv", bvw)
                        sc.wait("v", f"vsp{vb}", vst_free[vb])
                        pv_free[pb] = sc.op(
                            "v", "V", lambda E, pb=pb, vb=vb, dh=dh:
                            E.tensor_add(vst[vb][:], pv[pb][:],
                                         bv_bc[:, dh * 1024:(dh + 1) * 1024]))
                        vref = sc.cnt["V"]
                        sc.wait("y", "V", vref)
                        row0 = (grp * 4 + tb) * 128
                        vst_free[vb] = sc.dma(
                            "y", f"vsp{vb}",
                            v_sp[row0:row0 + 128, dh * 1024:(dh + 1) * 1024],
                            vst[vb][:])
                xst_free[xbuf] = lastmm
            sc.barrier()
            sc.flush()

    if stop >= 5:
        phase_vproj()

    def phase_attn():
        KH = int(os.environ.get("KH", "16"))
        es = ExitStack()
        with es:
            Kh = [es.enter_context(nc.sbuf_tensor(f"Kh{i}", [128, T], BF16))
                  for i in range(2)]
            Qh = [es.enter_context(nc.sbuf_tensor(f"Qh{i}", [128, NTOK], BF16))
                  for i in range(2)]
            v2 = [es.enter_context(nc.sbuf_tensor(f"v2_{i}", [128, 32, 256], BF16))
                  for i in range(2)]
            masksT = es.enter_context(nc.sbuf_tensor("masksT", [128, NSLOT * 256], BF16))
            mkw = sc.dma("g", "mks", masksT[:], masksT_in)
            ptq = [es.enter_context(nc.sbuf_tensor(f"ptq{i}", [128, 1024], BF16))
                   for i in range(3)]
            sums_sb = [es.enter_context(nc.sbuf_tensor(f"sums_sb{i}", [128, 1024], F32))
                       for i in range(2)]
            lnt = es.enter_context(nc.sbuf_tensor("lnt", [128, 1024], F32))
            avout = [es.enter_context(nc.sbuf_tensor(f"avout{i}", [128, 1024], BF16))
                     for i in range(2)]
            sps = [es.enter_context(nc.psum_tensor(f"sps{i}", [128, 1024], F32))
                   for i in range(2)]
            avp = es.enter_context(nc.psum_tensor("avp", [128, 1024], F32))
            smtp = es.enter_context(nc.psum_tensor("smtp", [128, 1024], F32))
            kq_free = [0, 0]
            v2_free = [0, 0]
            spsB_free = [0, 0]
            ptq_free = [0, 0, 0]
            avp_free = 0
            smtp_evac = 0
            lnt_free = 0
            sums_free = [0, 0]
            avout_free = [0, 0]
            spsrot = [0]
            vw_cur = [0]

            def stage_STAV(h, band, hb, kqw):
                nonlocal avp_free, smtp_evac, lnt_free
                bpar = band % 2
                tmax = 16 * band + 16
                hoff = (h % 2) * 128
                vb2 = (h // 2) % 2
                qbase = band * 1024
                erefs = {}

                def st_tile(t):
                    vbase = max(0, t // 2 - band * 8)
                    pb2 = spsrot[0] % 2
                    spsrot[0] += 1
                    sc.wait("t", "S", spsB_free[pb2])
                    c0 = vbase * 128
                    segs = [(c0, 512), (512, 1024)] if c0 < 512 else [(c0, 1024)]
                    jd = t // 2
                    jjd = jd - band * 8 if band * 8 <= jd < band * 8 + 8 else None
                    ref = None
                    for idx, (ca, cb) in enumerate(segs):
                        mask_here = jjd is not None and ca <= jjd * 128 < cb
                        last = (idx == len(segs) - 1)
                        smm = lambda E, pb2=pb2, ca=ca, cb=cb, t=t, hb=hb, \
                            stp=not mask_here: \
                            E.matmul(sps[pb2][:, ca:cb],
                                     Kh[hb][:, t * 128:(t + 1) * 128],
                                     Qh[hb][:, qbase + ca:qbase + cb],
                                     start=True, stop=stp)
                        if last and not mask_here:
                            ref = sc.op("t", "T", smm)
                        else:
                            sc.opn("t", smm)
                        if mask_here:
                            u = t % 2
                            fn = lambda E, pb2=pb2, jjd=jjd, jd=jd, u=u: \
                                E.matmul(sps[pb2][:, jjd * 128:(jjd + 1) * 128],
                                         ident[:],
                                         masksT[:, jd * 256 + u * 128:
                                                jd * 256 + (u + 1) * 128],
                                         start=False, stop=True)
                            if last:
                                ref = sc.op("t", "T", fn)
                            else:
                                sc.opn("t", fn)
                    return ref, pb2, vbase, c0

                def consume(t):
                    eref, rb, vbase, c0 = erefs[t]
                    sc.wait("t", "S", eref)
                    if t == 0:
                        sc.wait("t", "S", smtp_evac)
                    segs = [(c0, 512), (512, 1024)] if c0 < 512 else [(c0, 1024)]
                    for (ca, cb) in segs:
                        sc.opn("t", lambda E, ca=ca, cb=cb, rb=rb, t=t:
                               E.matmul(smtp[:, ca:cb], allones[:],
                                        ptq[rb][:, ca:cb],
                                        start=(t == 0), stop=False))
                    if t == 0:
                        sc.wait("t", "V", avp_free)
                    for si, (ca, cb) in enumerate(segs):
                        fn = lambda E, ca=ca, cb=cb, rb=rb, t=t, vb2=vb2, \
                            hoff=hoff: \
                            E.matmul(avp[:, ca:cb],
                                     v2[vb2][:, t, hoff:hoff + 128],
                                     ptq[rb][:, ca:cb],
                                     start=(t == 0), stop=False)
                        if si == len(segs) - 1:
                            ptq_free[rb] = sc.op("t", "T", fn)
                        else:
                            sc.opn("t", fn)

                if band == 0:
                    sc.wait("t", f"kq{hb}", kqw)
                    if h == 0:
                        sc.wait("t", "mks", mkw)
                    if h % 2 == 0:
                        sc.wait("t", f"v2_{vb2}", vw_cur[0])
                for t in range(tmax):
                    mmref, pb2, vbase, c0 = st_tile(t)
                    rb = t % 3
                    sc.wait("s", "T", mmref)
                    sc.wait("s", "T", ptq_free[rb])
                    spsB_free[pb2] = sc.op(
                        "s", "S", lambda E, pb2=pb2, rb=rb, c0=c0:
                        E.activation(ptq[rb][:, c0:1024], sps[pb2][:, c0:1024],
                                     AF.Exp, bias=0.0, scale=1.0))
                    erefs[t] = (spsB_free[pb2], rb, vbase, c0)
                    if t >= 1:
                        consume(t - 1)
                consume(tmax - 1)
                sc.opn("t", lambda E:
                       E.matmul(avp[:, 0:512], allzeros[:], ptq[0][:, 0:512],
                                start=False, stop=True))
                sc.opn("t", lambda E:
                       E.matmul(avp[:, 512:1024], allzeros[:], ptq[0][:, 512:1024],
                                start=False, stop=True))
                sc.opn("t", lambda E:
                       E.matmul(smtp[:, 0:512], allzeros[:], ptq[0][:, 0:512],
                                start=False, stop=True))
                avref = sc.op("t", "T", lambda E:
                              E.matmul(smtp[:, 512:1024], allzeros[:],
                                       ptq[0][:, 512:1024], start=False, stop=True))
                v2_free[vb2] = avref
                sc.wait("s", "T", avref)
                sc.wait("s", "S", lnt_free)
                smtp_evac = sc.op("s", "S", lambda E:
                                  E.activation(lnt[:], smtp[:], AF.Ln))
                sc.wait("s", "S", smtp_evac)
                sc.wait("s", "V", sums_free[bpar])
                rcref = sc.op("s", "S", lambda E, bpar=bpar:
                              E.activation(sums_sb[bpar][:], lnt[:], AF.Exp,
                                           scale=-1.0))
                lnt_free = rcref
                ab = bpar
                sc.wait("v", "S", rcref)
                sc.wait("v", "T", avref)
                sc.wait("v", f"avo{ab}", avout_free[ab])
                avp_free = sc.op("v", "V", lambda E, ab=ab, bpar=bpar:
                                 E.tensor_tensor(avout[ab][:], avp[:],
                                                 sums_sb[bpar][:], ALU.mult))
                sums_free[bpar] = avp_free
                vref = sc.cnt["V"]
                sc.wait("y", "V", vref)
                avout_free[ab] = sc.dma(
                    "y", f"avo{ab}", av_sp[h, :, band * 1024:(band + 1) * 1024],
                    avout[ab][:])

            def head_loads(h):
                hb2 = h % 2
                sc.wait("y", "T", kq_free[hb2])
                sc.dma("y", f"kq{hb2}", Kh[hb2][:], k_sp[h])
                kqn = sc.dma("y", f"kq{hb2}", Qh[hb2][:], q_sp[h])
                if h % 2 == 0:
                    vb2 = (h // 2) % 2
                    sc.wait("y", "T", v2_free[vb2])
                    vw_cur[0] = sc.dma("y", f"v2_{vb2}", v2[vb2][:],
                                       v_sp.rearrange("(tt p) d -> p tt d", p=128)
                                       [:, :, h * 128:(h + 2) * 128])
                return kqn

            kqw = head_loads(0)
            for h in range(KH):
                hb = h % 2
                stage_STAV(h, 0, hb, kqw)
                kqw_next = head_loads(h + 1) if h + 1 < KH else 0
                stage_STAV(h, 1, hb, kqw)
                kq_free[hb] = sc.cnt["T"]
                kqw = kqw_next
            sc.barrier()
            sc.flush()

    if stop >= 6:
        phase_attn()

    y1nfm_cm = nc.sbuf_tensor("y1nfm", [128, 16, NTOK], BF16)
    y1nfm = y1nfm_cm.__enter__()

    def phase_outproj():
        es = ExitStack()
        with es:
            woT = es.enter_context(nc.sbuf_tensor("woT", [128, 16, C], BF16))
            wtw_f = [None] * 16
            for fg4 in range(4):
                for f4 in range(4):
                    f = fg4 * 4 + f4
                    cntf = sc.dma("g", f"wto{fg4}", woT[:, f, :],
                                  woT_in[f * 128:(f + 1) * 128, :])
                for f4 in range(4):
                    wtw_f[fg4 * 4 + f4] = (f"wto{fg4}", cntf)
            bo_bc = es.enter_context(nc.sbuf_tensor("bo_bc_sb", [128, C], F32))
            bow = sc.dma("y", "bo", bo_bc[:], bo_bc_in)
            avs = [es.enter_context(nc.sbuf_tensor(f"avs{i}", [128, 16, 512], BF16))
                   for i in range(2)]
            xot = [es.enter_context(nc.sbuf_tensor(f"xot{i}", [128, C], F32))
                   for i in range(2)]
            y1f = es.enter_context(nc.sbuf_tensor("y1f", [128, C], F32))
            y1nst = es.enter_context(nc.sbuf_tensor("y1nst", [128, C], BF16))
            bn6 = es.enter_context(nc.sbuf_tensor("bn6", [128, 24], F32))
            mv2 = es.enter_context(nc.sbuf_tensor("mv2", [128, 2], F32))
            sd1 = es.enter_context(nc.sbuf_tensor("sd1", [128, 1], F32))
            rstd = es.enter_context(nc.sbuf_tensor("rstd", [128, 1], F32))
            nmr = es.enter_context(nc.sbuf_tensor("nmr", [128, 1], F32))
            py1 = [es.enter_context(nc.psum_tensor(f"py1_{i}", [128, 1024], F32))
                   for i in range(2)]
            ttp = [es.enter_context(nc.psum_tensor(f"ttp3{i}", [128, 1024], BF16))
                   for i in range(2)]
            py_free = [0, 0]
            ttp_free = [0, 0]
            y1f_free = 0
            y1n_free = 0
            xo_free = [0, 0]
            avs_free = [0, 0]
            for tb in range(16):
                g4 = tb // 4
                if tb % 4 == 0:
                    abuf = g4 % 2
                    sc.wait("y", "T", avs_free[abuf])
                    for d in range(16):
                        avw = sc.dma("y", f"avs{abuf}", avs[abuf][:, d, :],
                                     av_sp[d, :, g4 * 512:(g4 + 1) * 512])
                xbuf = tb % 2
                sc.wait("y", "V", xo_free[xbuf])
                xw = sc.dma("y", f"xo{xbuf}", xot[xbuf][:],
                            xown[tb * 128:(tb + 1) * 128, :])
                abuf = g4 % 2
                toff = (tb % 4) * 128
                lastmm = [0, 0]
                for fh in range(2):
                    pb = fh
                    sc.wait("t", "V", py_free[pb])
                    if tb % 4 == 0 and fh == 0:
                        sc.wait("t", f"avs{abuf}", avw)
                    for d in range(16):
                        if tb == 0 and d % 4 == 0:
                            sc.wait("t", wtw_f[d][0], wtw_f[d][1])
                        for cpos in (0, 512):
                            last = (d == 15 and cpos == 512)
                            fn = lambda E, pb=pb, d=d, c=cpos, fh=fh, ab=abuf, \
                                toff=toff: \
                                E.matmul(py1[pb][:, c:c + 512],
                                         avs[ab][:, d, toff:toff + 128],
                                         woT[:, d, fh * 1024 + c:fh * 1024 + c + 512],
                                         start=(d == 0), stop=(d == 15))
                            if last:
                                lastmm[fh] = sc.op("t", "T", fn)
                            else:
                                sc.opn("t", fn)
                if tb % 4 == 3:
                    avs_free[abuf] = lastmm[1]
                for fh in range(2):
                    pb = fh
                    sc.wait("v", "T", lastmm[fh])
                    if tb == 0 and fh == 0:
                        sc.wait("v", "bo", bow)
                    sc.wait("v", f"xo{xbuf}", xw)
                    sc.wait("v", "S", y1f_free)
                    a1ref = sc.op("v", "V", lambda E, pb=pb, fh=fh:
                                  E.tensor_add(y1f[:, fh * 1024:(fh + 1) * 1024],
                                               py1[pb][:],
                                               bo_bc[:, fh * 1024:(fh + 1) * 1024]))
                    sc.wait("v", "V", a1ref)
                    py_free[pb] = sc.op(
                        "v", "V", lambda E, fh=fh, xbuf=xbuf:
                        E.tensor_add(y1f[:, fh * 1024:(fh + 1) * 1024],
                                     y1f[:, fh * 1024:(fh + 1) * 1024],
                                     xot[xbuf][:, fh * 1024:(fh + 1) * 1024]))
                xo_free[xbuf] = py_free[1]
                sc.wait("v", "V", py_free[1])
                for kb in range(3):
                    sc.opn("v", lambda E, kb=kb:
                           E.bn_stats(bn6[:, kb * 6:(kb + 1) * 6],
                                      y1f[:, kb * 512:(kb + 1) * 512]))
                bsref = sc.op("v", "V", lambda E:
                              E.bn_stats(bn6[:, 18:24], y1f[:, 1536:2048]))
                sc.wait("v", "V", bsref)
                bnref = sc.op("v", "V", lambda E: E.bn_aggr(mv2[:], bn6[:]))
                sc.wait("s", "V", bnref)
                sdref = sc.op("s", "S", lambda E:
                              E.activation(sd1[:], mv2[:, 1:2], AF.Sqrt, bias=eps_sb[:]))
                sc.wait("v", "S", sdref)
                rcref = sc.op("v", "V", lambda E: E.reciprocal(rstd[:], sd1[:]))
                sc.wait("v", "V", rcref)
                vref = sc.op("v", "V", lambda E:
                             E.tensor_scalar(nmr[:], mv2[:, 0:1], rstd[:], -1.0,
                                             ALU.mult, ALU.mult))
                sc.wait("s", "V", vref)
                sc.wait("s", "T", y1n_free)
                sc.wait("s", "y1sp", sc.cnt.get("y1sp", 0))
                y1f_free = sc.op("s", "S", lambda E:
                                 E.activation(y1nst[:], y1f[:], AF.Identity,
                                              bias=nmr[:], scale=rstd[:]))
                sc.wait("y", "S", y1f_free)
                sc.dma("y", "y1sp", y1tm_sp[tb], y1nst[:])
                sc.wait("t", "S", y1f_free)
                for fg in range(2):
                    pb4 = fg
                    sc.wait("t", "S", ttp_free[pb4])
                    for f8 in range(7):
                        f = fg * 8 + f8
                        sc.opn("t", lambda E, pb4=pb4, f8=f8, f=f:
                               E.transpose(ttp[pb4][:, f8 * 128:(f8 + 1) * 128],
                                           y1nst[:, f * 128:(f + 1) * 128],
                                           ident[:]))
                    f = fg * 8 + 7
                    tref = sc.op("t", "T", lambda E, pb4=pb4, f=f:
                                 E.transpose(ttp[pb4][:, 7 * 128:8 * 128],
                                             y1nst[:, f * 128:(f + 1) * 128],
                                             ident[:]))
                    y1n_free = tref
                    sc.wait("s", "T", tref)
                    ttp_free[pb4] = sc.op(
                        "s", "S", lambda E, pb4=pb4, fg=fg, tb=tb:
                        E.activation(
                            y1nfm[:, fg * 8:(fg + 1) * 8, tb * 128:(tb + 1) * 128],
                            ttp[pb4][:].rearrange("p (a b) -> p a b", a=8),
                            AF.Identity))
            sc.barrier()
            sc.flush()

    if stop >= 7:
        phase_outproj()


    def phase_gateup():
        es = ExitStack()
        with es:
            wg_sb = [es.enter_context(nc.sbuf_tensor(f"wg_sb{i}", [128, 16, 512], BF16))
                     for i in range(2)]
            wu_sb = [es.enter_context(nc.sbuf_tensor(f"wu_sb{i}", [128, 16, 512], BF16))
                     for i in range(2)]
            tg = es.enter_context(nc.sbuf_tensor("tg", [128, 1024], BF16))
            tu = es.enter_context(nc.sbuf_tensor("tu", [128, 1024], BF16))
            gsl = es.enter_context(nc.sbuf_tensor("gsl", [128, 1024], BF16))
            hst = [es.enter_context(nc.sbuf_tensor(f"hst{i}", [128, 1024], BF16))
                   for i in range(2)]
            sy_bc = es.enter_context(nc.sbuf_tensor("sy_bc", [128, C], F32))
            gps = es.enter_context(nc.psum_tensor("gps", [128, 1024], F32))
            ups = es.enter_context(nc.psum_tensor("ups", [128, 1024], F32))
            syp = es.enter_context(nc.psum_tensor("syp", [128, 1024], F32))
            syref = 0
            for c4 in range(2):
                sc.wait("t", "V", syref)
                for f in range(16):
                    for cpos in (0, 512):
                        last = (f == 15 and cpos == 512)
                        fn = lambda E, f=f, c=cpos, c4=c4: \
                            E.matmul(syp[:, c:c + 512], allones[:],
                                     y1nfm[:, f, c4 * 1024 + c:c4 * 1024 + c + 512],
                                     start=(f == 0), stop=(f == 15))
                        if last:
                            smr = sc.op("t", "T", fn)
                        else:
                            sc.opn("t", fn)
                sc.wait("v", "T", smr)
                syref = sc.op("v", "V", lambda E, c4=c4:
                              E.tensor_copy(sy_bc[:, c4 * 1024:(c4 + 1) * 1024],
                                            syp[:]))
            wfree = [0, 0]
            gps_free = 0
            ups_free = 0
            tg_free = 0
            tu_free = 0
            gsl_free = 0
            hst_free = [0, 0]
            for icg in range(16):
                buf = icg % 2
                sc.wait("g", "T", wfree[buf])
                for f in range(16):
                    sc.dma("g", f"gu{buf}", wg_sb[buf][:, f, :],
                           gwT[f * 128:(f + 1) * 128, icg * 512:(icg + 1) * 512])
                    ww = sc.dma("g", f"gu{buf}", wu_sb[buf][:, f, :],
                                uwT[f * 128:(f + 1) * 128, icg * 512:(icg + 1) * 512])
                for ici in range(4):
                    ic = icg * 4 + ici
                    for th in range(2):
                        sc.wait("t", "V", gps_free)
                        if ici == 0 and th == 0:
                            sc.wait("t", f"gu{buf}", ww)
                            if icg == 0:
                                sc.wait("t", "V", syref)
                        for f in range(16):
                            for cpos in (0, 512):
                                last = (f == 15 and cpos == 512)
                                fn = lambda E, f=f, c=cpos, th=th, buf=buf, ici=ici: \
                                    E.matmul(gps[:, c:c + 512],
                                             wg_sb[buf][:, f, ici * 128:(ici + 1) * 128],
                                             y1nfm[:, f, th * 1024 + c:th * 1024 + c + 512],
                                             start=(f == 0), stop=(f == 15))
                                if last:
                                    gmref = sc.op("t", "T", fn)
                                else:
                                    sc.opn("t", fn)
                        sc.wait("v", "T", gmref)
                        sc.wait("v", "S", tg_free)
                        gps_free = sc.op(
                            "v", "V", lambda E, th=th, ic=ic:
                            E.scalar_tensor_tensor(
                                tg[:], sy_bc[:, th * 1024:(th + 1) * 1024],
                                -128.0, gps[:], ALU.mult, ALU.add))
                        sc.wait("s", "V", gps_free)
                        sc.wait("s", "V", gsl_free)
                        tg_free = sc.op("s", "S", lambda E, ic=ic:
                                        E.activation(gsl[:], tg[:], AF.Silu,
                                                     scale=gsc_sb[:, ic:ic + 1]))
                        sc.wait("t", "V", ups_free)
                        for f in range(16):
                            for cpos in (0, 512):
                                last = (f == 15 and cpos == 512)
                                fn = lambda E, f=f, c=cpos, th=th, buf=buf, ici=ici: \
                                    E.matmul(ups[:, c:c + 512],
                                             wu_sb[buf][:, f, ici * 128:(ici + 1) * 128],
                                             y1nfm[:, f, th * 1024 + c:th * 1024 + c + 512],
                                             start=(f == 0), stop=(f == 15))
                                if last:
                                    umref = sc.op("t", "T", fn)
                                else:
                                    sc.opn("t", fn)
                        if ici == 3 and th == 1:
                            wfree[buf] = umref
                        hb = th % 2
                        sc.wait("v", "T", umref)
                        sc.wait("v", "V", tu_free)
                        ups_free = sc.op(
                            "v", "V", lambda E, th=th, ic=ic:
                            E.scalar_tensor_tensor(
                                tu[:], sy_bc[:, th * 1024:(th + 1) * 1024],
                                -128.0, ups[:], ALU.mult, ALU.add))
                        sc.wait("v", "V", ups_free)
                        sc.wait("v", "S", tg_free)
                        sc.wait("v", f"hsp{hb}", hst_free[hb])
                        tu_free = sc.op(
                            "v", "V", lambda E, hb=hb, ic=ic:
                            E.scalar_tensor_tensor(hst[hb][:], tu[:],
                                                   usc_sb[:, ic:ic + 1], gsl[:],
                                                   ALU.mult, ALU.mult))
                        gsl_free = tu_free
                        vref = sc.cnt["V"]
                        sc.wait("y", "V", vref)
                        sc.dma("y", f"hsp{hb}",
                               h_sp[2 * th, :, ic * 512:(ic + 1) * 512],
                               hst[hb][:, 0:512])
                        hst_free[hb] = sc.dma(
                            "y", f"hsp{hb}",
                            h_sp[2 * th + 1, :, ic * 512:(ic + 1) * 512],
                            hst[hb][:, 512:1024])
            sc.barrier()
            sc.flush()

    if stop >= 9:
        phase_gateup()

    def phase_down():
        es = ExitStack()
        with es:
            wdm = [es.enter_context(nc.sbuf_tensor(f"wdm{i}", [128, 64, 512], BF16))
                   for i in range(2)]
            hsb = [es.enter_context(nc.sbuf_tensor(f"hsb{i}", [128, 32, 128], BF16))
                   for i in range(3)]
            y1t = [es.enter_context(nc.sbuf_tensor(f"y1t{i}", [128, 512], BF16))
                   for i in range(2)]
            dsbc = es.enter_context(nc.sbuf_tensor("dsbc_sb", [128, C], F32))
            y2row = es.enter_context(nc.sbuf_tensor("y2row", [128, C], BF16))
            yfsl = es.enter_context(nc.sbuf_tensor("yfsl", [128, 512], F32))
            yfb = [es.enter_context(nc.sbuf_tensor(f"yfb{i}", [128, 512], BF16))
                   for i in range(2)]
            shcol = es.enter_context(nc.sbuf_tensor("shcol", [128, 16], F32))
            bnacc = es.enter_context(nc.sbuf_tensor("bnacc", [128, 16 * 24], F32))
            mv2 = es.enter_context(nc.sbuf_tensor("mv2b", [128, 2], F32))
            sd1 = es.enter_context(nc.sbuf_tensor("sd1b", [128, 1], F32))
            rstd = es.enter_context(nc.sbuf_tensor("rstdb", [128, 1], F32))
            nmr = es.enter_context(nc.sbuf_tensor("nmrb", [128, 1], F32))
            outst = [es.enter_context(nc.sbuf_tensor(f"outst{i}", [128, 1024], F32))
                     for i in range(2)]
            y2p = [es.enter_context(nc.psum_tensor(f"y2p{i}", [128, 512], F32))
                   for i in range(3)]
            shp = es.enter_context(nc.psum_tensor("shp", [128, 16], F32))
            dsw = sc.dma("y", "dsb", dsbc[:], dsbc_in)

            wdm_free = [0, 0]
            wload = [0, 0]
            hsb_free = [0, 0, 0]
            hload = [0, 0, 0]
            y1t_free = [0, 0]
            y2p_freeV = [0, 0, 0]
            yfb_free = [0, 0]
            y2row_free = 0
            outst_free = [0, 0]
            yfsl_free = 0

            def h_load(ci):
                r = ci % 32
                tb_, hh = r // 2, r % 2
                g4, off = tb_ // 4, (tb_ % 4) * 128
                sl = ci % 3
                sc.wait("y", "T", hsb_free[sl])
                hload[sl] = sc.dma(
                    "y", f"hsb{sl}", hsb[sl][:],
                    h_sp[g4].rearrange("p (ic c) -> p ic c", ic=64)
                    [:, hh * 32:(hh + 1) * 32, off:off + 128])
                return hload[sl]

            wl0a = sc.dma("g", "wdl0h", wdm[0][:, 0:32, :], dwM[:, 0:32, 0:512])
            wl0b = sc.dma("g", "wdl0", wdm[0][:, 32:64, :], dwM[:, 32:64, 0:512])
            wload[0] = [("wdl0h", wl0a), ("wdl0", wl0b)]
            h_load(0)
            h_load(1)
            for cg in range(4):
                wb = cg % 2
                if cg + 1 < 4:
                    nwb = (cg + 1) % 2
                    sc.wait("g", "T", wdm_free[nwb])
                    wn = sc.dma("g", f"wdl{nwb}", wdm[nwb][:],
                                dwM[:, :, (cg + 1) * 512:(cg + 2) * 512])
                    wload[nwb] = [(f"wdl{nwb}", wn), (f"wdl{nwb}", wn)]
                for tb in range(16):
                    ci0 = cg * 32 + tb * 2
                    if ci0 + 2 < 128:
                        h_load(ci0 + 2)
                    pb = (cg * 16 + tb) % 3
                    rbt = tb % 2
                    sc.wait("y", "V", y1t_free[rbt])
                    ylw = sc.dma("y", f"y1t{rbt}", y1t[rbt][:],
                                 y1tm_sp[tb][:, cg * 512:(cg + 1) * 512])
                    sc.wait("t", "V", y2p_freeV[pb])
                    shmm = 0
                    mmref = 0
                    for hh in range(2):
                        sl = (ci0 + hh) % 3
                        sc.wait("t", f"hsb{sl}", hload[sl])
                        if tb == 0:
                            sc.wait("t", wload[wb][hh][0], wload[wb][hh][1])
                        for i2 in range(32):
                            ic = hh * 32 + i2
                            lastm = (ic == 63)
                            fn = lambda E, pb=pb, sl=sl, i2=i2, ic=ic, wb=wb: \
                                E.matmul(y2p[pb][:],
                                         hsb[sl][:, i2, :],
                                         wdm[wb][:, ic, :],
                                         start=(ic == 0), stop=(ic == 63))
                            if lastm or (cg != 0 and i2 == 31):
                                mmref = sc.op("t", "T", fn)
                            else:
                                sc.opn("t", fn)
                            if cg == 0:
                                fs = lambda E, sl=sl, i2=i2, ic=ic, tb=tb: \
                                    E.matmul(shp[:, tb:tb + 1],
                                             hsb[sl][:, i2, :],
                                             allones[:, 0:1],
                                             start=(ic == 0), stop=(ic == 63))
                                if i2 == 31:
                                    shmm = sc.op("t", "T", fs)
                                else:
                                    sc.opn("t", fs)
                        hsb_free[sl] = sc.cnt["T"]
                        if hh == 0 and ci0 + 3 < 128:
                            h_load(ci0 + 3)
                    if tb == 15:
                        wdm_free[wb] = mmref if cg != 0 else shmm
                    if cg == 0:
                        sc.wait("v", "T", shmm)
                        shv = sc.op("v", "V", lambda E, tb=tb:
                                    E.tensor_scalar_mul(shcol[:, tb:tb + 1],
                                                        shp[:, tb:tb + 1], -128.0))
                        sc.wait("v", "V", shv)
                    sc.wait("v", "T", mmref)
                    sc.wait("v", f"y1t{rbt}", ylw)
                    if cg == 0 and tb == 0:
                        sc.wait("v", "dsb", dsw)
                    sc.wait("v", "V", yfsl_free)
                    v1 = sc.op("v", "V", lambda E, pb=pb, cg=cg, tb=tb:
                               E.scalar_tensor_tensor(
                                   yfsl[:], y2p[pb][:], shcol[:, tb:tb + 1],
                                   dsbc[:, cg * 512:(cg + 1) * 512],
                                   ALU.add, ALU.mult))
                    y2p_freeV[pb] = v1
                    sc.wait("v", "V", v1)
                    sc.wait("v", f"yspl{rbt}", yfb_free[rbt])
                    v2 = sc.op("v", "V", lambda E, rbt=rbt:
                               E.tensor_add(yfb[rbt][:], yfsl[:], y1t[rbt][:]))
                    yfsl_free = v2
                    y1t_free[rbt] = v2
                    sc.wait("v", "V", v2)
                    v3 = sc.op("v", "V", lambda E, rbt=rbt, tb=tb, cg=cg:
                               E.bn_stats(bnacc[:, tb * 24 + cg * 6:
                                                 tb * 24 + cg * 6 + 6],
                                          yfb[rbt][:]))
                    sc.wait("y", "V", v3)
                    yfb_free[rbt] = sc.dma("y", f"yspl{rbt}",
                                           y2_sp[tb][:, cg * 512:(cg + 1) * 512],
                                           yfb[rbt][:])
                    if cg == 3:
                        sc.wait("v", "V", v3)
                        bnref = sc.op("v", "V", lambda E, tb=tb:
                                      E.bn_aggr(mv2[:], bnacc[:, tb * 24:
                                                               (tb + 1) * 24]))
                        sc.wait("s", "V", bnref)
                        sdref = sc.op("s", "S", lambda E:
                                      E.activation(sd1[:], mv2[:, 1:2], AF.Sqrt,
                                                   bias=eps_sb[:]))
                        sc.wait("v", "S", sdref)
                        rcref = sc.op("v", "V", lambda E:
                                      E.reciprocal(rstd[:], sd1[:]))
                        sc.wait("v", "V", rcref)
                        vref = sc.op("v", "V", lambda E:
                                     E.tensor_scalar(nmr[:], mv2[:, 0:1], rstd[:],
                                                     -1.0, ALU.mult, ALU.mult))
                        sc.wait("y", "S", y2row_free)
                        sc.wait("y", "yspl0", sc.cnt.get("yspl0", 0))
                        sc.wait("y", "yspl1", sc.cnt.get("yspl1", 0))
                        y2rw = sc.dma("y", "y2r", y2row[:], y2_sp[tb])
                        sc.wait("s", "y2r", y2rw)
                        sc.wait("s", "V", vref)
                        row0 = tb * 128
                        for half in range(2):
                            sc.wait("s", f"outd{half}", outst_free[half])
                            apref = sc.op(
                                "s", "S", lambda E, half=half:
                                E.activation(outst[half][:],
                                             y2row[:, half * 1024:
                                                   (half + 1) * 1024],
                                             AF.Identity, bias=nmr[:],
                                             scale=rstd[:]))
                            if half == 1:
                                y2row_free = apref
                            sc.wait("y", "S", apref)
                            outst_free[half] = sc.dma(
                                "y", f"outd{half}",
                                out[row0:row0 + 128,
                                    half * 1024:(half + 1) * 1024],
                                outst[half][:])
            sc.wait("y", "outd0", sc.cnt.get("outd0", 0))
            sc.wait("y", "outd1", sc.cnt.get("outd1", 0))
            sc.barrier()
            sc.flush()

    y1nfm_cm.__exit__(None, None, None)
    if stop >= 10:
        phase_down()
    eso.close()
    return nc


_CACHE = {}
_LEAK = []


def _host_tables(par):
    th = np.float32(10000.0) ** (-(np.arange(DH, dtype=np.float32) / np.float32(DH)))
    pos_all = np.arange(T, dtype=np.float32)
    ang = pos_all[:, None] * th[None, :]
    cosk, sink = np.cos(ang).astype(np.float32), np.sin(ang).astype(np.float32)
    ckA = np.concatenate([cosk.T, sink.T], axis=0)
    ckB = np.concatenate([sink.T, cosk.T], axis=0)
    own_blocks = [2 * j + par for j in range(NSLOT)]
    pos_own = np.concatenate([np.arange(g * 128, (g + 1) * 128) for g in own_blocks])
    ango = pos_own[:, None].astype(np.float32) * th[None, :]
    cosq, sinq = np.cos(ango).astype(np.float32), np.sin(ango).astype(np.float32)
    s = np.float32(1.0 / math.sqrt(D))
    cqA = np.concatenate([cosq.T, sinq.T], axis=0) * s
    cqB = np.concatenate([sinq.T, cosq.T], axis=0) * s
    M = np.zeros((NSLOT, 128, 256), dtype=np.float32)
    kk = np.arange(256)[None, :]
    qq = np.arange(128)[:, None]
    for j in range(NSLOT):
        lim = qq + (0 if par == 0 else 128)
        M[j] = np.where(kk <= lim, 0.0, NEG)
    masks = M.transpose(1, 0, 2).reshape(128, NSLOT * 256)
    MT = np.zeros((NSLOT, 2, 128, 128), dtype=np.float32)
    for j in range(NSLOT):
        for u in range(2):
            MT[j, u] = M[j][:, u * 128:(u + 1) * 128].T
    masksT = MT.transpose(2, 0, 1, 3).reshape(128, NSLOT * 256)
    return ckA, ckB, cqA, cqB, masks, masksT, own_blocks


def kernel(L0_attn_x, L0_attn_w_qkv, L0_attn_b_qkv, L0_attn_w_out, L0_attn_b_out,
           L0_ffn_gate_w, L0_ffn_gate_scale, L0_ffn_up_w, L0_ffn_up_scale,
           L0_ffn_down_w, L0_ffn_down_scale):
    x = np.asarray(L0_attn_x, dtype=np.float32)
    wqkv = np.asarray(L0_attn_w_qkv, dtype=np.float32)
    bqkv = np.asarray(L0_attn_b_qkv, dtype=np.float32)
    wout = np.ascontiguousarray(np.asarray(L0_attn_w_out, dtype=np.float32))
    bout = np.asarray(L0_attn_b_out, dtype=np.float32)
    gwn = np.ascontiguousarray(np.asarray(L0_ffn_gate_w, dtype=np.int32))
    gscale = np.asarray(L0_ffn_gate_scale, dtype=np.float32)
    uwn = np.ascontiguousarray(np.asarray(L0_ffn_up_w, dtype=np.int32))
    uscale = np.asarray(L0_ffn_up_scale, dtype=np.float32)
    dwn = np.ascontiguousarray(np.asarray(L0_ffn_down_w, dtype=np.int32))
    dscale = np.asarray(L0_ffn_down_scale, dtype=np.float32)

    pair = np.concatenate([np.arange(0, 128, 2), np.arange(1, 128, 2)])
    perm = (np.arange(H)[:, None] * 128 + pair[None, :]).reshape(-1)
    wqT_h = np.ascontiguousarray(wqkv[0:C][perm].T)
    wkT_h = np.ascontiguousarray(wqkv[C:2 * C][perm].T)
    wvT_h = np.ascontiguousarray(wqkv[2 * C:3 * C].T)
    woT_h = np.ascontiguousarray(wout.T)
    bq_r = np.ascontiguousarray(bqkv[0:C][perm].reshape(H, 128).T)
    bk_r = np.ascontiguousarray(bqkv[C:2 * C][perm].reshape(H, 128).T)
    bv_bc = np.ascontiguousarray(np.broadcast_to(bqkv[2 * C:3 * C], (128, C)))
    bo_bc = np.ascontiguousarray(np.broadcast_to(bout, (128, C)))
    gwT_h = np.ascontiguousarray(gwn.T)
    uwT_h = np.ascontiguousarray(uwn.T)
    dwM_h = np.ascontiguousarray(dwn.T.reshape(64, 128, C).transpose(1, 0, 2))
    gsc = np.ascontiguousarray(gscale.reshape(64, 128).T)
    usc = np.ascontiguousarray(uscale.reshape(64, 128).T)
    dsbc_h = np.ascontiguousarray(np.broadcast_to(dscale, (128, C)))
    eye = np.eye(64, dtype=np.float32)
    m1 = np.concatenate([eye, -eye], axis=0)
    m2 = np.concatenate([eye, eye], axis=0)
    identity = np.eye(128, dtype=np.float32)

    _stop = int(os.environ.get("KSTOP", "99"))
    if "nc" not in _CACHE:
        _CACHE["nc"] = build_program(_stop)
    nc = _CACHE["nc"]

    tabs = {par: _host_tables(par) for par in range(2)}
    in_maps = []
    own_rows = {}
    for c in range(NCORE):
        b, par = c // 2, c % 2
        ckA, ckB, cqA, cqB, masks, masksT, own_blocks = tabs[par]
        rows = np.concatenate([np.arange(g * 128, (g + 1) * 128) for g in own_blocks])
        own_rows[c] = rows
        xo = np.ascontiguousarray(x[b][rows])
        in_maps.append({
            "xbT": np.ascontiguousarray(x[b].T),
            "xown": xo,
            "xownT": np.ascontiguousarray(xo.T),
            "wqT": wqT_h, "wkT": wkT_h, "wvT_in": wvT_h, "woT_in": woT_h,
            "bq_r": bq_r, "bk_r": bk_r, "bv_bc": bv_bc, "bo_bc": bo_bc,
            "gwT": gwT_h, "uwT": uwT_h, "dwM": dwM_h,
            "gsc": gsc, "usc": usc, "dsbc": dsbc_h,
            "ckA": np.ascontiguousarray(ckA), "ckB": np.ascontiguousarray(ckB),
            "cqA": np.ascontiguousarray(cqA), "cqB": np.ascontiguousarray(cqB),
            "m1_in": m1, "m2_in": m2, "ident_in": identity,
            "masks_in": np.ascontiguousarray(masks),
            "masksT_in": np.ascontiguousarray(masksT),
        })

    res = run_bass_kernel_spmd(nc, in_maps, core_ids=list(range(NCORE)))
    outf = np.empty((B, T, C), dtype=np.float32)
    for c in range(NCORE):
        b = c // 2
        outf[b][own_rows[c]] = np.asarray(res.results[c]["out"])
    return outf


# version 14
# speedup vs baseline: 1.1929x; 1.0740x over previous
import math
import numpy as np
import ml_dtypes
from contextlib import ExitStack

import os
import concourse.bass as bass
import concourse.mybir as mybir
from concourse.bass_utils import run_bass_kernel_spmd

F32 = mybir.dt.float32
BF16 = mybir.dt.bfloat16
I32 = mybir.dt.int32
AF = mybir.ActivationFunctionType
ALU = mybir.AluOpType
AXX = mybir.AxisListType.X

B, T, C, H, D, DH = 4, 4096, 2048, 16, 128, 64
II = 8192
NCORE = 8
NTOK = 2048
NSLOT = 16
NEG = -1.0e9
EPS = 1.0e-5


class Sched:
    ENGS = ("t", "v", "s", "g", "y")

    def __init__(self, nc, es):
        self.nc = nc
        self.es = es
        self.q = {k: [] for k in self.ENGS}
        self.sem = {}
        self.cnt = {}
        self.bar_mark = {}
        for name in ("T", "V", "S"):
            self._mk(name)

    def _mk(self, name):
        self.sem[name] = self.es.enter_context(self.nc.semaphore(name))
        self.cnt[name] = 0
        self.bar_mark[name] = 0

    def op(self, eng, semname, fn, inc=1):
        self.cnt[semname] += inc
        sem = self.sem[semname]

        def play(E, fn=fn, sem=sem, inc=inc):
            fn(E).then_inc(sem, inc)

        self.q[eng].append(play)
        return self.cnt[semname]

    def opn(self, eng, fn):
        self.q[eng].append(fn)

    def wait(self, eng, semname, val):
        if val <= 0:
            return
        sem = self.sem[semname]
        self.q[eng].append(lambda E, sem=sem, val=val: E.wait_ge(sem, val))

    def dma(self, eng, slot, out, in_):
        if slot not in self.sem:
            self._mk(slot)
        return self.op(eng, slot,
                       lambda E, out=out, in_=in_: E.dma_start(out=out, in_=in_),
                       inc=16)

    def barrier(self):
        todo = [(n, v) for n, v in self.cnt.items() if v > self.bar_mark[n]]
        for e in self.ENGS:
            for sname, val in todo:
                self.wait(e, sname, val)
        for n, v in self.cnt.items():
            self.bar_mark[n] = v

    def flush(self):
        with self.nc.Block() as block:
            def mk(ops):
                def body(E):
                    for f in ops:
                        f(E)
                return body
            if self.q["t"]:
                block.tensor(mk(self.q["t"]))
            if self.q["v"]:
                block.vector(mk(self.q["v"]))
            if self.q["s"]:
                block.scalar(mk(self.q["s"]))
            if self.q["g"]:
                block.gpsimd(mk(self.q["g"]))
            if self.q["y"]:
                block.sync(mk(self.q["y"]))
        self.q = {k: [] for k in self.ENGS}


def build_program(stop=99):
    nc = bass.Bass("TRN2", target_bir_lowering=False)
    dt = nc.dram_tensor
    xbT = dt("xbT", [C, T], F32, kind="ExternalInput").ap()
    xown = dt("xown", [NTOK, C], F32, kind="ExternalInput").ap()
    xownT = dt("xownT", [C, NTOK], F32, kind="ExternalInput").ap()
    wqT = dt("wqT", [C, C], F32, kind="ExternalInput").ap()
    wkT = dt("wkT", [C, C], F32, kind="ExternalInput").ap()
    wvT_in = dt("wvT_in", [C, C], F32, kind="ExternalInput").ap()
    woT_in = dt("woT_in", [C, C], F32, kind="ExternalInput").ap()
    bq_r = dt("bq_r", [128, H], F32, kind="ExternalInput").ap()
    bk_r = dt("bk_r", [128, H], F32, kind="ExternalInput").ap()
    bv_bc_in = dt("bv_bc", [128, C], F32, kind="ExternalInput").ap()
    bo_bc_in = dt("bo_bc", [128, C], F32, kind="ExternalInput").ap()
    gwT = dt("gwT", [C, II], I32, kind="ExternalInput").ap()
    uwT = dt("uwT", [C, II], I32, kind="ExternalInput").ap()
    dwM = dt("dwM", [128, 64, C], I32, kind="ExternalInput").ap()
    gsc_in = dt("gsc", [128, 64], F32, kind="ExternalInput").ap()
    usc_in = dt("usc", [128, 64], F32, kind="ExternalInput").ap()
    dsbc_in = dt("dsbc", [128, C], F32, kind="ExternalInput").ap()
    ckA_in = dt("ckA", [128, T], F32, kind="ExternalInput").ap()
    ckB_in = dt("ckB", [128, T], F32, kind="ExternalInput").ap()
    cqA_in = dt("cqA", [128, NTOK], F32, kind="ExternalInput").ap()
    cqB_in = dt("cqB", [128, NTOK], F32, kind="ExternalInput").ap()
    m1_in = dt("m1_in", [128, 64], F32, kind="ExternalInput").ap()
    m2_in = dt("m2_in", [128, 64], F32, kind="ExternalInput").ap()
    ident_in = dt("ident_in", [128, 128], F32, kind="ExternalInput").ap()
    masks_in = dt("masks_in", [128, NSLOT * 256], F32, kind="ExternalInput").ap()
    masksT_in = dt("masksT_in", [128, NSLOT * 256], F32, kind="ExternalInput").ap()
    out = dt("out", [NTOK, C], F32, kind="ExternalOutput").ap()

    k_sp = dt("k_sp", [H, 128, T], BF16, kind="Internal").ap()
    q_sp = dt("q_sp", [H, 128, NTOK], BF16, kind="Internal").ap()
    v_sp = dt("v_sp", [T, C], BF16, kind="Internal").ap()
    av_sp = dt("av_sp", [H, 128, NTOK], BF16, kind="Internal").ap()
    h_sp = dt("h_sp", [4, 128, 64 * 512], BF16, kind="Internal").ap()
    dbg_sp = dt("dbg_sp", [128, 8192], F32, kind="Internal").ap()
    y1tm_sp = dt("y1tm_sp", [16, 128, C], BF16, kind="Internal").ap()
    x_bf = dt("x_bf", [C, T], BF16, kind="Internal").ap()
    y2_sp = dt("y2_sp", [16, 128, C], BF16, kind="Internal").ap()

    eso = ExitStack()
    EO = eso.enter_context
    sc = Sched(nc, eso)

    ident = EO(nc.sbuf_tensor("ident", [128, 128], BF16))
    identf = EO(nc.sbuf_tensor("identf", [128, 128], F32))
    m1 = EO(nc.sbuf_tensor("m1", [128, 64], BF16))
    m2 = EO(nc.sbuf_tensor("m2", [128, 64], BF16))
    allones = EO(nc.sbuf_tensor("allones", [128, 128], BF16))
    allzeros = EO(nc.sbuf_tensor("allzeros", [128, 128], BF16))
    bqr_sb = EO(nc.sbuf_tensor("bqr_sb", [128, H], F32))
    bkr_sb = EO(nc.sbuf_tensor("bkr_sb", [128, H], F32))
    gsc_sb = EO(nc.sbuf_tensor("gsc_sb", [128, 64], F32))
    usc_sb = EO(nc.sbuf_tensor("usc_sb", [128, 64], F32))
    gbias_sb = EO(nc.sbuf_tensor("gbias_sb", [128, 64], F32))
    ubias_sb = EO(nc.sbuf_tensor("ubias_sb", [128, 64], F32))
    eps_sb = EO(nc.sbuf_tensor("eps_sb", [128, 1], F32))

    for dst, src in ((ident, ident_in), (m1, m1_in), (m2, m2_in)):
        sc.dma("g", "c0g", dst[:], src)
    sc.dma("y", "c0y", identf[:], ident_in)
    for dst, src in ((bqr_sb, bq_r), (bkr_sb, bk_r), (gsc_sb, gsc_in),
                     (usc_sb, usc_in)):
        sc.dma("y", "c0y", dst[:], src)
    sc.wait("v", "c0g", sc.cnt["c0g"])
    sc.wait("v", "c0y", sc.cnt["c0y"])
    sc.opn("v", lambda E: E.memset(allones[:], 1.0))
    sc.opn("v", lambda E: E.memset(allzeros[:], 0.0))
    for dst, src in ((gbias_sb, gsc_sb), (ubias_sb, usc_sb)):
        sc.opn("v", lambda E, d=dst, s=src: E.tensor_scalar_mul(d[:], s[:], -128.0))
    sc.op("v", "V", lambda E: E.memset(eps_sb[:], EPS))
    sc.barrier()
    sc.flush()

    def phase_proj_rope(wT_in, b_sb, cA_in, cB_in, xT_in, ntok, dst_sp, tag):
        es = ExitStack()
        with es:
            wT = es.enter_context(nc.sbuf_tensor(f"wT1b{tag}", [128, 16, C], BF16))
            xq = [es.enter_context(nc.sbuf_tensor(f"xq{tag}{i}", [128, 16, 1024], BF16))
                  for i in range(2)]
            cA = es.enter_context(nc.sbuf_tensor(f"cA1b{tag}", [128, 1024], F32))
            cB = es.enter_context(nc.sbuf_tensor(f"cB1b{tag}", [128, 1024], F32))
            p1st = es.enter_context(nc.sbuf_tensor(f"p1st{tag}", [128, 1024], BF16))
            p2st = es.enter_context(nc.sbuf_tensor(f"p2st{tag}", [128, 1024], BF16))
            kst = [es.enter_context(nc.sbuf_tensor(f"kst{tag}{i}", [128, 1024], BF16))
                   for i in range(2)]
            pk = [es.enter_context(nc.psum_tensor(f"pk{tag}{i}", [128, 1024], F32))
                  for i in range(2)]
            krp = [es.enter_context(nc.psum_tensor(f"krp{tag}{i}", [128, 1024], F32))
                   for i in range(2)]
            for f in range(16):
                wtw = sc.dma("g", f"wt{tag}", wT[:, f, :],
                             wT_in[f * 128:(f + 1) * 128, :])
            nqt = ntok // 1024
            pk_free = [0, 0]
            rope_free = 0
            kr_free = [0, 0]
            krref = 0
            kst_free = [0, 0]
            xq_free = [0, 0]
            for qt in range(nqt):
                xbuf = qt % 2
                sc.wait("g", "T", xq_free[xbuf])
                if tag == 'k':
                    sc.wait("g", "xbf", sc.cnt.get("xbf", 0))
                sc.wait("y", "V", rope_free)
                sc.dma("y", f"tab{tag}", cA[:], cA_in[:, qt * 1024:(qt + 1) * 1024])
                tabw = sc.dma("y", f"tab{tag}", cB[:],
                              cB_in[:, qt * 1024:(qt + 1) * 1024])
                xw = 0
                for f in range(16):
                    xw = sc.dma("g", f"xq{tag}{xbuf}", xq[xbuf][:, f, :],
                                xT_in[f * 128:(f + 1) * 128,
                                      qt * 1024:(qt + 1) * 1024])
                if tag == 'k':
                    sc.wait("y", f"xq{tag}{xbuf}", xw)
                    for f in range(16):
                        sc.dma("y", "xbf", x_bf[f * 128:(f + 1) * 128,
                                               qt * 1024:(qt + 1) * 1024],
                               xq[xbuf][:, f, :])
                def stage_P(hh):
                    pb = hh % 2
                    sc.wait("t", "V", pk_free[pb])
                    if hh == 0:
                        sc.wait("t", f"xq{tag}{xbuf}", xw)
                        if qt == 0:
                            sc.wait("t", f"wt{tag}", wtw)
                    mmref = 0
                    for f in range(16):
                        for cpos in (0, 512):
                            last = (f == 15 and cpos == 512)
                            fn = lambda E, pb=pb, f=f, c=cpos, xb2=xbuf, hh=hh: \
                                E.matmul(pk[pb][:, c:c + 512],
                                         wT[:, f, hh * 128:(hh + 1) * 128],
                                         xq[xb2][:, f, c:c + 512],
                                         start=(f == 0), stop=(f == 15))
                            if last:
                                mmref = sc.op("t", "T", fn)
                            else:
                                sc.opn("t", fn)
                    if hh == 15:
                        xq_free[xbuf] = mmref
                    return mmref

                def stage_R(hh, mmref):
                    nonlocal krref, rope_free
                    pb = hh % 2
                    sc.wait("v", "T", mmref)
                    if hh == 0:
                        sc.wait("v", f"tab{tag}", tabw)
                    sc.wait("v", "T", krref)
                    sc.opn("v", lambda E, pb=pb, hh=hh:
                           E.scalar_tensor_tensor(p1st[:], pk[pb][:],
                                                  b_sb[:, hh:hh + 1], cA[:],
                                                  ALU.add, ALU.mult))
                    pk_free[pb] = sc.op(
                        "v", "V", lambda E, pb=pb, hh=hh:
                        E.scalar_tensor_tensor(p2st[:], pk[pb][:],
                                               b_sb[:, hh:hh + 1], cB[:],
                                               ALU.add, ALU.mult))
                    rope_free = pk_free[pb]
                    kb = hh % 2
                    sc.wait("t", "V", rope_free)
                    sc.wait("t", "S", kr_free[kb])
                    for cpos in (0, 512):
                        sc.opn("t", lambda E, c=cpos, kb=kb:
                               E.matmul(krp[kb][0:64, c:c + 512], m1[:],
                                        p1st[:, c:c + 512], start=True, stop=True))
                    sc.opn("t", lambda E, kb=kb:
                           E.matmul(krp[kb][64:128, 0:512], m2[:],
                                    p2st[:, 0:512], start=True, stop=True))
                    krref = sc.op("t", "T", lambda E, kb=kb:
                                  E.matmul(krp[kb][64:128, 512:1024], m2[:],
                                           p2st[:, 512:1024], start=True, stop=True))
                    sc.wait("s", "T", krref)
                    sc.wait("s", f"kst{tag}{kb}", kst_free[kb])
                    kr_free[kb] = sc.op("s", "S", lambda E, kb=kb:
                                        E.activation(kst[kb][:], krp[kb][:],
                                                     AF.Identity))
                    sref = sc.cnt["S"]
                    sc.wait("y", "S", sref)
                    kst_free[kb] = sc.dma(
                        "y", f"kst{tag}{kb}",
                        dst_sp[hh, :, qt * 1024:(qt + 1) * 1024], kst[kb][:])

                pend = None
                for hh in range(H):
                    mmref = stage_P(hh)
                    if pend is not None:
                        stage_R(*pend)
                    pend = (hh, mmref)
                stage_R(*pend)
            sc.barrier()
            sc.flush()

    if stop >= 3:
        phase_proj_rope(wkT, bkr_sb, ckA_in, ckB_in, xbT, T, k_sp, 'k')
    if stop >= 4:
        phase_proj_rope(wqT, bqr_sb, cqA_in, cqB_in, xownT, NTOK, q_sp, 'q')

    def phase_vproj():
        es = ExitStack()
        with es:
            wvT = es.enter_context(nc.sbuf_tensor("wvT", [128, 16, C], BF16))
            bv_bc = es.enter_context(nc.sbuf_tensor("bv_bc_sb", [128, C], F32))
            bvw = sc.dma("y", "bv", bv_bc[:], bv_bc_in)
            for f in range(16):
                wtw = sc.dma("g", "wtv", wvT[:, f, :],
                             wvT_in[f * 128:(f + 1) * 128, :])
            xst = [es.enter_context(nc.sbuf_tensor(f"xst{i}", [128, 16, 512], BF16))
                   for i in range(2)]
            vst = [es.enter_context(nc.sbuf_tensor(f"vst{i}", [128, 1024], BF16))
                   for i in range(2)]
            pv = [es.enter_context(nc.psum_tensor(f"pv{i}", [128, 1024], F32))
                  for i in range(2)]
            pv_free = [0, 0]
            vst_free = [0, 0]
            xst_free = [0, 0]
            xst_w = [0, 0]

            def xload(g):
                xb = g % 2
                sc.wait("y", "T", xst_free[xb])
                for f in range(16):
                    xst_w[xb] = sc.dma("y", f"xv{xb}", xst[xb][:, f, :],
                                       x_bf[f * 128:(f + 1) * 128,
                                            g * 512:(g + 1) * 512])

            xload(0)
            for grp in range(8):
                xbuf = grp % 2
                if grp + 1 < 8:
                    xload(grp + 1)
                xw = xst_w[xbuf]
                lastmm = 0
                for tb in range(4):
                    for dh in range(2):
                        it = tb * 2 + dh
                        pb = it % 2
                        sc.wait("t", "V", pv_free[pb])
                        if tb == 0 and dh == 0:
                            sc.wait("t", f"xv{xbuf}", xw)
                            if grp == 0:
                                sc.wait("t", "wtv", wtw)
                        for f in range(16):
                            for cpos in (0, 512):
                                last = (f == 15 and cpos == 512)
                                fn = lambda E, pb=pb, f=f, c=cpos, xb2=xbuf, \
                                    tb=tb, dh=dh: \
                                    E.matmul(pv[pb][:, c:c + 512],
                                             xst[xb2][:, f, tb * 128:(tb + 1) * 128],
                                             wvT[:, f, dh * 1024 + c:dh * 1024 + c + 512],
                                             start=(f == 0), stop=(f == 15))
                                if last:
                                    mmref = sc.op("t", "T", fn)
                                else:
                                    sc.opn("t", fn)
                        lastmm = mmref
                        vb = it % 2
                        sc.wait("v", "T", mmref)
                        if grp == 0 and it == 0:
                            sc.wait("v", "bv", bvw)
                        sc.wait("v", f"vsp{vb}", vst_free[vb])
                        pv_free[pb] = sc.op(
                            "v", "V", lambda E, pb=pb, vb=vb, dh=dh:
                            E.tensor_add(vst[vb][:], pv[pb][:],
                                         bv_bc[:, dh * 1024:(dh + 1) * 1024]))
                        vref = sc.cnt["V"]
                        sc.wait("y", "V", vref)
                        row0 = (grp * 4 + tb) * 128
                        vst_free[vb] = sc.dma(
                            "y", f"vsp{vb}",
                            v_sp[row0:row0 + 128, dh * 1024:(dh + 1) * 1024],
                            vst[vb][:])
                xst_free[xbuf] = lastmm
            sc.barrier()
            sc.flush()

    if stop >= 5:
        phase_vproj()

    def phase_attn():
        KH = int(os.environ.get("KH", "16"))
        es = ExitStack()
        with es:
            Kh = [es.enter_context(nc.sbuf_tensor(f"Kh{i}", [128, T], BF16))
                  for i in range(2)]
            Qh = [es.enter_context(nc.sbuf_tensor(f"Qh{i}", [128, NTOK], BF16))
                  for i in range(2)]
            v2 = [es.enter_context(nc.sbuf_tensor(f"v2_{i}", [128, 32, 256], BF16))
                  for i in range(2)]
            masksT = es.enter_context(nc.sbuf_tensor("masksT", [128, NSLOT * 256], BF16))
            mkw = sc.dma("g", "mks", masksT[:], masksT_in)
            ptq = [es.enter_context(nc.sbuf_tensor(f"ptq{i}", [128, 1024], BF16))
                   for i in range(3)]
            sums_sb = [es.enter_context(nc.sbuf_tensor(f"sums_sb{i}", [128, 1024], F32))
                       for i in range(2)]
            lnt = es.enter_context(nc.sbuf_tensor("lnt", [128, 1024], F32))
            avout = [es.enter_context(nc.sbuf_tensor(f"avout{i}", [128, 1024], BF16))
                     for i in range(2)]
            sps = [es.enter_context(nc.psum_tensor(f"sps{i}", [128, 1024], F32))
                   for i in range(2)]
            avp = es.enter_context(nc.psum_tensor("avp", [128, 1024], F32))
            smtp = es.enter_context(nc.psum_tensor("smtp", [128, 1024], F32))
            kq_free = [0, 0]
            v2_free = [0, 0]
            spsB_free = [0, 0]
            ptq_free = [0, 0, 0]
            avp_free = 0
            smtp_evac = 0
            lnt_free = 0
            sums_free = [0, 0]
            avout_free = [0, 0]
            spsrot = [0]
            vw_cur = [0]

            def stage_STAV(h, band, hb, kqw):
                nonlocal avp_free, smtp_evac, lnt_free
                bpar = band % 2
                tmax = 16 * band + 16
                hoff = (h % 2) * 128
                vb2 = (h // 2) % 2
                qbase = band * 1024
                erefs = {}

                def st_tile(t):
                    vbase = max(0, t // 2 - band * 8)
                    pb2 = spsrot[0] % 2
                    spsrot[0] += 1
                    sc.wait("t", "S", spsB_free[pb2])
                    c0 = vbase * 128
                    segs = [(c0, 512), (512, 1024)] if c0 < 512 else [(c0, 1024)]
                    jd = t // 2
                    jjd = jd - band * 8 if band * 8 <= jd < band * 8 + 8 else None
                    ref = None
                    for idx, (ca, cb) in enumerate(segs):
                        mask_here = jjd is not None and ca <= jjd * 128 < cb
                        last = (idx == len(segs) - 1)
                        smm = lambda E, pb2=pb2, ca=ca, cb=cb, t=t, hb=hb, \
                            stp=not mask_here: \
                            E.matmul(sps[pb2][:, ca:cb],
                                     Kh[hb][:, t * 128:(t + 1) * 128],
                                     Qh[hb][:, qbase + ca:qbase + cb],
                                     start=True, stop=stp)
                        if last and not mask_here:
                            ref = sc.op("t", "T", smm)
                        else:
                            sc.opn("t", smm)
                        if mask_here:
                            u = t % 2
                            fn = lambda E, pb2=pb2, jjd=jjd, jd=jd, u=u: \
                                E.matmul(sps[pb2][:, jjd * 128:(jjd + 1) * 128],
                                         ident[:],
                                         masksT[:, jd * 256 + u * 128:
                                                jd * 256 + (u + 1) * 128],
                                         start=False, stop=True)
                            if last:
                                ref = sc.op("t", "T", fn)
                            else:
                                sc.opn("t", fn)
                    return ref, pb2, vbase, c0

                def consume(t):
                    eref, rb, vbase, c0 = erefs[t]
                    sc.wait("t", "S", eref)
                    if t == 0:
                        sc.wait("t", "S", smtp_evac)
                    segs = [(c0, 512), (512, 1024)] if c0 < 512 else [(c0, 1024)]
                    for (ca, cb) in segs:
                        sc.opn("t", lambda E, ca=ca, cb=cb, rb=rb, t=t:
                               E.matmul(smtp[:, ca:cb], allones[:],
                                        ptq[rb][:, ca:cb],
                                        start=(t == 0), stop=False))
                    if t == 0:
                        sc.wait("t", "V", avp_free)
                    for si, (ca, cb) in enumerate(segs):
                        fn = lambda E, ca=ca, cb=cb, rb=rb, t=t, vb2=vb2, \
                            hoff=hoff: \
                            E.matmul(avp[:, ca:cb],
                                     v2[vb2][:, t, hoff:hoff + 128],
                                     ptq[rb][:, ca:cb],
                                     start=(t == 0), stop=False)
                        if si == len(segs) - 1:
                            ptq_free[rb] = sc.op("t", "T", fn)
                        else:
                            sc.opn("t", fn)

                if band == 0:
                    sc.wait("t", f"kq{hb}", kqw)
                    if h == 0:
                        sc.wait("t", "mks", mkw)
                    if h % 2 == 0:
                        sc.wait("t", f"v2_{vb2}", vw_cur[0])
                for t in range(tmax):
                    mmref, pb2, vbase, c0 = st_tile(t)
                    rb = t % 3
                    sc.wait("s", "T", mmref)
                    sc.wait("s", "T", ptq_free[rb])
                    spsB_free[pb2] = sc.op(
                        "s", "S", lambda E, pb2=pb2, rb=rb, c0=c0:
                        E.activation(ptq[rb][:, c0:1024], sps[pb2][:, c0:1024],
                                     AF.Exp, bias=0.0, scale=1.0))
                    erefs[t] = (spsB_free[pb2], rb, vbase, c0)
                    if t >= 1:
                        consume(t - 1)
                consume(tmax - 1)
                sc.opn("t", lambda E:
                       E.matmul(avp[:, 0:512], allzeros[:], ptq[0][:, 0:512],
                                start=False, stop=True))
                sc.opn("t", lambda E:
                       E.matmul(avp[:, 512:1024], allzeros[:], ptq[0][:, 512:1024],
                                start=False, stop=True))
                sc.opn("t", lambda E:
                       E.matmul(smtp[:, 0:512], allzeros[:], ptq[0][:, 0:512],
                                start=False, stop=True))
                avref = sc.op("t", "T", lambda E:
                              E.matmul(smtp[:, 512:1024], allzeros[:],
                                       ptq[0][:, 512:1024], start=False, stop=True))
                v2_free[vb2] = avref
                sc.wait("s", "T", avref)
                sc.wait("s", "S", lnt_free)
                smtp_evac = sc.op("s", "S", lambda E:
                                  E.activation(lnt[:], smtp[:], AF.Ln))
                sc.wait("s", "S", smtp_evac)
                sc.wait("s", "V", sums_free[bpar])
                rcref = sc.op("s", "S", lambda E, bpar=bpar:
                              E.activation(sums_sb[bpar][:], lnt[:], AF.Exp,
                                           scale=-1.0))
                lnt_free = rcref
                ab = bpar
                sc.wait("v", "S", rcref)
                sc.wait("v", "T", avref)
                sc.wait("v", f"avo{ab}", avout_free[ab])
                avp_free = sc.op("v", "V", lambda E, ab=ab, bpar=bpar:
                                 E.tensor_tensor(avout[ab][:], avp[:],
                                                 sums_sb[bpar][:], ALU.mult))
                sums_free[bpar] = avp_free
                vref = sc.cnt["V"]
                sc.wait("y", "V", vref)
                avout_free[ab] = sc.dma(
                    "y", f"avo{ab}", av_sp[h, :, band * 1024:(band + 1) * 1024],
                    avout[ab][:])

            def head_loads(h):
                hb2 = h % 2
                sc.wait("y", "T", kq_free[hb2])
                sc.dma("y", f"kq{hb2}", Kh[hb2][:], k_sp[h])
                kqn = sc.dma("y", f"kq{hb2}", Qh[hb2][:], q_sp[h])
                if h % 2 == 0:
                    vb2 = (h // 2) % 2
                    sc.wait("y", "T", v2_free[vb2])
                    vw_cur[0] = sc.dma("y", f"v2_{vb2}", v2[vb2][:],
                                       v_sp.rearrange("(tt p) d -> p tt d", p=128)
                                       [:, :, h * 128:(h + 2) * 128])
                return kqn

            kqw = head_loads(0)
            for h in range(KH):
                hb = h % 2
                stage_STAV(h, 0, hb, kqw)
                kqw_next = head_loads(h + 1) if h + 1 < KH else 0
                stage_STAV(h, 1, hb, kqw)
                kq_free[hb] = sc.cnt["T"]
                kqw = kqw_next
            sc.barrier()
            sc.flush()

    if stop >= 6:
        phase_attn()

    y1nfm_cm = nc.sbuf_tensor("y1nfm", [128, 16, NTOK], BF16)
    y1nfm = y1nfm_cm.__enter__()

    def phase_outproj():
        es = ExitStack()
        with es:
            woT = es.enter_context(nc.sbuf_tensor("woT", [128, 16, C], BF16))
            wtw_f = [None] * 16
            for fg4 in range(4):
                for f4 in range(4):
                    f = fg4 * 4 + f4
                    cntf = sc.dma("g", f"wto{fg4}", woT[:, f, :],
                                  woT_in[f * 128:(f + 1) * 128, :])
                for f4 in range(4):
                    wtw_f[fg4 * 4 + f4] = (f"wto{fg4}", cntf)
            bo_bc = es.enter_context(nc.sbuf_tensor("bo_bc_sb", [128, C], F32))
            bow = sc.dma("y", "bo", bo_bc[:], bo_bc_in)
            avs = [es.enter_context(nc.sbuf_tensor("avs0", [128, 16, 512], BF16))]
            xot = [es.enter_context(nc.sbuf_tensor(f"xot{i}", [128, C], F32))
                   for i in range(2)]
            y1f = [es.enter_context(nc.sbuf_tensor(f"y1f{i}", [128, C], F32))
                   for i in range(2)]
            y1nst = [es.enter_context(nc.sbuf_tensor(f"y1nst{i}", [128, C], BF16))
                     for i in range(2)]
            bn6 = es.enter_context(nc.sbuf_tensor("bn6", [128, 48], F32))
            mv2 = es.enter_context(nc.sbuf_tensor("mv2", [128, 4], F32))
            sd1 = es.enter_context(nc.sbuf_tensor("sd1", [128, 2], F32))
            rstd = es.enter_context(nc.sbuf_tensor("rstd", [128, 2], F32))
            nmr = es.enter_context(nc.sbuf_tensor("nmr", [128, 2], F32))
            py1 = [es.enter_context(nc.psum_tensor(f"py1_{i}", [128, 1024], F32))
                   for i in range(2)]
            ttp = [es.enter_context(nc.psum_tensor(f"ttp3{i}", [128, 1024], BF16))
                   for i in range(2)]
            py_free = [0, 0]
            ttp_free = [0, 0]
            y1f_free = [0, 0]
            y1n_free = [0, 0]
            xo_free = [0, 0]
            avs_free = [0, 0]
            for tb in range(16):
                g4 = tb // 4
                if tb % 4 == 0:
                    abuf = 0
                    sc.wait("y", "T", avs_free[abuf])
                    for d in range(16):
                        avw = sc.dma("y", f"avs{abuf}", avs[abuf][:, d, :],
                                     av_sp[d, :, g4 * 512:(g4 + 1) * 512])
                xbuf = tb % 2
                sc.wait("y", "V", xo_free[xbuf])
                xw = sc.dma("y", f"xo{xbuf}", xot[xbuf][:],
                            xown[tb * 128:(tb + 1) * 128, :])
                abuf = 0
                toff = (tb % 4) * 128
                lastmm = [0, 0]
                for fh in range(2):
                    pb = fh
                    sc.wait("t", "V", py_free[pb])
                    if tb % 4 == 0 and fh == 0:
                        sc.wait("t", f"avs{abuf}", avw)
                    for d in range(16):
                        if tb == 0 and d % 4 == 0:
                            sc.wait("t", wtw_f[d][0], wtw_f[d][1])
                        for cpos in (0, 512):
                            last = (d == 15 and cpos == 512)
                            fn = lambda E, pb=pb, d=d, c=cpos, fh=fh, ab=abuf, \
                                toff=toff: \
                                E.matmul(py1[pb][:, c:c + 512],
                                         avs[ab][:, d, toff:toff + 128],
                                         woT[:, d, fh * 1024 + c:fh * 1024 + c + 512],
                                         start=(d == 0), stop=(d == 15))
                            if last:
                                lastmm[fh] = sc.op("t", "T", fn)
                            else:
                                sc.opn("t", fn)
                if tb % 4 == 3:
                    avs_free[abuf] = lastmm[1]
                tp = tb % 2
                for fh in range(2):
                    pb = fh
                    sc.wait("v", "T", lastmm[fh])
                    if tb == 0 and fh == 0:
                        sc.wait("v", "bo", bow)
                    sc.wait("v", f"xo{xbuf}", xw)
                    sc.wait("v", "S", y1f_free[tp])
                    a1ref = sc.op("v", "V", lambda E, pb=pb, fh=fh, tp=tp:
                                  E.tensor_add(y1f[tp][:, fh * 1024:(fh + 1) * 1024],
                                               py1[pb][:],
                                               bo_bc[:, fh * 1024:(fh + 1) * 1024]))
                    sc.wait("v", "V", a1ref)
                    py_free[pb] = sc.op(
                        "v", "V", lambda E, fh=fh, xbuf=xbuf, tp=tp:
                        E.tensor_add(y1f[tp][:, fh * 1024:(fh + 1) * 1024],
                                     y1f[tp][:, fh * 1024:(fh + 1) * 1024],
                                     xot[xbuf][:, fh * 1024:(fh + 1) * 1024]))
                xo_free[xbuf] = py_free[1]
                sc.wait("v", "V", py_free[1])
                for kb in range(3):
                    sc.opn("v", lambda E, kb=kb, tp=tp:
                           E.bn_stats(bn6[:, tp * 24 + kb * 6:tp * 24 + (kb + 1) * 6],
                                      y1f[tp][:, kb * 512:(kb + 1) * 512]))
                bsref = sc.op("v", "V", lambda E, tp=tp:
                              E.bn_stats(bn6[:, tp * 24 + 18:tp * 24 + 24],
                                         y1f[tp][:, 1536:2048]))
                sc.wait("v", "V", bsref)
                bnref = sc.op("v", "V", lambda E, tp=tp:
                              E.bn_aggr(mv2[:, tp * 2:tp * 2 + 2],
                                        bn6[:, tp * 24:(tp + 1) * 24]))
                sc.wait("s", "V", bnref)
                sdref = sc.op("s", "S", lambda E, tp=tp:
                              E.activation(sd1[:, tp:tp + 1],
                                           mv2[:, tp * 2 + 1:tp * 2 + 2], AF.Sqrt,
                                           bias=eps_sb[:]))
                sc.wait("v", "S", sdref)
                rcref = sc.op("v", "V", lambda E, tp=tp:
                              E.reciprocal(rstd[:, tp:tp + 1], sd1[:, tp:tp + 1]))
                sc.wait("v", "V", rcref)
                vref = sc.op("v", "V", lambda E, tp=tp:
                             E.tensor_scalar(nmr[:, tp:tp + 1],
                                             mv2[:, tp * 2:tp * 2 + 1],
                                             rstd[:, tp:tp + 1], -1.0,
                                             ALU.mult, ALU.mult))
                sc.wait("s", "V", vref)
                sc.wait("s", "T", y1n_free[tp])
                sc.wait("s", "y1sp", sc.cnt.get("y1sp", 0))
                y1f_free[tp] = sc.op("s", "S", lambda E, tp=tp:
                                     E.activation(y1nst[tp][:], y1f[tp][:],
                                                  AF.Identity,
                                                  bias=nmr[:, tp:tp + 1],
                                                  scale=rstd[:, tp:tp + 1]))
                sc.wait("y", "S", y1f_free[tp])
                sc.dma("y", "y1sp", y1tm_sp[tb], y1nst[tp][:])
                sc.wait("t", "S", y1f_free[tp])
                for fg in range(2):
                    pb4 = fg
                    sc.wait("t", "S", ttp_free[pb4])
                    for f8 in range(7):
                        f = fg * 8 + f8
                        sc.opn("t", lambda E, pb4=pb4, f8=f8, f=f, tp=tp:
                               E.transpose(ttp[pb4][:, f8 * 128:(f8 + 1) * 128],
                                           y1nst[tp][:, f * 128:(f + 1) * 128],
                                           ident[:]))
                    f = fg * 8 + 7
                    tref = sc.op("t", "T", lambda E, pb4=pb4, f=f, tp=tp:
                                 E.transpose(ttp[pb4][:, 7 * 128:8 * 128],
                                             y1nst[tp][:, f * 128:(f + 1) * 128],
                                             ident[:]))
                    y1n_free[tp] = tref
                    sc.wait("s", "T", tref)
                    ttp_free[pb4] = sc.op(
                        "s", "S", lambda E, pb4=pb4, fg=fg, tb=tb:
                        E.activation(
                            y1nfm[:, fg * 8:(fg + 1) * 8, tb * 128:(tb + 1) * 128],
                            ttp[pb4][:].rearrange("p (a b) -> p a b", a=8),
                            AF.Identity))
            sc.barrier()
            sc.flush()

    if stop >= 7:
        phase_outproj()


    def phase_gateup():
        es = ExitStack()
        with es:
            wg_sb = [es.enter_context(nc.sbuf_tensor(f"wg_sb{i}", [128, 16, 512], BF16))
                     for i in range(2)]
            wu_sb = [es.enter_context(nc.sbuf_tensor(f"wu_sb{i}", [128, 16, 512], BF16))
                     for i in range(2)]
            tg = es.enter_context(nc.sbuf_tensor("tg", [128, 1024], BF16))
            tu = es.enter_context(nc.sbuf_tensor("tu", [128, 1024], BF16))
            gsl = es.enter_context(nc.sbuf_tensor("gsl", [128, 1024], BF16))
            hst = [es.enter_context(nc.sbuf_tensor(f"hst{i}", [128, 1024], BF16))
                   for i in range(2)]
            sy_bc = es.enter_context(nc.sbuf_tensor("sy_bc", [128, C], F32))
            gps = es.enter_context(nc.psum_tensor("gps", [128, 1024], F32))
            ups = es.enter_context(nc.psum_tensor("ups", [128, 1024], F32))
            syp = es.enter_context(nc.psum_tensor("syp", [128, 1024], F32))
            syref = 0
            for c4 in range(2):
                sc.wait("t", "V", syref)
                for f in range(16):
                    for cpos in (0, 512):
                        last = (f == 15 and cpos == 512)
                        fn = lambda E, f=f, c=cpos, c4=c4: \
                            E.matmul(syp[:, c:c + 512], allones[:],
                                     y1nfm[:, f, c4 * 1024 + c:c4 * 1024 + c + 512],
                                     start=(f == 0), stop=(f == 15))
                        if last:
                            smr = sc.op("t", "T", fn)
                        else:
                            sc.opn("t", fn)
                sc.wait("v", "T", smr)
                syref = sc.op("v", "V", lambda E, c4=c4:
                              E.tensor_copy(sy_bc[:, c4 * 1024:(c4 + 1) * 1024],
                                            syp[:]))
            wfree = [0, 0]
            gps_free = 0
            ups_free = 0
            tg_free = 0
            tu_free = 0
            gsl_free = 0
            hst_free = [0, 0]
            for icg in range(16):
                buf = icg % 2
                sc.wait("g", "T", wfree[buf])
                for f in range(16):
                    sc.dma("g", f"gu{buf}", wg_sb[buf][:, f, :],
                           gwT[f * 128:(f + 1) * 128, icg * 512:(icg + 1) * 512])
                    ww = sc.dma("g", f"gu{buf}", wu_sb[buf][:, f, :],
                                uwT[f * 128:(f + 1) * 128, icg * 512:(icg + 1) * 512])
                for ici in range(4):
                    ic = icg * 4 + ici
                    for th in range(2):
                        sc.wait("t", "V", gps_free)
                        if ici == 0 and th == 0:
                            sc.wait("t", f"gu{buf}", ww)
                            if icg == 0:
                                sc.wait("t", "V", syref)
                        for f in range(16):
                            for cpos in (0, 512):
                                last = (f == 15 and cpos == 512)
                                fn = lambda E, f=f, c=cpos, th=th, buf=buf, ici=ici: \
                                    E.matmul(gps[:, c:c + 512],
                                             wg_sb[buf][:, f, ici * 128:(ici + 1) * 128],
                                             y1nfm[:, f, th * 1024 + c:th * 1024 + c + 512],
                                             start=(f == 0), stop=(f == 15))
                                if last:
                                    gmref = sc.op("t", "T", fn)
                                else:
                                    sc.opn("t", fn)
                        sc.wait("v", "T", gmref)
                        sc.wait("v", "S", tg_free)
                        gps_free = sc.op(
                            "v", "V", lambda E, th=th, ic=ic:
                            E.scalar_tensor_tensor(
                                tg[:], sy_bc[:, th * 1024:(th + 1) * 1024],
                                -128.0, gps[:], ALU.mult, ALU.add))
                        sc.wait("s", "V", gps_free)
                        sc.wait("s", "V", gsl_free)
                        tg_free = sc.op("s", "S", lambda E, ic=ic:
                                        E.activation(gsl[:], tg[:], AF.Silu,
                                                     scale=gsc_sb[:, ic:ic + 1]))
                        sc.wait("t", "V", ups_free)
                        for f in range(16):
                            for cpos in (0, 512):
                                last = (f == 15 and cpos == 512)
                                fn = lambda E, f=f, c=cpos, th=th, buf=buf, ici=ici: \
                                    E.matmul(ups[:, c:c + 512],
                                             wu_sb[buf][:, f, ici * 128:(ici + 1) * 128],
                                             y1nfm[:, f, th * 1024 + c:th * 1024 + c + 512],
                                             start=(f == 0), stop=(f == 15))
                                if last:
                                    umref = sc.op("t", "T", fn)
                                else:
                                    sc.opn("t", fn)
                        if ici == 3 and th == 1:
                            wfree[buf] = umref
                        hb = th % 2
                        sc.wait("v", "T", umref)
                        sc.wait("v", "V", tu_free)
                        ups_free = sc.op(
                            "v", "V", lambda E, th=th, ic=ic:
                            E.scalar_tensor_tensor(
                                tu[:], sy_bc[:, th * 1024:(th + 1) * 1024],
                                -128.0, ups[:], ALU.mult, ALU.add))
                        sc.wait("v", "V", ups_free)
                        sc.wait("v", "S", tg_free)
                        sc.wait("v", f"hsp{hb}", hst_free[hb])
                        tu_free = sc.op(
                            "v", "V", lambda E, hb=hb, ic=ic:
                            E.scalar_tensor_tensor(hst[hb][:], tu[:],
                                                   usc_sb[:, ic:ic + 1], gsl[:],
                                                   ALU.mult, ALU.mult))
                        gsl_free = tu_free
                        vref = sc.cnt["V"]
                        sc.wait("y", "V", vref)
                        sc.dma("y", f"hsp{hb}",
                               h_sp[2 * th, :, ic * 512:(ic + 1) * 512],
                               hst[hb][:, 0:512])
                        hst_free[hb] = sc.dma(
                            "y", f"hsp{hb}",
                            h_sp[2 * th + 1, :, ic * 512:(ic + 1) * 512],
                            hst[hb][:, 512:1024])
            sc.barrier()
            sc.flush()

    if stop >= 9:
        phase_gateup()

    def phase_down():
        es = ExitStack()
        with es:
            wdm = [es.enter_context(nc.sbuf_tensor(f"wdm{i}", [128, 64, 512], BF16))
                   for i in range(2)]
            hsb = [es.enter_context(nc.sbuf_tensor(f"hsb{i}", [128, 32, 128], BF16))
                   for i in range(3)]
            y1t = [es.enter_context(nc.sbuf_tensor(f"y1t{i}", [128, 512], BF16))
                   for i in range(2)]
            dsbc = es.enter_context(nc.sbuf_tensor("dsbc_sb", [128, C], F32))
            y2row = es.enter_context(nc.sbuf_tensor("y2row", [128, C], BF16))
            yfsl = es.enter_context(nc.sbuf_tensor("yfsl", [128, 512], F32))
            yfb = [es.enter_context(nc.sbuf_tensor(f"yfb{i}", [128, 512], BF16))
                   for i in range(2)]
            shcol = es.enter_context(nc.sbuf_tensor("shcol", [128, 16], F32))
            bnacc = es.enter_context(nc.sbuf_tensor("bnacc", [128, 16 * 24], F32))
            mv2 = es.enter_context(nc.sbuf_tensor("mv2b", [128, 2], F32))
            sd1 = es.enter_context(nc.sbuf_tensor("sd1b", [128, 1], F32))
            rstd = es.enter_context(nc.sbuf_tensor("rstdb", [128, 1], F32))
            nmr = es.enter_context(nc.sbuf_tensor("nmrb", [128, 1], F32))
            outst = [es.enter_context(nc.sbuf_tensor(f"outst{i}", [128, 1024], F32))
                     for i in range(2)]
            y2p = [es.enter_context(nc.psum_tensor(f"y2p{i}", [128, 512], F32))
                   for i in range(3)]
            shp = es.enter_context(nc.psum_tensor("shp", [128, 16], F32))
            dsw = sc.dma("y", "dsb", dsbc[:], dsbc_in)

            wdm_free = [0, 0]
            wload = [0, 0]
            hsb_free = [0, 0, 0]
            hload = [0, 0, 0]
            y1t_free = [0, 0]
            y2p_freeV = [0, 0, 0]
            yfb_free = [0, 0]
            y2row_free = 0
            outst_free = [0, 0]
            yfsl_free = 0

            def h_load(ci):
                r = ci % 32
                tb_, hh = r // 2, r % 2
                g4, off = tb_ // 4, (tb_ % 4) * 128
                sl = ci % 3
                sc.wait("y", "T", hsb_free[sl])
                hload[sl] = sc.dma(
                    "y", f"hsb{sl}", hsb[sl][:],
                    h_sp[g4].rearrange("p (ic c) -> p ic c", ic=64)
                    [:, hh * 32:(hh + 1) * 32, off:off + 128])
                return hload[sl]

            wl0a = sc.dma("g", "wdl0h", wdm[0][:, 0:32, :], dwM[:, 0:32, 0:512])
            wl0b = sc.dma("g", "wdl0", wdm[0][:, 32:64, :], dwM[:, 32:64, 0:512])
            wload[0] = [("wdl0h", wl0a), ("wdl0", wl0b)]
            h_load(0)
            h_load(1)
            for cg in range(4):
                wb = cg % 2
                if cg + 1 < 4:
                    nwb = (cg + 1) % 2
                    sc.wait("g", "T", wdm_free[nwb])
                    wn = sc.dma("g", f"wdl{nwb}", wdm[nwb][:],
                                dwM[:, :, (cg + 1) * 512:(cg + 2) * 512])
                    wload[nwb] = [(f"wdl{nwb}", wn), (f"wdl{nwb}", wn)]
                for tb in range(16):
                    ci0 = cg * 32 + tb * 2
                    if ci0 + 2 < 128:
                        h_load(ci0 + 2)
                    pb = (cg * 16 + tb) % 3
                    rbt = tb % 2
                    sc.wait("y", "V", y1t_free[rbt])
                    ylw = sc.dma("y", f"y1t{rbt}", y1t[rbt][:],
                                 y1tm_sp[tb][:, cg * 512:(cg + 1) * 512])
                    sc.wait("t", "V", y2p_freeV[pb])
                    shmm = 0
                    mmref = 0
                    for hh in range(2):
                        sl = (ci0 + hh) % 3
                        sc.wait("t", f"hsb{sl}", hload[sl])
                        if tb == 0:
                            sc.wait("t", wload[wb][hh][0], wload[wb][hh][1])
                        for i2 in range(32):
                            ic = hh * 32 + i2
                            lastm = (ic == 63)
                            fn = lambda E, pb=pb, sl=sl, i2=i2, ic=ic, wb=wb: \
                                E.matmul(y2p[pb][:],
                                         hsb[sl][:, i2, :],
                                         wdm[wb][:, ic, :],
                                         start=(ic == 0), stop=(ic == 63))
                            if lastm or (cg != 0 and i2 == 31):
                                mmref = sc.op("t", "T", fn)
                            else:
                                sc.opn("t", fn)
                            if cg == 0:
                                fs = lambda E, sl=sl, i2=i2, ic=ic, tb=tb: \
                                    E.matmul(shp[:, tb:tb + 1],
                                             hsb[sl][:, i2, :],
                                             allones[:, 0:1],
                                             start=(ic == 0), stop=(ic == 63))
                                if i2 == 31:
                                    shmm = sc.op("t", "T", fs)
                                else:
                                    sc.opn("t", fs)
                        hsb_free[sl] = sc.cnt["T"]
                        if hh == 0 and ci0 + 3 < 128:
                            h_load(ci0 + 3)
                    if tb == 15:
                        wdm_free[wb] = mmref if cg != 0 else shmm
                    if cg == 0:
                        sc.wait("v", "T", shmm)
                        shv = sc.op("v", "V", lambda E, tb=tb:
                                    E.tensor_scalar_mul(shcol[:, tb:tb + 1],
                                                        shp[:, tb:tb + 1], -128.0))
                        sc.wait("v", "V", shv)
                    sc.wait("v", "T", mmref)
                    sc.wait("v", f"y1t{rbt}", ylw)
                    if cg == 0 and tb == 0:
                        sc.wait("v", "dsb", dsw)
                    sc.wait("v", "V", yfsl_free)
                    v1 = sc.op("v", "V", lambda E, pb=pb, cg=cg, tb=tb:
                               E.scalar_tensor_tensor(
                                   yfsl[:], y2p[pb][:], shcol[:, tb:tb + 1],
                                   dsbc[:, cg * 512:(cg + 1) * 512],
                                   ALU.add, ALU.mult))
                    y2p_freeV[pb] = v1
                    sc.wait("v", "V", v1)
                    sc.wait("v", f"yspl{rbt}", yfb_free[rbt])
                    v2 = sc.op("v", "V", lambda E, rbt=rbt:
                               E.tensor_add(yfb[rbt][:], yfsl[:], y1t[rbt][:]))
                    yfsl_free = v2
                    y1t_free[rbt] = v2
                    sc.wait("v", "V", v2)
                    v3 = sc.op("v", "V", lambda E, rbt=rbt, tb=tb, cg=cg:
                               E.bn_stats(bnacc[:, tb * 24 + cg * 6:
                                                 tb * 24 + cg * 6 + 6],
                                          yfb[rbt][:]))
                    sc.wait("y", "V", v3)
                    yfb_free[rbt] = sc.dma("y", f"yspl{rbt}",
                                           y2_sp[tb][:, cg * 512:(cg + 1) * 512],
                                           yfb[rbt][:])
                    if cg == 3:
                        sc.wait("v", "V", v3)
                        bnref = sc.op("v", "V", lambda E, tb=tb:
                                      E.bn_aggr(mv2[:], bnacc[:, tb * 24:
                                                               (tb + 1) * 24]))
                        sc.wait("s", "V", bnref)
                        sdref = sc.op("s", "S", lambda E:
                                      E.activation(sd1[:], mv2[:, 1:2], AF.Sqrt,
                                                   bias=eps_sb[:]))
                        sc.wait("v", "S", sdref)
                        rcref = sc.op("v", "V", lambda E:
                                      E.reciprocal(rstd[:], sd1[:]))
                        sc.wait("v", "V", rcref)
                        vref = sc.op("v", "V", lambda E:
                                     E.tensor_scalar(nmr[:], mv2[:, 0:1], rstd[:],
                                                     -1.0, ALU.mult, ALU.mult))
                        sc.wait("y", "S", y2row_free)
                        sc.wait("y", "yspl0", sc.cnt.get("yspl0", 0))
                        sc.wait("y", "yspl1", sc.cnt.get("yspl1", 0))
                        y2rw = sc.dma("y", "y2r", y2row[:], y2_sp[tb])
                        sc.wait("s", "y2r", y2rw)
                        sc.wait("s", "V", vref)
                        row0 = tb * 128
                        for half in range(2):
                            sc.wait("s", f"outd{half}", outst_free[half])
                            apref = sc.op(
                                "s", "S", lambda E, half=half:
                                E.activation(outst[half][:],
                                             y2row[:, half * 1024:
                                                   (half + 1) * 1024],
                                             AF.Identity, bias=nmr[:],
                                             scale=rstd[:]))
                            if half == 1:
                                y2row_free = apref
                            sc.wait("y", "S", apref)
                            outst_free[half] = sc.dma(
                                "y", f"outd{half}",
                                out[row0:row0 + 128,
                                    half * 1024:(half + 1) * 1024],
                                outst[half][:])
            sc.wait("y", "outd0", sc.cnt.get("outd0", 0))
            sc.wait("y", "outd1", sc.cnt.get("outd1", 0))
            sc.barrier()
            sc.flush()

    y1nfm_cm.__exit__(None, None, None)
    if stop >= 10:
        phase_down()
    eso.close()
    return nc


_CACHE = {}
_LEAK = []


def _host_tables(par):
    th = np.float32(10000.0) ** (-(np.arange(DH, dtype=np.float32) / np.float32(DH)))
    pos_all = np.arange(T, dtype=np.float32)
    ang = pos_all[:, None] * th[None, :]
    cosk, sink = np.cos(ang).astype(np.float32), np.sin(ang).astype(np.float32)
    ckA = np.concatenate([cosk.T, sink.T], axis=0)
    ckB = np.concatenate([sink.T, cosk.T], axis=0)
    own_blocks = [2 * j + par for j in range(NSLOT)]
    pos_own = np.concatenate([np.arange(g * 128, (g + 1) * 128) for g in own_blocks])
    ango = pos_own[:, None].astype(np.float32) * th[None, :]
    cosq, sinq = np.cos(ango).astype(np.float32), np.sin(ango).astype(np.float32)
    s = np.float32(1.0 / math.sqrt(D))
    cqA = np.concatenate([cosq.T, sinq.T], axis=0) * s
    cqB = np.concatenate([sinq.T, cosq.T], axis=0) * s
    M = np.zeros((NSLOT, 128, 256), dtype=np.float32)
    kk = np.arange(256)[None, :]
    qq = np.arange(128)[:, None]
    for j in range(NSLOT):
        lim = qq + (0 if par == 0 else 128)
        M[j] = np.where(kk <= lim, 0.0, NEG)
    masks = M.transpose(1, 0, 2).reshape(128, NSLOT * 256)
    MT = np.zeros((NSLOT, 2, 128, 128), dtype=np.float32)
    for j in range(NSLOT):
        for u in range(2):
            MT[j, u] = M[j][:, u * 128:(u + 1) * 128].T
    masksT = MT.transpose(2, 0, 1, 3).reshape(128, NSLOT * 256)
    return ckA, ckB, cqA, cqB, masks, masksT, own_blocks


def kernel(L0_attn_x, L0_attn_w_qkv, L0_attn_b_qkv, L0_attn_w_out, L0_attn_b_out,
           L0_ffn_gate_w, L0_ffn_gate_scale, L0_ffn_up_w, L0_ffn_up_scale,
           L0_ffn_down_w, L0_ffn_down_scale):
    x = np.asarray(L0_attn_x, dtype=np.float32)
    wqkv = np.asarray(L0_attn_w_qkv, dtype=np.float32)
    bqkv = np.asarray(L0_attn_b_qkv, dtype=np.float32)
    wout = np.ascontiguousarray(np.asarray(L0_attn_w_out, dtype=np.float32))
    bout = np.asarray(L0_attn_b_out, dtype=np.float32)
    gwn = np.ascontiguousarray(np.asarray(L0_ffn_gate_w, dtype=np.int32))
    gscale = np.asarray(L0_ffn_gate_scale, dtype=np.float32)
    uwn = np.ascontiguousarray(np.asarray(L0_ffn_up_w, dtype=np.int32))
    uscale = np.asarray(L0_ffn_up_scale, dtype=np.float32)
    dwn = np.ascontiguousarray(np.asarray(L0_ffn_down_w, dtype=np.int32))
    dscale = np.asarray(L0_ffn_down_scale, dtype=np.float32)

    pair = np.concatenate([np.arange(0, 128, 2), np.arange(1, 128, 2)])
    perm = (np.arange(H)[:, None] * 128 + pair[None, :]).reshape(-1)
    wqT_h = np.ascontiguousarray(wqkv[0:C][perm].T)
    wkT_h = np.ascontiguousarray(wqkv[C:2 * C][perm].T)
    wvT_h = np.ascontiguousarray(wqkv[2 * C:3 * C].T)
    woT_h = np.ascontiguousarray(wout.T)
    bq_r = np.ascontiguousarray(bqkv[0:C][perm].reshape(H, 128).T)
    bk_r = np.ascontiguousarray(bqkv[C:2 * C][perm].reshape(H, 128).T)
    bv_bc = np.ascontiguousarray(np.broadcast_to(bqkv[2 * C:3 * C], (128, C)))
    bo_bc = np.ascontiguousarray(np.broadcast_to(bout, (128, C)))
    gwT_h = np.ascontiguousarray(gwn.T)
    uwT_h = np.ascontiguousarray(uwn.T)
    dwM_h = np.ascontiguousarray(dwn.T.reshape(64, 128, C).transpose(1, 0, 2))
    gsc = np.ascontiguousarray(gscale.reshape(64, 128).T)
    usc = np.ascontiguousarray(uscale.reshape(64, 128).T)
    dsbc_h = np.ascontiguousarray(np.broadcast_to(dscale, (128, C)))
    eye = np.eye(64, dtype=np.float32)
    m1 = np.concatenate([eye, -eye], axis=0)
    m2 = np.concatenate([eye, eye], axis=0)
    identity = np.eye(128, dtype=np.float32)

    _stop = int(os.environ.get("KSTOP", "99"))
    if "nc" not in _CACHE:
        _CACHE["nc"] = build_program(_stop)
    nc = _CACHE["nc"]

    tabs = {par: _host_tables(par) for par in range(2)}
    in_maps = []
    own_rows = {}
    for c in range(NCORE):
        b, par = c // 2, c % 2
        ckA, ckB, cqA, cqB, masks, masksT, own_blocks = tabs[par]
        rows = np.concatenate([np.arange(g * 128, (g + 1) * 128) for g in own_blocks])
        own_rows[c] = rows
        xo = np.ascontiguousarray(x[b][rows])
        in_maps.append({
            "xbT": np.ascontiguousarray(x[b].T),
            "xown": xo,
            "xownT": np.ascontiguousarray(xo.T),
            "wqT": wqT_h, "wkT": wkT_h, "wvT_in": wvT_h, "woT_in": woT_h,
            "bq_r": bq_r, "bk_r": bk_r, "bv_bc": bv_bc, "bo_bc": bo_bc,
            "gwT": gwT_h, "uwT": uwT_h, "dwM": dwM_h,
            "gsc": gsc, "usc": usc, "dsbc": dsbc_h,
            "ckA": np.ascontiguousarray(ckA), "ckB": np.ascontiguousarray(ckB),
            "cqA": np.ascontiguousarray(cqA), "cqB": np.ascontiguousarray(cqB),
            "m1_in": m1, "m2_in": m2, "ident_in": identity,
            "masks_in": np.ascontiguousarray(masks),
            "masksT_in": np.ascontiguousarray(masksT),
        })

    res = run_bass_kernel_spmd(nc, in_maps, core_ids=list(range(NCORE)))
    outf = np.empty((B, T, C), dtype=np.float32)
    for c in range(NCORE):
        b = c // 2
        outf[b][own_rows[c]] = np.asarray(res.results[c]["out"])
    return outf


# version 15
# speedup vs baseline: 1.2609x; 1.0570x over previous
import math
import numpy as np
import ml_dtypes
from contextlib import ExitStack

import os
import concourse.bass as bass
import concourse.mybir as mybir
from concourse.bass_utils import run_bass_kernel_spmd

F32 = mybir.dt.float32
BF16 = mybir.dt.bfloat16
I32 = mybir.dt.int32
AF = mybir.ActivationFunctionType
ALU = mybir.AluOpType
AXX = mybir.AxisListType.X

B, T, C, H, D, DH = 4, 4096, 2048, 16, 128, 64
II = 8192
NCORE = 8
NTOK = 2048
NSLOT = 16
NEG = -1.0e9
EPS = 1.0e-5


class Sched:
    ENGS = ("t", "v", "s", "g", "y")

    def __init__(self, nc, es):
        self.nc = nc
        self.es = es
        self.q = {k: [] for k in self.ENGS}
        self.sem = {}
        self.cnt = {}
        self.bar_mark = {}
        for name in ("T", "V", "S"):
            self._mk(name)

    def _mk(self, name):
        self.sem[name] = self.es.enter_context(self.nc.semaphore(name))
        self.cnt[name] = 0
        self.bar_mark[name] = 0

    def op(self, eng, semname, fn, inc=1):
        self.cnt[semname] += inc
        sem = self.sem[semname]

        def play(E, fn=fn, sem=sem, inc=inc):
            fn(E).then_inc(sem, inc)

        self.q[eng].append(play)
        return self.cnt[semname]

    def opn(self, eng, fn):
        self.q[eng].append(fn)

    def wait(self, eng, semname, val):
        if val <= 0:
            return
        sem = self.sem[semname]
        self.q[eng].append(lambda E, sem=sem, val=val: E.wait_ge(sem, val))

    def dma(self, eng, slot, out, in_):
        if slot not in self.sem:
            self._mk(slot)
        return self.op(eng, slot,
                       lambda E, out=out, in_=in_: E.dma_start(out=out, in_=in_),
                       inc=16)

    def barrier(self):
        todo = [(n, v) for n, v in self.cnt.items() if v > self.bar_mark[n]]
        for e in self.ENGS:
            for sname, val in todo:
                self.wait(e, sname, val)
        for n, v in self.cnt.items():
            self.bar_mark[n] = v

    def flush(self):
        with self.nc.Block() as block:
            def mk(ops):
                def body(E):
                    for f in ops:
                        f(E)
                return body
            if self.q["t"]:
                block.tensor(mk(self.q["t"]))
            if self.q["v"]:
                block.vector(mk(self.q["v"]))
            if self.q["s"]:
                block.scalar(mk(self.q["s"]))
            if self.q["g"]:
                block.gpsimd(mk(self.q["g"]))
            if self.q["y"]:
                block.sync(mk(self.q["y"]))
        self.q = {k: [] for k in self.ENGS}


def build_program(stop=99):
    nc = bass.Bass("TRN2", target_bir_lowering=False)
    dt = nc.dram_tensor
    xbT = dt("xbT", [C, T], F32, kind="ExternalInput").ap()
    xown = dt("xown", [NTOK, C], F32, kind="ExternalInput").ap()
    xownT = dt("xownT", [C, NTOK], F32, kind="ExternalInput").ap()
    wqT = dt("wqT", [C, C], F32, kind="ExternalInput").ap()
    wkT = dt("wkT", [C, C], F32, kind="ExternalInput").ap()
    wvT_in = dt("wvT_in", [C, C], F32, kind="ExternalInput").ap()
    woT_in = dt("woT_in", [C, C], F32, kind="ExternalInput").ap()
    bq_r = dt("bq_r", [128, H], F32, kind="ExternalInput").ap()
    bk_r = dt("bk_r", [128, H], F32, kind="ExternalInput").ap()
    bv_bc_in = dt("bv_bc", [128, C], F32, kind="ExternalInput").ap()
    bo_bc_in = dt("bo_bc", [128, C], F32, kind="ExternalInput").ap()
    gwT = dt("gwT", [C, II], I32, kind="ExternalInput").ap()
    uwT = dt("uwT", [C, II], I32, kind="ExternalInput").ap()
    dwM = dt("dwM", [128, 64, C], I32, kind="ExternalInput").ap()
    gsc_in = dt("gsc", [128, 64], F32, kind="ExternalInput").ap()
    usc_in = dt("usc", [128, 64], F32, kind="ExternalInput").ap()
    dsbc_in = dt("dsbc", [128, C], F32, kind="ExternalInput").ap()
    ckA_in = dt("ckA", [128, T], F32, kind="ExternalInput").ap()
    ckB_in = dt("ckB", [128, T], F32, kind="ExternalInput").ap()
    cqA_in = dt("cqA", [128, NTOK], F32, kind="ExternalInput").ap()
    cqB_in = dt("cqB", [128, NTOK], F32, kind="ExternalInput").ap()
    m1_in = dt("m1_in", [128, 64], F32, kind="ExternalInput").ap()
    m2_in = dt("m2_in", [128, 64], F32, kind="ExternalInput").ap()
    ident_in = dt("ident_in", [128, 128], F32, kind="ExternalInput").ap()
    masks_in = dt("masks_in", [128, NSLOT * 256], F32, kind="ExternalInput").ap()
    masksT_in = dt("masksT_in", [128, NSLOT * 256], F32, kind="ExternalInput").ap()
    out = dt("out", [NTOK, C], F32, kind="ExternalOutput").ap()

    k_sp = dt("k_sp", [H, 128, T], BF16, kind="Internal").ap()
    q_sp = dt("q_sp", [H, 128, NTOK], BF16, kind="Internal").ap()
    v_sp = dt("v_sp", [T, C], BF16, kind="Internal").ap()
    av_sp = dt("av_sp", [H, 128, NTOK], BF16, kind="Internal").ap()
    h_sp = dt("h_sp", [4, 128, 64 * 512], BF16, kind="Internal").ap()
    dbg_sp = dt("dbg_sp", [128, 8192], F32, kind="Internal").ap()
    y1tm_sp = dt("y1tm_sp", [16, 128, C], BF16, kind="Internal").ap()
    x_bf = dt("x_bf", [C, T], BF16, kind="Internal").ap()
    y2_sp = dt("y2_sp", [16, 128, C], BF16, kind="Internal").ap()

    eso = ExitStack()
    EO = eso.enter_context
    sc = Sched(nc, eso)

    ident = EO(nc.sbuf_tensor("ident", [128, 128], BF16))
    identf = EO(nc.sbuf_tensor("identf", [128, 128], F32))
    m1 = EO(nc.sbuf_tensor("m1", [128, 64], BF16))
    m2 = EO(nc.sbuf_tensor("m2", [128, 64], BF16))
    allones = EO(nc.sbuf_tensor("allones", [128, 128], BF16))
    allzeros = EO(nc.sbuf_tensor("allzeros", [128, 128], BF16))
    bqr_sb = EO(nc.sbuf_tensor("bqr_sb", [128, H], F32))
    bkr_sb = EO(nc.sbuf_tensor("bkr_sb", [128, H], F32))
    gsc_sb = EO(nc.sbuf_tensor("gsc_sb", [128, 64], F32))
    usc_sb = EO(nc.sbuf_tensor("usc_sb", [128, 64], F32))
    gbias_sb = EO(nc.sbuf_tensor("gbias_sb", [128, 64], F32))
    ubias_sb = EO(nc.sbuf_tensor("ubias_sb", [128, 64], F32))
    eps_sb = EO(nc.sbuf_tensor("eps_sb", [128, 1], F32))

    for dst, src in ((ident, ident_in), (m1, m1_in), (m2, m2_in)):
        sc.dma("g", "c0g", dst[:], src)
    sc.dma("y", "c0y", identf[:], ident_in)
    for dst, src in ((bqr_sb, bq_r), (bkr_sb, bk_r), (gsc_sb, gsc_in),
                     (usc_sb, usc_in)):
        sc.dma("y", "c0y", dst[:], src)
    sc.wait("v", "c0g", sc.cnt["c0g"])
    sc.wait("v", "c0y", sc.cnt["c0y"])
    sc.opn("v", lambda E: E.memset(allones[:], 1.0))
    sc.opn("v", lambda E: E.memset(allzeros[:], 0.0))
    for dst, src in ((gbias_sb, gsc_sb), (ubias_sb, usc_sb)):
        sc.opn("v", lambda E, d=dst, s=src: E.tensor_scalar_mul(d[:], s[:], -128.0))
    sc.op("v", "V", lambda E: E.memset(eps_sb[:], EPS))
    sc.barrier()
    sc.flush()

    def phase_proj_rope(wT_in, b_sb, cA_in, cB_in, xT_in, ntok, dst_sp, tag):
        es = ExitStack()
        with es:
            wT = es.enter_context(nc.sbuf_tensor(f"wT1b{tag}", [128, 16, C], BF16))
            xq = [es.enter_context(nc.sbuf_tensor(f"xq{tag}{i}", [128, 16, 1024], BF16))
                  for i in range(2)]
            cA = es.enter_context(nc.sbuf_tensor(f"cA1b{tag}", [128, 1024], F32))
            cB = es.enter_context(nc.sbuf_tensor(f"cB1b{tag}", [128, 1024], F32))
            p1st = es.enter_context(nc.sbuf_tensor(f"p1st{tag}", [128, 1024], BF16))
            p2st = es.enter_context(nc.sbuf_tensor(f"p2st{tag}", [128, 1024], BF16))
            kst = [es.enter_context(nc.sbuf_tensor(f"kst{tag}{i}", [128, 1024], BF16))
                   for i in range(2)]
            pk = [es.enter_context(nc.psum_tensor(f"pk{tag}{i}", [128, 1024], F32))
                  for i in range(2)]
            krp = [es.enter_context(nc.psum_tensor(f"krp{tag}{i}", [128, 1024], F32))
                   for i in range(2)]
            for f in range(16):
                wtw = sc.dma("g", f"wt{tag}", wT[:, f, :],
                             wT_in[f * 128:(f + 1) * 128, :])
            nqt = ntok // 1024
            pk_free = [0, 0]
            rope_free = 0
            kr_free = [0, 0]
            krref = 0
            kst_free = [0, 0]
            xq_free = [0, 0]
            for qt in range(nqt):
                xbuf = qt % 2
                sc.wait("g", "T", xq_free[xbuf])
                if tag == 'k':
                    sc.wait("g", "xbf", sc.cnt.get("xbf", 0))
                sc.wait("y", "V", rope_free)
                sc.dma("y", f"tab{tag}", cA[:], cA_in[:, qt * 1024:(qt + 1) * 1024])
                tabw = sc.dma("y", f"tab{tag}", cB[:],
                              cB_in[:, qt * 1024:(qt + 1) * 1024])
                xw = 0
                for f in range(16):
                    xw = sc.dma("g", f"xq{tag}{xbuf}", xq[xbuf][:, f, :],
                                xT_in[f * 128:(f + 1) * 128,
                                      qt * 1024:(qt + 1) * 1024])
                if tag == 'k':
                    sc.wait("y", f"xq{tag}{xbuf}", xw)
                    for f in range(16):
                        sc.dma("y", "xbf", x_bf[f * 128:(f + 1) * 128,
                                               qt * 1024:(qt + 1) * 1024],
                               xq[xbuf][:, f, :])
                def stage_P(hh):
                    pb = hh % 2
                    sc.wait("t", "V", pk_free[pb])
                    if hh == 0:
                        sc.wait("t", f"xq{tag}{xbuf}", xw)
                        if qt == 0:
                            sc.wait("t", f"wt{tag}", wtw)
                    mmref = 0
                    for f in range(16):
                        for cpos in (0, 512):
                            last = (f == 15 and cpos == 512)
                            fn = lambda E, pb=pb, f=f, c=cpos, xb2=xbuf, hh=hh: \
                                E.matmul(pk[pb][:, c:c + 512],
                                         wT[:, f, hh * 128:(hh + 1) * 128],
                                         xq[xb2][:, f, c:c + 512],
                                         start=(f == 0), stop=(f == 15))
                            if last:
                                mmref = sc.op("t", "T", fn)
                            else:
                                sc.opn("t", fn)
                    if hh == 15:
                        xq_free[xbuf] = mmref
                    return mmref

                def stage_R(hh, mmref):
                    nonlocal krref, rope_free
                    pb = hh % 2
                    sc.wait("v", "T", mmref)
                    if hh == 0:
                        sc.wait("v", f"tab{tag}", tabw)
                    sc.wait("v", "T", krref)
                    sc.opn("v", lambda E, pb=pb, hh=hh:
                           E.scalar_tensor_tensor(p1st[:], pk[pb][:],
                                                  b_sb[:, hh:hh + 1], cA[:],
                                                  ALU.add, ALU.mult))
                    pk_free[pb] = sc.op(
                        "v", "V", lambda E, pb=pb, hh=hh:
                        E.scalar_tensor_tensor(p2st[:], pk[pb][:],
                                               b_sb[:, hh:hh + 1], cB[:],
                                               ALU.add, ALU.mult))
                    rope_free = pk_free[pb]
                    kb = hh % 2
                    sc.wait("t", "V", rope_free)
                    sc.wait("t", "S", kr_free[kb])
                    for cpos in (0, 512):
                        sc.opn("t", lambda E, c=cpos, kb=kb:
                               E.matmul(krp[kb][0:64, c:c + 512], m1[:],
                                        p1st[:, c:c + 512], start=True, stop=True))
                    sc.opn("t", lambda E, kb=kb:
                           E.matmul(krp[kb][64:128, 0:512], m2[:],
                                    p2st[:, 0:512], start=True, stop=True))
                    krref = sc.op("t", "T", lambda E, kb=kb:
                                  E.matmul(krp[kb][64:128, 512:1024], m2[:],
                                           p2st[:, 512:1024], start=True, stop=True))
                    sc.wait("s", "T", krref)
                    sc.wait("s", f"kst{tag}{kb}", kst_free[kb])
                    kr_free[kb] = sc.op("s", "S", lambda E, kb=kb:
                                        E.activation(kst[kb][:], krp[kb][:],
                                                     AF.Identity))
                    sref = sc.cnt["S"]
                    sc.wait("y", "S", sref)
                    kst_free[kb] = sc.dma(
                        "y", f"kst{tag}{kb}",
                        dst_sp[hh, :, qt * 1024:(qt + 1) * 1024], kst[kb][:])

                pend = None
                for hh in range(H):
                    mmref = stage_P(hh)
                    if pend is not None:
                        stage_R(*pend)
                    pend = (hh, mmref)
                stage_R(*pend)
            sc.barrier()
            sc.flush()

    if stop >= 3:
        phase_proj_rope(wkT, bkr_sb, ckA_in, ckB_in, xbT, T, k_sp, 'k')
    if stop >= 4:
        phase_proj_rope(wqT, bqr_sb, cqA_in, cqB_in, xownT, NTOK, q_sp, 'q')

    def phase_vproj():
        es = ExitStack()
        with es:
            wvT = es.enter_context(nc.sbuf_tensor("wvT", [128, 16, C], BF16))
            bv_bc = es.enter_context(nc.sbuf_tensor("bv_bc_sb", [128, C], F32))
            bvw = sc.dma("y", "bv", bv_bc[:], bv_bc_in)
            for f in range(16):
                wtw = sc.dma("g", "wtv", wvT[:, f, :],
                             wvT_in[f * 128:(f + 1) * 128, :])
            xst = [es.enter_context(nc.sbuf_tensor(f"xst{i}", [128, 16, 512], BF16))
                   for i in range(2)]
            vst = [es.enter_context(nc.sbuf_tensor(f"vst{i}", [128, 1024], BF16))
                   for i in range(2)]
            pv = [es.enter_context(nc.psum_tensor(f"pv{i}", [128, 1024], F32))
                  for i in range(2)]
            pv_free = [0, 0]
            vst_free = [0, 0]
            xst_free = [0, 0]
            xst_w = [0, 0]

            def xload(g):
                xb = g % 2
                sc.wait("y", "T", xst_free[xb])
                for f in range(16):
                    xst_w[xb] = sc.dma("y", f"xv{xb}", xst[xb][:, f, :],
                                       x_bf[f * 128:(f + 1) * 128,
                                            g * 512:(g + 1) * 512])

            xload(0)
            for grp in range(8):
                xbuf = grp % 2
                if grp + 1 < 8:
                    xload(grp + 1)
                xw = xst_w[xbuf]
                lastmm = 0
                for tb in range(4):
                    for dh in range(2):
                        it = tb * 2 + dh
                        pb = it % 2
                        sc.wait("t", "V", pv_free[pb])
                        if tb == 0 and dh == 0:
                            sc.wait("t", f"xv{xbuf}", xw)
                            if grp == 0:
                                sc.wait("t", "wtv", wtw)
                        for f in range(16):
                            for cpos in (0, 512):
                                last = (f == 15 and cpos == 512)
                                fn = lambda E, pb=pb, f=f, c=cpos, xb2=xbuf, \
                                    tb=tb, dh=dh: \
                                    E.matmul(pv[pb][:, c:c + 512],
                                             xst[xb2][:, f, tb * 128:(tb + 1) * 128],
                                             wvT[:, f, dh * 1024 + c:dh * 1024 + c + 512],
                                             start=(f == 0), stop=(f == 15))
                                if last:
                                    mmref = sc.op("t", "T", fn)
                                else:
                                    sc.opn("t", fn)
                        lastmm = mmref
                        vb = it % 2
                        sc.wait("v", "T", mmref)
                        if grp == 0 and it == 0:
                            sc.wait("v", "bv", bvw)
                        sc.wait("v", f"vsp{vb}", vst_free[vb])
                        pv_free[pb] = sc.op(
                            "v", "V", lambda E, pb=pb, vb=vb, dh=dh:
                            E.tensor_add(vst[vb][:], pv[pb][:],
                                         bv_bc[:, dh * 1024:(dh + 1) * 1024]))
                        vref = sc.cnt["V"]
                        sc.wait("y", "V", vref)
                        row0 = (grp * 4 + tb) * 128
                        vst_free[vb] = sc.dma(
                            "y", f"vsp{vb}",
                            v_sp[row0:row0 + 128, dh * 1024:(dh + 1) * 1024],
                            vst[vb][:])
                xst_free[xbuf] = lastmm
            sc.barrier()
            sc.flush()

    if stop >= 5:
        phase_vproj()

    def phase_attn():
        KH = int(os.environ.get("KH", "16"))
        es = ExitStack()
        with es:
            Kh = [es.enter_context(nc.sbuf_tensor(f"Kh{i}", [128, T], BF16))
                  for i in range(2)]
            Qh = [es.enter_context(nc.sbuf_tensor(f"Qh{i}", [128, NTOK], BF16))
                  for i in range(2)]
            v2 = [es.enter_context(nc.sbuf_tensor(f"v2_{i}", [128, 32, 256], BF16))
                  for i in range(2)]
            masksT = es.enter_context(nc.sbuf_tensor("masksT", [128, NSLOT * 256], BF16))
            mkw = sc.dma("g", "mks", masksT[:], masksT_in)
            ptq = [es.enter_context(nc.sbuf_tensor(f"ptq{i}", [128, 1024], BF16))
                   for i in range(3)]
            sums_sb = [es.enter_context(nc.sbuf_tensor(f"sums_sb{i}", [128, 1024], F32))
                       for i in range(2)]
            lnt = es.enter_context(nc.sbuf_tensor("lnt", [128, 1024], F32))
            avout = [es.enter_context(nc.sbuf_tensor(f"avout{i}", [128, 1024], BF16))
                     for i in range(2)]
            sps = [es.enter_context(nc.psum_tensor(f"sps{i}", [128, 1024], F32))
                   for i in range(2)]
            avp = es.enter_context(nc.psum_tensor("avp", [128, 1024], F32))
            smtp = es.enter_context(nc.psum_tensor("smtp", [128, 1024], F32))
            kq_free = [0, 0]
            v2_free = [0, 0]
            spsB_free = [0, 0]
            ptq_free = [0, 0, 0]
            avp_free = 0
            smtp_evac = 0
            lnt_free = 0
            sums_free = [0, 0]
            avout_free = [0, 0]
            spsrot = [0]
            vw_cur = [0]

            def stage_STAV(h, band, hb, kqw):
                nonlocal avp_free, smtp_evac, lnt_free
                bpar = band % 2
                tmax = 16 * band + 16
                hoff = (h % 2) * 128
                vb2 = (h // 2) % 2
                qbase = band * 1024
                erefs = {}

                def st_tile(t):
                    vbase = max(0, t // 2 - band * 8)
                    pb2 = spsrot[0] % 2
                    spsrot[0] += 1
                    sc.wait("t", "S", spsB_free[pb2])
                    c0 = vbase * 128
                    segs = [(c0, 512), (512, 1024)] if c0 < 512 else [(c0, 1024)]
                    jd = t // 2
                    jjd = jd - band * 8 if band * 8 <= jd < band * 8 + 8 else None
                    ref = None
                    for idx, (ca, cb) in enumerate(segs):
                        mask_here = jjd is not None and ca <= jjd * 128 < cb
                        last = (idx == len(segs) - 1)
                        smm = lambda E, pb2=pb2, ca=ca, cb=cb, t=t, hb=hb, \
                            stp=not mask_here: \
                            E.matmul(sps[pb2][:, ca:cb],
                                     Kh[hb][:, t * 128:(t + 1) * 128],
                                     Qh[hb][:, qbase + ca:qbase + cb],
                                     start=True, stop=stp)
                        if last and not mask_here:
                            ref = sc.op("t", "T", smm)
                        else:
                            sc.opn("t", smm)
                        if mask_here:
                            u = t % 2
                            fn = lambda E, pb2=pb2, jjd=jjd, jd=jd, u=u: \
                                E.matmul(sps[pb2][:, jjd * 128:(jjd + 1) * 128],
                                         ident[:],
                                         masksT[:, jd * 256 + u * 128:
                                                jd * 256 + (u + 1) * 128],
                                         start=False, stop=True)
                            if last:
                                ref = sc.op("t", "T", fn)
                            else:
                                sc.opn("t", fn)
                    return ref, pb2, vbase, c0

                def consume(t):
                    eref, rb, vbase, c0 = erefs[t]
                    sc.wait("t", "S", eref)
                    if t == 0:
                        sc.wait("t", "S", smtp_evac)
                    segs = [(c0, 512), (512, 1024)] if c0 < 512 else [(c0, 1024)]
                    for (ca, cb) in segs:
                        sc.opn("t", lambda E, ca=ca, cb=cb, rb=rb, t=t:
                               E.matmul(smtp[:, ca:cb], allones[:],
                                        ptq[rb][:, ca:cb],
                                        start=(t == 0), stop=False))
                    if t == 0:
                        sc.wait("t", "V", avp_free)
                    for si, (ca, cb) in enumerate(segs):
                        fn = lambda E, ca=ca, cb=cb, rb=rb, t=t, vb2=vb2, \
                            hoff=hoff: \
                            E.matmul(avp[:, ca:cb],
                                     v2[vb2][:, t, hoff:hoff + 128],
                                     ptq[rb][:, ca:cb],
                                     start=(t == 0), stop=False)
                        if si == len(segs) - 1:
                            ptq_free[rb] = sc.op("t", "T", fn)
                        else:
                            sc.opn("t", fn)

                if band == 0:
                    sc.wait("t", f"kq{hb}", kqw)
                    if h == 0:
                        sc.wait("t", "mks", mkw)
                    if h % 2 == 0:
                        sc.wait("t", f"v2_{vb2}", vw_cur[0])
                for t in range(tmax):
                    mmref, pb2, vbase, c0 = st_tile(t)
                    rb = t % 3
                    sc.wait("s", "T", mmref)
                    sc.wait("s", "T", ptq_free[rb])
                    spsB_free[pb2] = sc.op(
                        "s", "S", lambda E, pb2=pb2, rb=rb, c0=c0:
                        E.activation(ptq[rb][:, c0:1024], sps[pb2][:, c0:1024],
                                     AF.Exp, bias=0.0, scale=1.0))
                    erefs[t] = (spsB_free[pb2], rb, vbase, c0)
                    if t >= 1:
                        consume(t - 1)
                consume(tmax - 1)
                sc.opn("t", lambda E:
                       E.matmul(avp[:, 0:512], allzeros[:], ptq[0][:, 0:512],
                                start=False, stop=True))
                sc.opn("t", lambda E:
                       E.matmul(avp[:, 512:1024], allzeros[:], ptq[0][:, 512:1024],
                                start=False, stop=True))
                sc.opn("t", lambda E:
                       E.matmul(smtp[:, 0:512], allzeros[:], ptq[0][:, 0:512],
                                start=False, stop=True))
                avref = sc.op("t", "T", lambda E:
                              E.matmul(smtp[:, 512:1024], allzeros[:],
                                       ptq[0][:, 512:1024], start=False, stop=True))
                v2_free[vb2] = avref
                sc.wait("s", "T", avref)
                sc.wait("s", "S", lnt_free)
                smtp_evac = sc.op("s", "S", lambda E:
                                  E.activation(lnt[:], smtp[:], AF.Ln))
                sc.wait("s", "S", smtp_evac)
                sc.wait("s", "V", sums_free[bpar])
                rcref = sc.op("s", "S", lambda E, bpar=bpar:
                              E.activation(sums_sb[bpar][:], lnt[:], AF.Exp,
                                           scale=-1.0))
                lnt_free = rcref
                ab = bpar
                sc.wait("v", "S", rcref)
                sc.wait("v", "T", avref)
                sc.wait("v", f"avo{ab}", avout_free[ab])
                avp_free = sc.op("v", "V", lambda E, ab=ab, bpar=bpar:
                                 E.tensor_tensor(avout[ab][:], avp[:],
                                                 sums_sb[bpar][:], ALU.mult))
                sums_free[bpar] = avp_free
                vref = sc.cnt["V"]
                sc.wait("y", "V", vref)
                avout_free[ab] = sc.dma(
                    "y", f"avo{ab}", av_sp[h, :, band * 1024:(band + 1) * 1024],
                    avout[ab][:])

            def head_loads(h):
                hb2 = h % 2
                sc.wait("y", "T", kq_free[hb2])
                sc.dma("y", f"kq{hb2}", Kh[hb2][:], k_sp[h])
                kqn = sc.dma("y", f"kq{hb2}", Qh[hb2][:], q_sp[h])
                if h % 2 == 0:
                    vb2 = (h // 2) % 2
                    sc.wait("y", "T", v2_free[vb2])
                    vw_cur[0] = sc.dma("y", f"v2_{vb2}", v2[vb2][:],
                                       v_sp.rearrange("(tt p) d -> p tt d", p=128)
                                       [:, :, h * 128:(h + 2) * 128])
                return kqn

            kqw = head_loads(0)
            for h in range(KH):
                hb = h % 2
                stage_STAV(h, 0, hb, kqw)
                kqw_next = head_loads(h + 1) if h + 1 < KH else 0
                stage_STAV(h, 1, hb, kqw)
                kq_free[hb] = sc.cnt["T"]
                kqw = kqw_next
            sc.barrier()
            sc.flush()

    if stop >= 6:
        phase_attn()

    y1nfm_cm = nc.sbuf_tensor("y1nfm", [128, 16, NTOK], BF16)
    y1nfm = y1nfm_cm.__enter__()

    def phase_outproj():
        es = ExitStack()
        with es:
            woT = es.enter_context(nc.sbuf_tensor("woT", [128, 16, C], BF16))
            wtw_f = [None] * 16
            for fg4 in range(4):
                for f4 in range(4):
                    f = fg4 * 4 + f4
                    cntf = sc.dma("g", f"wto{fg4}", woT[:, f, :],
                                  woT_in[f * 128:(f + 1) * 128, :])
                for f4 in range(4):
                    wtw_f[fg4 * 4 + f4] = (f"wto{fg4}", cntf)
            bo_bc = es.enter_context(nc.sbuf_tensor("bo_bc_sb", [128, C], F32))
            bow = sc.dma("y", "bo", bo_bc[:], bo_bc_in)
            avs = [es.enter_context(nc.sbuf_tensor("avs0", [128, 16, 512], BF16))]
            xot = [es.enter_context(nc.sbuf_tensor(f"xot{i}", [128, C], F32))
                   for i in range(2)]
            y1f = [es.enter_context(nc.sbuf_tensor(f"y1f{i}", [128, C], F32))
                   for i in range(2)]
            y1nst = [es.enter_context(nc.sbuf_tensor(f"y1nst{i}", [128, C], BF16))
                     for i in range(2)]
            bn6 = es.enter_context(nc.sbuf_tensor("bn6", [128, 48], F32))
            mv2 = es.enter_context(nc.sbuf_tensor("mv2", [128, 4], F32))
            sd1 = es.enter_context(nc.sbuf_tensor("sd1", [128, 2], F32))
            rstd = es.enter_context(nc.sbuf_tensor("rstd", [128, 2], F32))
            nmr = es.enter_context(nc.sbuf_tensor("nmr", [128, 2], F32))
            py1 = [es.enter_context(nc.psum_tensor(f"py1_{i}", [128, 1024], F32))
                   for i in range(2)]
            ttp = [es.enter_context(nc.psum_tensor(f"ttp3{i}", [128, 1024], BF16))
                   for i in range(2)]
            py_free = [0, 0]
            ttp_free = [0, 0]
            y1f_free = [0, 0]
            y1n_free = [0, 0]
            xo_free = [0, 0]
            avs_free = [0, 0]
            pend_tr = []
            for tb in range(16):
                g4 = tb // 4
                if tb % 4 == 0:
                    abuf = 0
                    sc.wait("y", "T", avs_free[abuf])
                    for d in range(16):
                        avw = sc.dma("y", f"avs{abuf}", avs[abuf][:, d, :],
                                     av_sp[d, :, g4 * 512:(g4 + 1) * 512])
                xbuf = tb % 2
                sc.wait("y", "V", xo_free[xbuf])
                xw = sc.dma("y", f"xo{xbuf}", xot[xbuf][:],
                            xown[tb * 128:(tb + 1) * 128, :])
                abuf = 0
                toff = (tb % 4) * 128
                lastmm = [0, 0]
                for fh in range(2):
                    pb = fh
                    sc.wait("t", "V", py_free[pb])
                    if tb % 4 == 0 and fh == 0:
                        sc.wait("t", f"avs{abuf}", avw)
                    for d in range(16):
                        if tb == 0 and d % 4 == 0:
                            sc.wait("t", wtw_f[d][0], wtw_f[d][1])
                        for cpos in (0, 512):
                            last = (d == 15 and cpos == 512)
                            fn = lambda E, pb=pb, d=d, c=cpos, fh=fh, ab=abuf, \
                                toff=toff: \
                                E.matmul(py1[pb][:, c:c + 512],
                                         avs[ab][:, d, toff:toff + 128],
                                         woT[:, d, fh * 1024 + c:fh * 1024 + c + 512],
                                         start=(d == 0), stop=(d == 15))
                            if last:
                                lastmm[fh] = sc.op("t", "T", fn)
                            else:
                                sc.opn("t", fn)
                if tb % 4 == 3:
                    avs_free[abuf] = lastmm[1]
                if pend_tr:
                    stage_TR(*pend_tr.pop(0))
                tp = tb % 2
                for fh in range(2):
                    pb = fh
                    sc.wait("v", "T", lastmm[fh])
                    if tb == 0 and fh == 0:
                        sc.wait("v", "bo", bow)
                    sc.wait("v", f"xo{xbuf}", xw)
                    sc.wait("v", "S", y1f_free[tp])
                    a1ref = sc.op("v", "V", lambda E, pb=pb, fh=fh, tp=tp:
                                  E.tensor_add(y1f[tp][:, fh * 1024:(fh + 1) * 1024],
                                               py1[pb][:],
                                               bo_bc[:, fh * 1024:(fh + 1) * 1024]))
                    sc.wait("v", "V", a1ref)
                    py_free[pb] = sc.op(
                        "v", "V", lambda E, fh=fh, xbuf=xbuf, tp=tp:
                        E.tensor_add(y1f[tp][:, fh * 1024:(fh + 1) * 1024],
                                     y1f[tp][:, fh * 1024:(fh + 1) * 1024],
                                     xot[xbuf][:, fh * 1024:(fh + 1) * 1024]))
                xo_free[xbuf] = py_free[1]
                sc.wait("v", "V", py_free[1])
                for kb in range(3):
                    sc.opn("v", lambda E, kb=kb, tp=tp:
                           E.bn_stats(bn6[:, tp * 24 + kb * 6:tp * 24 + (kb + 1) * 6],
                                      y1f[tp][:, kb * 512:(kb + 1) * 512]))
                bsref = sc.op("v", "V", lambda E, tp=tp:
                              E.bn_stats(bn6[:, tp * 24 + 18:tp * 24 + 24],
                                         y1f[tp][:, 1536:2048]))
                sc.wait("v", "V", bsref)
                bnref = sc.op("v", "V", lambda E, tp=tp:
                              E.bn_aggr(mv2[:, tp * 2:tp * 2 + 2],
                                        bn6[:, tp * 24:(tp + 1) * 24]))
                sc.wait("s", "V", bnref)
                sdref = sc.op("s", "S", lambda E, tp=tp:
                              E.activation(sd1[:, tp:tp + 1],
                                           mv2[:, tp * 2 + 1:tp * 2 + 2], AF.Sqrt,
                                           bias=eps_sb[:]))
                sc.wait("v", "S", sdref)
                rcref = sc.op("v", "V", lambda E, tp=tp:
                              E.reciprocal(rstd[:, tp:tp + 1], sd1[:, tp:tp + 1]))
                sc.wait("v", "V", rcref)
                vref = sc.op("v", "V", lambda E, tp=tp:
                             E.tensor_scalar(nmr[:, tp:tp + 1],
                                             mv2[:, tp * 2:tp * 2 + 1],
                                             rstd[:, tp:tp + 1], -1.0,
                                             ALU.mult, ALU.mult))
                sc.wait("s", "V", vref)
                sc.wait("s", "T", y1n_free[tp])
                sc.wait("s", "y1sp", sc.cnt.get("y1sp", 0))
                y1f_free[tp] = sc.op("s", "S", lambda E, tp=tp:
                                     E.activation(y1nst[tp][:], y1f[tp][:],
                                                  AF.Identity,
                                                  bias=nmr[:, tp:tp + 1],
                                                  scale=rstd[:, tp:tp + 1]))
                sc.wait("y", "S", y1f_free[tp])
                sc.dma("y", "y1sp", y1tm_sp[tb], y1nst[tp][:])

                def stage_TR(tb, tp):
                    sc.wait("t", "S", y1f_free[tp])
                    for fg in range(2):
                        pb4 = fg
                        sc.wait("t", "S", ttp_free[pb4])
                        for f8 in range(7):
                            f = fg * 8 + f8
                            sc.opn("t", lambda E, pb4=pb4, f8=f8, f=f, tp=tp:
                                   E.transpose(ttp[pb4][:, f8 * 128:(f8 + 1) * 128],
                                               y1nst[tp][:, f * 128:(f + 1) * 128],
                                               ident[:]))
                        f = fg * 8 + 7
                        tref = sc.op("t", "T", lambda E, pb4=pb4, f=f, tp=tp:
                                     E.transpose(ttp[pb4][:, 7 * 128:8 * 128],
                                                 y1nst[tp][:, f * 128:(f + 1) * 128],
                                                 ident[:]))
                        y1n_free[tp] = tref
                        sc.wait("s", "T", tref)
                        ttp_free[pb4] = sc.op(
                            "s", "S", lambda E, pb4=pb4, fg=fg, tb=tb:
                            E.activation(
                                y1nfm[:, fg * 8:(fg + 1) * 8,
                                      tb * 128:(tb + 1) * 128],
                                ttp[pb4][:].rearrange("p (a b) -> p a b", a=8),
                                AF.Identity))
                pend_tr.append((tb, tp))
            while pend_tr:
                stage_TR(*pend_tr.pop(0))
            sc.barrier()
            sc.flush()

    if stop >= 7:
        phase_outproj()


    def phase_gateup():
        es = ExitStack()
        with es:
            wg_sb = [es.enter_context(nc.sbuf_tensor(f"wg_sb{i}", [128, 16, 512], BF16))
                     for i in range(2)]
            wu_sb = [es.enter_context(nc.sbuf_tensor(f"wu_sb{i}", [128, 16, 512], BF16))
                     for i in range(2)]
            tg = es.enter_context(nc.sbuf_tensor("tg", [128, 1024], BF16))
            tu = es.enter_context(nc.sbuf_tensor("tu", [128, 1024], BF16))
            gsl = es.enter_context(nc.sbuf_tensor("gsl", [128, 1024], BF16))
            hst = [es.enter_context(nc.sbuf_tensor(f"hst{i}", [128, 1024], BF16))
                   for i in range(2)]
            sy_bc = es.enter_context(nc.sbuf_tensor("sy_bc", [128, C], F32))
            gps = es.enter_context(nc.psum_tensor("gps", [128, 1024], F32))
            ups = es.enter_context(nc.psum_tensor("ups", [128, 1024], F32))
            syp = es.enter_context(nc.psum_tensor("syp", [128, 1024], F32))
            syref = 0
            for c4 in range(2):
                sc.wait("t", "V", syref)
                for f in range(16):
                    for cpos in (0, 512):
                        last = (f == 15 and cpos == 512)
                        fn = lambda E, f=f, c=cpos, c4=c4: \
                            E.matmul(syp[:, c:c + 512], allones[:],
                                     y1nfm[:, f, c4 * 1024 + c:c4 * 1024 + c + 512],
                                     start=(f == 0), stop=(f == 15))
                        if last:
                            smr = sc.op("t", "T", fn)
                        else:
                            sc.opn("t", fn)
                sc.wait("v", "T", smr)
                syref = sc.op("v", "V", lambda E, c4=c4:
                              E.tensor_copy(sy_bc[:, c4 * 1024:(c4 + 1) * 1024],
                                            syp[:]))
            wfree = [0, 0]
            gps_free = 0
            ups_free = 0
            tg_free = 0
            tu_free = 0
            gsl_free = 0
            hst_free = [0, 0]
            for icg in range(16):
                buf = icg % 2
                sc.wait("g", "T", wfree[buf])
                for f in range(16):
                    sc.dma("g", f"gu{buf}", wg_sb[buf][:, f, :],
                           gwT[f * 128:(f + 1) * 128, icg * 512:(icg + 1) * 512])
                    ww = sc.dma("g", f"gu{buf}", wu_sb[buf][:, f, :],
                                uwT[f * 128:(f + 1) * 128, icg * 512:(icg + 1) * 512])
                for ici in range(4):
                    ic = icg * 4 + ici
                    for th in range(2):
                        sc.wait("t", "V", gps_free)
                        if ici == 0 and th == 0:
                            sc.wait("t", f"gu{buf}", ww)
                            if icg == 0:
                                sc.wait("t", "V", syref)
                        for f in range(16):
                            for cpos in (0, 512):
                                last = (f == 15 and cpos == 512)
                                fn = lambda E, f=f, c=cpos, th=th, buf=buf, ici=ici: \
                                    E.matmul(gps[:, c:c + 512],
                                             wg_sb[buf][:, f, ici * 128:(ici + 1) * 128],
                                             y1nfm[:, f, th * 1024 + c:th * 1024 + c + 512],
                                             start=(f == 0), stop=(f == 15))
                                if last:
                                    gmref = sc.op("t", "T", fn)
                                else:
                                    sc.opn("t", fn)
                        sc.wait("v", "T", gmref)
                        sc.wait("v", "S", tg_free)
                        gps_free = sc.op(
                            "v", "V", lambda E, th=th, ic=ic:
                            E.scalar_tensor_tensor(
                                tg[:], sy_bc[:, th * 1024:(th + 1) * 1024],
                                -128.0, gps[:], ALU.mult, ALU.add))
                        sc.wait("s", "V", gps_free)
                        sc.wait("s", "V", gsl_free)
                        tg_free = sc.op("s", "S", lambda E, ic=ic:
                                        E.activation(gsl[:], tg[:], AF.Silu,
                                                     scale=gsc_sb[:, ic:ic + 1]))
                        sc.wait("t", "V", ups_free)
                        for f in range(16):
                            for cpos in (0, 512):
                                last = (f == 15 and cpos == 512)
                                fn = lambda E, f=f, c=cpos, th=th, buf=buf, ici=ici: \
                                    E.matmul(ups[:, c:c + 512],
                                             wu_sb[buf][:, f, ici * 128:(ici + 1) * 128],
                                             y1nfm[:, f, th * 1024 + c:th * 1024 + c + 512],
                                             start=(f == 0), stop=(f == 15))
                                if last:
                                    umref = sc.op("t", "T", fn)
                                else:
                                    sc.opn("t", fn)
                        if ici == 3 and th == 1:
                            wfree[buf] = umref
                        hb = th % 2
                        sc.wait("v", "T", umref)
                        sc.wait("v", "V", tu_free)
                        ups_free = sc.op(
                            "v", "V", lambda E, th=th, ic=ic:
                            E.scalar_tensor_tensor(
                                tu[:], sy_bc[:, th * 1024:(th + 1) * 1024],
                                -128.0, ups[:], ALU.mult, ALU.add))
                        sc.wait("v", "V", ups_free)
                        sc.wait("v", "S", tg_free)
                        sc.wait("v", f"hsp{hb}", hst_free[hb])
                        tu_free = sc.op(
                            "v", "V", lambda E, hb=hb, ic=ic:
                            E.scalar_tensor_tensor(hst[hb][:], tu[:],
                                                   usc_sb[:, ic:ic + 1], gsl[:],
                                                   ALU.mult, ALU.mult))
                        gsl_free = tu_free
                        vref = sc.cnt["V"]
                        sc.wait("y", "V", vref)
                        sc.dma("y", f"hsp{hb}",
                               h_sp[2 * th, :, ic * 512:(ic + 1) * 512],
                               hst[hb][:, 0:512])
                        hst_free[hb] = sc.dma(
                            "y", f"hsp{hb}",
                            h_sp[2 * th + 1, :, ic * 512:(ic + 1) * 512],
                            hst[hb][:, 512:1024])
            sc.barrier()
            sc.flush()

    if stop >= 9:
        phase_gateup()

    def phase_down():
        es = ExitStack()
        with es:
            wdm = [es.enter_context(nc.sbuf_tensor(f"wdm{i}", [128, 32, 512], BF16))
                   for i in range(3)]
            hsb = [es.enter_context(nc.sbuf_tensor(f"hsb{i}", [128, 32, 512], BF16))
                   for i in range(2)]
            y1t = [es.enter_context(nc.sbuf_tensor(f"y1t{i}", [128, 512], BF16))
                   for i in range(4)]
            dsbc = es.enter_context(nc.sbuf_tensor("dsbc_sb", [128, C], F32))
            y2row = es.enter_context(nc.sbuf_tensor("y2row", [128, C], BF16))
            yfsl = es.enter_context(nc.sbuf_tensor("yfsl", [128, 512], F32))
            yfb = [es.enter_context(nc.sbuf_tensor(f"yfb{i}", [128, 512], BF16))
                   for i in range(2)]
            shcol = es.enter_context(nc.sbuf_tensor("shcol", [128, 16], F32))
            bnacc = es.enter_context(nc.sbuf_tensor("bnacc", [128, 16 * 24], F32))
            mv2 = es.enter_context(nc.sbuf_tensor("mv2b", [128, 2], F32))
            sd1 = es.enter_context(nc.sbuf_tensor("sd1b", [128, 1], F32))
            rstd = es.enter_context(nc.sbuf_tensor("rstdb", [128, 1], F32))
            nmr = es.enter_context(nc.sbuf_tensor("nmrb", [128, 1], F32))
            outst = [es.enter_context(nc.sbuf_tensor(f"outst{i}", [128, 1024], F32))
                     for i in range(2)]
            y2p = [es.enter_context(nc.psum_tensor(f"y2p{i}", [128, 512], F32))
                   for i in range(4)]
            shp = [es.enter_context(nc.psum_tensor(f"shp{i}", [128, 1], F32))
                   for i in range(4)]
            dsw = sc.dma("y", "dsb", dsbc[:], dsbc_in)

            wdm_freeT = [0, 0, 0]
            wload = [0, 0, 0]
            hsb_freeT = [0, 0]
            hload = [0, 0]
            y1t_free = [0, 0, 0, 0]
            y2p_freeV = [0, 0, 0, 0]
            yfb_free = [0, 0]
            y2row_free = 0
            outst_free = [0, 0]
            yfsl_free = 0

            def wd_load(wc):
                cg_, hh_ = wc // 2, wc % 2
                sl = wc % 3
                sc.wait("g", "T", wdm_freeT[sl])
                wload[sl] = sc.dma("g", f"wdl{sl}", wdm[sl][:],
                                   dwM[:, hh_ * 32:(hh_ + 1) * 32,
                                       cg_ * 512:(cg_ + 1) * 512])

            def h_load(hc):
                r = hc % 8
                g4_, hh_ = r // 2, r % 2
                sl = hc % 2
                sc.wait("y", "T", hsb_freeT[sl])
                hload[sl] = sc.dma(
                    "y", f"hsb{sl}", hsb[sl][:],
                    h_sp[g4_].rearrange("p (ic c) -> p ic c", ic=64)
                    [:, hh_ * 32:(hh_ + 1) * 32, :])

            wd_load(0)
            wd_load(1)
            h_load(0)
            h_load(1)
            for cg in range(4):
                for g4 in range(4):
                    ylw = [0, 0, 0, 0]
                    for tb4 in range(4):
                        tb = g4 * 4 + tb4
                        sc.wait("y", "V", y1t_free[tb4])
                        ylw[tb4] = sc.dma("y", f"y1t{tb4}", y1t[tb4][:],
                                          y1tm_sp[tb][:, cg * 512:(cg + 1) * 512])
                    mmref = [0, 0, 0, 0]
                    shmm = [0, 0, 0, 0]
                    for hh in range(2):
                        wsl = (cg * 2 + hh) % 3
                        hsl = (cg * 8 + g4 * 2 + hh) % 2
                        sc.wait("t", f"wdl{wsl}", wload[wsl])
                        sc.wait("t", f"hsb{hsl}", hload[hsl])
                        for tb4 in range(4):
                            tb = g4 * 4 + tb4
                            if hh == 0:
                                sc.wait("t", "V", y2p_freeV[tb4])
                            for i2 in range(32):
                                lastm = (hh == 1 and i2 == 31)
                                fn = lambda E, tb4=tb4, hsl=hsl, wsl=wsl, \
                                    i2=i2, hh=hh: \
                                    E.matmul(y2p[tb4][:],
                                             hsb[hsl][:, i2,
                                                      tb4 * 128:(tb4 + 1) * 128],
                                             wdm[wsl][:, i2, :],
                                             start=(hh == 0 and i2 == 0),
                                             stop=(hh == 1 and i2 == 31))
                                if lastm:
                                    mmref[tb4] = sc.op("t", "T", fn)
                                elif i2 == 31:
                                    sc.op("t", "T", fn)
                                else:
                                    sc.opn("t", fn)
                                if cg == 0:
                                    fs = lambda E, tb=tb, tb4=tb4, hsl=hsl, \
                                        i2=i2, hh=hh: \
                                        E.matmul(shp[tb4][:],
                                                 hsb[hsl][:, i2,
                                                          tb4 * 128:(tb4 + 1) * 128],
                                                 allones[:, 0:1],
                                                 start=(hh == 0 and i2 == 0),
                                                 stop=(hh == 1 and i2 == 31))
                                    if hh == 1 and i2 == 31:
                                        shmm[tb4] = sc.op("t", "T", fs)
                                    elif i2 == 31:
                                        sc.op("t", "T", fs)
                                    else:
                                        sc.opn("t", fs)
                        hsb_freeT[hsl] = sc.cnt["T"]
                        hc = cg * 8 + g4 * 2 + hh
                        if hc + 2 < 32:
                            h_load(hc + 2)
                        if g4 == 3:
                            wc = cg * 2 + hh
                            wdm_freeT[wsl] = sc.cnt["T"]
                            if wc + 2 < 8:
                                wd_load(wc + 2)
                    for tb4 in range(4):
                        tb = g4 * 4 + tb4
                        rbt = tb % 2
                        if cg == 0:
                            sc.wait("v", "T", shmm[tb4])
                            shv = sc.op("v", "V", lambda E, tb=tb, tb4=tb4:
                                        E.tensor_scalar_mul(shcol[:, tb:tb + 1],
                                                            shp[tb4][:],
                                                            -128.0))
                            sc.wait("v", "V", shv)
                        sc.wait("v", "T", mmref[tb4])
                        sc.wait("v", f"y1t{tb4}", ylw[tb4])
                        if cg == 0 and tb == 0:
                            sc.wait("v", "dsb", dsw)
                        sc.wait("v", "V", yfsl_free)
                        v1 = sc.op("v", "V", lambda E, tb4=tb4, cg=cg, tb=tb:
                                   E.scalar_tensor_tensor(
                                       yfsl[:], y2p[tb4][:], shcol[:, tb:tb + 1],
                                       dsbc[:, cg * 512:(cg + 1) * 512],
                                       ALU.add, ALU.mult))
                        y2p_freeV[tb4] = v1
                        sc.wait("v", "V", v1)
                        sc.wait("v", f"yspl{rbt}", yfb_free[rbt])
                        v2 = sc.op("v", "V", lambda E, rbt=rbt, tb4=tb4:
                                   E.tensor_add(yfb[rbt][:], yfsl[:],
                                                y1t[tb4][:]))
                        yfsl_free = v2
                        y1t_free[tb4] = v2
                        sc.wait("v", "V", v2)
                        v3 = sc.op("v", "V", lambda E, rbt=rbt, tb=tb, cg=cg:
                                   E.bn_stats(bnacc[:, tb * 24 + cg * 6:
                                                     tb * 24 + cg * 6 + 6],
                                              yfb[rbt][:]))
                        sc.wait("y", "V", v3)
                        yfb_free[rbt] = sc.dma(
                            "y", f"yspl{rbt}",
                            y2_sp[tb][:, cg * 512:(cg + 1) * 512], yfb[rbt][:])
                        if cg == 3:
                            sc.wait("v", "V", v3)
                            bnref = sc.op("v", "V", lambda E, tb=tb:
                                          E.bn_aggr(mv2[:],
                                                    bnacc[:, tb * 24:
                                                          (tb + 1) * 24]))
                            sc.wait("s", "V", bnref)
                            sdref = sc.op("s", "S", lambda E:
                                          E.activation(sd1[:], mv2[:, 1:2],
                                                       AF.Sqrt, bias=eps_sb[:]))
                            sc.wait("v", "S", sdref)
                            rcref = sc.op("v", "V", lambda E:
                                          E.reciprocal(rstd[:], sd1[:]))
                            sc.wait("v", "V", rcref)
                            vref = sc.op("v", "V", lambda E:
                                         E.tensor_scalar(nmr[:], mv2[:, 0:1],
                                                         rstd[:], -1.0,
                                                         ALU.mult, ALU.mult))
                            sc.wait("y", "S", y2row_free)
                            sc.wait("y", "yspl0", sc.cnt.get("yspl0", 0))
                            sc.wait("y", "yspl1", sc.cnt.get("yspl1", 0))
                            y2rw = sc.dma("y", "y2r", y2row[:], y2_sp[tb])
                            sc.wait("s", "y2r", y2rw)
                            sc.wait("s", "V", vref)
                            row0 = tb * 128
                            for half in range(2):
                                sc.wait("s", f"outd{half}", outst_free[half])
                                apref = sc.op(
                                    "s", "S", lambda E, half=half:
                                    E.activation(outst[half][:],
                                                 y2row[:, half * 1024:
                                                       (half + 1) * 1024],
                                                 AF.Identity, bias=nmr[:],
                                                 scale=rstd[:]))
                                if half == 1:
                                    y2row_free = apref
                                sc.wait("y", "S", apref)
                                outst_free[half] = sc.dma(
                                    "y", f"outd{half}",
                                    out[row0:row0 + 128,
                                        half * 1024:(half + 1) * 1024],
                                    outst[half][:])
            sc.wait("y", "outd0", sc.cnt.get("outd0", 0))
            sc.wait("y", "outd1", sc.cnt.get("outd1", 0))
            sc.barrier()
            sc.flush()

    y1nfm_cm.__exit__(None, None, None)
    if stop >= 10:
        phase_down()
    eso.close()
    return nc


_CACHE = {}
_LEAK = []


def _host_tables(par):
    th = np.float32(10000.0) ** (-(np.arange(DH, dtype=np.float32) / np.float32(DH)))
    pos_all = np.arange(T, dtype=np.float32)
    ang = pos_all[:, None] * th[None, :]
    cosk, sink = np.cos(ang).astype(np.float32), np.sin(ang).astype(np.float32)
    ckA = np.concatenate([cosk.T, sink.T], axis=0)
    ckB = np.concatenate([sink.T, cosk.T], axis=0)
    own_blocks = [2 * j + par for j in range(NSLOT)]
    pos_own = np.concatenate([np.arange(g * 128, (g + 1) * 128) for g in own_blocks])
    ango = pos_own[:, None].astype(np.float32) * th[None, :]
    cosq, sinq = np.cos(ango).astype(np.float32), np.sin(ango).astype(np.float32)
    s = np.float32(1.0 / math.sqrt(D))
    cqA = np.concatenate([cosq.T, sinq.T], axis=0) * s
    cqB = np.concatenate([sinq.T, cosq.T], axis=0) * s
    M = np.zeros((NSLOT, 128, 256), dtype=np.float32)
    kk = np.arange(256)[None, :]
    qq = np.arange(128)[:, None]
    for j in range(NSLOT):
        lim = qq + (0 if par == 0 else 128)
        M[j] = np.where(kk <= lim, 0.0, NEG)
    masks = M.transpose(1, 0, 2).reshape(128, NSLOT * 256)
    MT = np.zeros((NSLOT, 2, 128, 128), dtype=np.float32)
    for j in range(NSLOT):
        for u in range(2):
            MT[j, u] = M[j][:, u * 128:(u + 1) * 128].T
    masksT = MT.transpose(2, 0, 1, 3).reshape(128, NSLOT * 256)
    return ckA, ckB, cqA, cqB, masks, masksT, own_blocks


def kernel(L0_attn_x, L0_attn_w_qkv, L0_attn_b_qkv, L0_attn_w_out, L0_attn_b_out,
           L0_ffn_gate_w, L0_ffn_gate_scale, L0_ffn_up_w, L0_ffn_up_scale,
           L0_ffn_down_w, L0_ffn_down_scale):
    x = np.asarray(L0_attn_x, dtype=np.float32)
    wqkv = np.asarray(L0_attn_w_qkv, dtype=np.float32)
    bqkv = np.asarray(L0_attn_b_qkv, dtype=np.float32)
    wout = np.ascontiguousarray(np.asarray(L0_attn_w_out, dtype=np.float32))
    bout = np.asarray(L0_attn_b_out, dtype=np.float32)
    gwn = np.ascontiguousarray(np.asarray(L0_ffn_gate_w, dtype=np.int32))
    gscale = np.asarray(L0_ffn_gate_scale, dtype=np.float32)
    uwn = np.ascontiguousarray(np.asarray(L0_ffn_up_w, dtype=np.int32))
    uscale = np.asarray(L0_ffn_up_scale, dtype=np.float32)
    dwn = np.ascontiguousarray(np.asarray(L0_ffn_down_w, dtype=np.int32))
    dscale = np.asarray(L0_ffn_down_scale, dtype=np.float32)

    pair = np.concatenate([np.arange(0, 128, 2), np.arange(1, 128, 2)])
    perm = (np.arange(H)[:, None] * 128 + pair[None, :]).reshape(-1)
    wqT_h = np.ascontiguousarray(wqkv[0:C][perm].T)
    wkT_h = np.ascontiguousarray(wqkv[C:2 * C][perm].T)
    wvT_h = np.ascontiguousarray(wqkv[2 * C:3 * C].T)
    woT_h = np.ascontiguousarray(wout.T)
    bq_r = np.ascontiguousarray(bqkv[0:C][perm].reshape(H, 128).T)
    bk_r = np.ascontiguousarray(bqkv[C:2 * C][perm].reshape(H, 128).T)
    bv_bc = np.ascontiguousarray(np.broadcast_to(bqkv[2 * C:3 * C], (128, C)))
    bo_bc = np.ascontiguousarray(np.broadcast_to(bout, (128, C)))
    gwT_h = np.ascontiguousarray(gwn.T)
    uwT_h = np.ascontiguousarray(uwn.T)
    dwM_h = np.ascontiguousarray(dwn.T.reshape(64, 128, C).transpose(1, 0, 2))
    gsc = np.ascontiguousarray(gscale.reshape(64, 128).T)
    usc = np.ascontiguousarray(uscale.reshape(64, 128).T)
    dsbc_h = np.ascontiguousarray(np.broadcast_to(dscale, (128, C)))
    eye = np.eye(64, dtype=np.float32)
    m1 = np.concatenate([eye, -eye], axis=0)
    m2 = np.concatenate([eye, eye], axis=0)
    identity = np.eye(128, dtype=np.float32)

    _stop = int(os.environ.get("KSTOP", "99"))
    if "nc" not in _CACHE:
        _CACHE["nc"] = build_program(_stop)
    nc = _CACHE["nc"]

    tabs = {par: _host_tables(par) for par in range(2)}
    in_maps = []
    own_rows = {}
    for c in range(NCORE):
        b, par = c // 2, c % 2
        ckA, ckB, cqA, cqB, masks, masksT, own_blocks = tabs[par]
        rows = np.concatenate([np.arange(g * 128, (g + 1) * 128) for g in own_blocks])
        own_rows[c] = rows
        xo = np.ascontiguousarray(x[b][rows])
        in_maps.append({
            "xbT": np.ascontiguousarray(x[b].T),
            "xown": xo,
            "xownT": np.ascontiguousarray(xo.T),
            "wqT": wqT_h, "wkT": wkT_h, "wvT_in": wvT_h, "woT_in": woT_h,
            "bq_r": bq_r, "bk_r": bk_r, "bv_bc": bv_bc, "bo_bc": bo_bc,
            "gwT": gwT_h, "uwT": uwT_h, "dwM": dwM_h,
            "gsc": gsc, "usc": usc, "dsbc": dsbc_h,
            "ckA": np.ascontiguousarray(ckA), "ckB": np.ascontiguousarray(ckB),
            "cqA": np.ascontiguousarray(cqA), "cqB": np.ascontiguousarray(cqB),
            "m1_in": m1, "m2_in": m2, "ident_in": identity,
            "masks_in": np.ascontiguousarray(masks),
            "masksT_in": np.ascontiguousarray(masksT),
        })

    res = run_bass_kernel_spmd(nc, in_maps, core_ids=list(range(NCORE)))
    outf = np.empty((B, T, C), dtype=np.float32)
    for c in range(NCORE):
        b = c // 2
        outf[b][own_rows[c]] = np.asarray(res.results[c]["out"])
    return outf


# version 16
# speedup vs baseline: 1.2800x; 1.0151x over previous
import math
import numpy as np
import ml_dtypes
from contextlib import ExitStack

import os
import concourse.bass as bass
import concourse.mybir as mybir
from concourse.bass_utils import run_bass_kernel_spmd

F32 = mybir.dt.float32
BF16 = mybir.dt.bfloat16
I32 = mybir.dt.int32
AF = mybir.ActivationFunctionType
ALU = mybir.AluOpType
AXX = mybir.AxisListType.X

B, T, C, H, D, DH = 4, 4096, 2048, 16, 128, 64
II = 8192
NCORE = 8
NTOK = 2048
NSLOT = 16
NEG = -1.0e9
EPS = 1.0e-5


class Sched:
    ENGS = ("t", "v", "s", "g", "y")

    def __init__(self, nc, es):
        self.nc = nc
        self.es = es
        self.q = {k: [] for k in self.ENGS}
        self.sem = {}
        self.cnt = {}
        self.bar_mark = {}
        for name in ("T", "V", "S"):
            self._mk(name)

    def _mk(self, name):
        self.sem[name] = self.es.enter_context(self.nc.semaphore(name))
        self.cnt[name] = 0
        self.bar_mark[name] = 0

    def op(self, eng, semname, fn, inc=1):
        self.cnt[semname] += inc
        sem = self.sem[semname]

        def play(E, fn=fn, sem=sem, inc=inc):
            fn(E).then_inc(sem, inc)

        self.q[eng].append(play)
        return self.cnt[semname]

    def opn(self, eng, fn):
        self.q[eng].append(fn)

    def wait(self, eng, semname, val):
        if val <= 0:
            return
        sem = self.sem[semname]
        self.q[eng].append(lambda E, sem=sem, val=val: E.wait_ge(sem, val))

    def dma(self, eng, slot, out, in_):
        if slot not in self.sem:
            self._mk(slot)
        return self.op(eng, slot,
                       lambda E, out=out, in_=in_: E.dma_start(out=out, in_=in_),
                       inc=16)

    def barrier(self):
        todo = [(n, v) for n, v in self.cnt.items() if v > self.bar_mark[n]]
        for e in self.ENGS:
            for sname, val in todo:
                self.wait(e, sname, val)
        for n, v in self.cnt.items():
            self.bar_mark[n] = v

    def flush(self):
        with self.nc.Block() as block:
            def mk(ops):
                def body(E):
                    for f in ops:
                        f(E)
                return body
            if self.q["t"]:
                block.tensor(mk(self.q["t"]))
            if self.q["v"]:
                block.vector(mk(self.q["v"]))
            if self.q["s"]:
                block.scalar(mk(self.q["s"]))
            if self.q["g"]:
                block.gpsimd(mk(self.q["g"]))
            if self.q["y"]:
                block.sync(mk(self.q["y"]))
        self.q = {k: [] for k in self.ENGS}


def build_program(stop=99):
    nc = bass.Bass("TRN2", target_bir_lowering=False)
    dt = nc.dram_tensor
    xbT = dt("xbT", [C, T], F32, kind="ExternalInput").ap()
    xown = dt("xown", [NTOK, C], F32, kind="ExternalInput").ap()
    xownT = dt("xownT", [C, NTOK], F32, kind="ExternalInput").ap()
    wqT = dt("wqT", [C, C], F32, kind="ExternalInput").ap()
    wkT = dt("wkT", [C, C], F32, kind="ExternalInput").ap()
    wvT_in = dt("wvT_in", [C, C], F32, kind="ExternalInput").ap()
    woT_in = dt("woT_in", [C, C], F32, kind="ExternalInput").ap()
    bq_r = dt("bq_r", [128, H], F32, kind="ExternalInput").ap()
    bk_r = dt("bk_r", [128, H], F32, kind="ExternalInput").ap()
    bv_bc_in = dt("bv_bc", [128, C], F32, kind="ExternalInput").ap()
    bo_bc_in = dt("bo_bc", [128, C], F32, kind="ExternalInput").ap()
    gwT = dt("gwT", [C, II], I32, kind="ExternalInput").ap()
    uwT = dt("uwT", [C, II], I32, kind="ExternalInput").ap()
    dwM = dt("dwM", [128, 64, C], I32, kind="ExternalInput").ap()
    gsc_in = dt("gsc", [128, 64], F32, kind="ExternalInput").ap()
    usc_in = dt("usc", [128, 64], F32, kind="ExternalInput").ap()
    dsbc_in = dt("dsbc", [128, C], F32, kind="ExternalInput").ap()
    ckA_in = dt("ckA", [128, T], F32, kind="ExternalInput").ap()
    ckB_in = dt("ckB", [128, T], F32, kind="ExternalInput").ap()
    cqA_in = dt("cqA", [128, NTOK], F32, kind="ExternalInput").ap()
    cqB_in = dt("cqB", [128, NTOK], F32, kind="ExternalInput").ap()
    m1_in = dt("m1_in", [128, 64], F32, kind="ExternalInput").ap()
    m2_in = dt("m2_in", [128, 64], F32, kind="ExternalInput").ap()
    ident_in = dt("ident_in", [128, 128], F32, kind="ExternalInput").ap()
    masks_in = dt("masks_in", [128, NSLOT * 256], F32, kind="ExternalInput").ap()
    masksT_in = dt("masksT_in", [128, NSLOT * 256], F32, kind="ExternalInput").ap()
    out = dt("out", [NTOK, C], F32, kind="ExternalOutput").ap()

    k_sp = dt("k_sp", [H, 128, T], BF16, kind="Internal").ap()
    q_sp = dt("q_sp", [H, 128, NTOK], BF16, kind="Internal").ap()
    v_sp = dt("v_sp", [T, C], BF16, kind="Internal").ap()
    av_sp = dt("av_sp", [H, 128, NTOK], BF16, kind="Internal").ap()
    h_sp = dt("h_sp", [4, 128, 64 * 512], BF16, kind="Internal").ap()
    dbg_sp = dt("dbg_sp", [128, 8192], F32, kind="Internal").ap()
    y1tm_sp = dt("y1tm_sp", [16, 128, C], BF16, kind="Internal").ap()
    x_bf = dt("x_bf", [C, T], BF16, kind="Internal").ap()
    y2_sp = dt("y2_sp", [16, 128, C], BF16, kind="Internal").ap()

    eso = ExitStack()
    EO = eso.enter_context
    sc = Sched(nc, eso)

    ident = EO(nc.sbuf_tensor("ident", [128, 128], BF16))
    m1 = EO(nc.sbuf_tensor("m1", [128, 64], BF16))
    m2 = EO(nc.sbuf_tensor("m2", [128, 64], BF16))
    allones = EO(nc.sbuf_tensor("allones", [128, 128], BF16))
    allzeros = EO(nc.sbuf_tensor("allzeros", [128, 128], BF16))
    bqr_sb = EO(nc.sbuf_tensor("bqr_sb", [128, H], F32))
    bkr_sb = EO(nc.sbuf_tensor("bkr_sb", [128, H], F32))
    gsc_sb = EO(nc.sbuf_tensor("gsc_sb", [128, 64], F32))
    usc_sb = EO(nc.sbuf_tensor("usc_sb", [128, 64], F32))
    eps_sb = EO(nc.sbuf_tensor("eps_sb", [128, 1], F32))

    for dst, src in ((ident, ident_in), (m1, m1_in), (m2, m2_in)):
        sc.dma("g", "c0g", dst[:], src)
    for dst, src in ((bqr_sb, bq_r), (bkr_sb, bk_r), (gsc_sb, gsc_in),
                     (usc_sb, usc_in)):
        sc.dma("y", "c0y", dst[:], src)
    sc.wait("v", "c0g", sc.cnt["c0g"])
    sc.wait("v", "c0y", sc.cnt["c0y"])
    sc.opn("v", lambda E: E.memset(allones[:], 1.0))
    sc.opn("v", lambda E: E.memset(allzeros[:], 0.0))
    sc.op("v", "V", lambda E: E.memset(eps_sb[:], EPS))
    sc.barrier()
    sc.flush()

    def phase_proj_rope(wT, wtw, b_sb, cA_in, cB_in, xT_in, ntok, dst_sp, tag,
                        pre_emit=None):
        es = ExitStack()
        with es:
            xq = [es.enter_context(nc.sbuf_tensor(f"xq{tag}{i}", [128, 16, 1024], BF16))
                  for i in range(2)]
            cA = es.enter_context(nc.sbuf_tensor(f"cA1b{tag}", [128, 1024], F32))
            cB = es.enter_context(nc.sbuf_tensor(f"cB1b{tag}", [128, 1024], F32))
            p1st = es.enter_context(nc.sbuf_tensor(f"p1st{tag}", [128, 1024], BF16))
            p2st = es.enter_context(nc.sbuf_tensor(f"p2st{tag}", [128, 1024], BF16))
            kst = [es.enter_context(nc.sbuf_tensor(f"kst{tag}0", [128, 1024], BF16))] * 2
            pk = [es.enter_context(nc.psum_tensor(f"pk{tag}{i}", [128, 1024], F32))
                  for i in range(2)]
            krp = [es.enter_context(nc.psum_tensor(f"krp{tag}{i}", [128, 1024], F32))
                   for i in range(2)]
            nqt = ntok // 1024
            pk_free = [0, 0]
            rope_free = 0
            kr_free = [0, 0]
            krref = 0
            kst_free = [0, 0]
            xq_free = [0, 0]
            for qt in range(nqt):
                xbuf = qt % 2
                sc.wait("g", "T", xq_free[xbuf])
                if tag == 'k':
                    sc.wait("g", "xbf", sc.cnt.get("xbf", 0))
                sc.wait("y", "V", rope_free)
                sc.dma("y", f"tab{tag}", cA[:], cA_in[:, qt * 1024:(qt + 1) * 1024])
                tabw = sc.dma("y", f"tab{tag}", cB[:],
                              cB_in[:, qt * 1024:(qt + 1) * 1024])
                xw = 0
                for f in range(16):
                    xw = sc.dma("g", f"xq{tag}{xbuf}", xq[xbuf][:, f, :],
                                xT_in[f * 128:(f + 1) * 128,
                                      qt * 1024:(qt + 1) * 1024])
                if tag == 'k':
                    sc.wait("y", f"xq{tag}{xbuf}", xw)
                    for f in range(16):
                        sc.dma("y", "xbf", x_bf[f * 128:(f + 1) * 128,
                                               qt * 1024:(qt + 1) * 1024],
                               xq[xbuf][:, f, :])
                if qt == 0 and pre_emit is not None:
                    pre_emit()
                def stage_P(hh):
                    pb = hh % 2
                    sc.wait("t", "V", pk_free[pb])
                    if hh == 0:
                        sc.wait("t", f"xq{tag}{xbuf}", xw)
                        if qt == 0:
                            sc.wait("t", f"wt{tag}", wtw)
                    mmref = 0
                    for f in range(16):
                        for cpos in (0, 512):
                            last = (f == 15 and cpos == 512)
                            fn = lambda E, pb=pb, f=f, c=cpos, xb2=xbuf, hh=hh: \
                                E.matmul(pk[pb][:, c:c + 512],
                                         wT[:, f, hh * 128:(hh + 1) * 128],
                                         xq[xb2][:, f, c:c + 512],
                                         start=(f == 0), stop=(f == 15))
                            if last:
                                mmref = sc.op("t", "T", fn)
                            else:
                                sc.opn("t", fn)
                    if hh == 15:
                        xq_free[xbuf] = mmref
                    return mmref

                def stage_R(hh, mmref):
                    nonlocal krref, rope_free
                    pb = hh % 2
                    sc.wait("v", "T", mmref)
                    if hh == 0:
                        sc.wait("v", f"tab{tag}", tabw)
                    sc.wait("v", "T", krref)
                    sc.opn("v", lambda E, pb=pb, hh=hh:
                           E.scalar_tensor_tensor(p1st[:], pk[pb][:],
                                                  b_sb[:, hh:hh + 1], cA[:],
                                                  ALU.add, ALU.mult))
                    pk_free[pb] = sc.op(
                        "v", "V", lambda E, pb=pb, hh=hh:
                        E.scalar_tensor_tensor(p2st[:], pk[pb][:],
                                               b_sb[:, hh:hh + 1], cB[:],
                                               ALU.add, ALU.mult))
                    rope_free = pk_free[pb]
                    kb = hh % 2
                    sc.wait("t", "V", rope_free)
                    sc.wait("t", "S", kr_free[kb])
                    for cpos in (0, 512):
                        sc.opn("t", lambda E, c=cpos, kb=kb:
                               E.matmul(krp[kb][0:64, c:c + 512], m1[:],
                                        p1st[:, c:c + 512], start=True, stop=True))
                    sc.opn("t", lambda E, kb=kb:
                           E.matmul(krp[kb][64:128, 0:512], m2[:],
                                    p2st[:, 0:512], start=True, stop=True))
                    krref = sc.op("t", "T", lambda E, kb=kb:
                                  E.matmul(krp[kb][64:128, 512:1024], m2[:],
                                           p2st[:, 512:1024], start=True, stop=True))
                    sc.wait("s", "T", krref)
                    sc.wait("s", f"kst{tag}0", sc.cnt.get(f"kst{tag}0", 0))
                    kr_free[kb] = sc.op("s", "S", lambda E, kb=kb:
                                        E.activation(kst[kb][:], krp[kb][:],
                                                     AF.Identity))
                    sref = sc.cnt["S"]
                    sc.wait("y", "S", sref)
                    kst_free[kb] = sc.dma(
                        "y", f"kst{tag}0",
                        dst_sp[hh, :, qt * 1024:(qt + 1) * 1024], kst[kb][:])

                pend = None
                for hh in range(H):
                    mmref = stage_P(hh)
                    if pend is not None:
                        stage_R(*pend)
                    pend = (hh, mmref)
                stage_R(*pend)
            sc.barrier()
            sc.flush()

    wpair_cm = [nc.sbuf_tensor(f"wpair{i}", [128, 16, C], BF16) for i in range(2)]
    wpair = [cm.__enter__() for cm in wpair_cm]

    def load_wT(dst, src_ap, tag):
        w = 0
        for f in range(16):
            w = sc.dma("g", f"wt{tag}", dst[:, f, :],
                       src_ap[f * 128:(f + 1) * 128, :])
        return w

    wload_box = {}
    wload_box['k'] = load_wT(wpair[0], wkT, 'k')
    if stop >= 3:
        phase_proj_rope(wpair[0], wload_box['k'], bkr_sb, ckA_in, ckB_in, xbT,
                        T, k_sp, 'k',
                        pre_emit=lambda: wload_box.__setitem__(
                            'q', load_wT(wpair[1], wqT, 'q')))
    if stop >= 4:
        phase_proj_rope(wpair[1], wload_box.get('q', 0), bqr_sb, cqA_in, cqB_in,
                        xownT, NTOK, q_sp, 'q',
                        pre_emit=lambda: wload_box.__setitem__(
                            'v', load_wT(wpair[0], wvT_in, 'v')))

    def phase_vproj():
        es = ExitStack()
        with es:
            wvT = wpair[0]
            wtw = wload_box.get('v', 0)
            bv_bc = es.enter_context(nc.sbuf_tensor("bv_bc_sb", [128, C], F32))
            bvw = sc.dma("y", "bv", bv_bc[:], bv_bc_in)
            xst = [es.enter_context(nc.sbuf_tensor(f"xst{i}", [128, 16, 512], BF16))
                   for i in range(2)]
            vst = [es.enter_context(nc.sbuf_tensor(f"vst{i}", [128, 1024], BF16))
                   for i in range(2)]
            pv = [es.enter_context(nc.psum_tensor(f"pv{i}", [128, 1024], F32))
                  for i in range(2)]
            pv_free = [0, 0]
            vst_free = [0, 0]
            xst_free = [0, 0]
            xst_w = [0, 0]

            def xload(g):
                xb = g % 2
                sc.wait("y", "T", xst_free[xb])
                for f in range(16):
                    xst_w[xb] = sc.dma("y", f"xv{xb}", xst[xb][:, f, :],
                                       x_bf[f * 128:(f + 1) * 128,
                                            g * 512:(g + 1) * 512])

            xload(0)
            for grp in range(8):
                xbuf = grp % 2
                if grp + 1 < 8:
                    xload(grp + 1)
                xw = xst_w[xbuf]
                lastmm = 0
                for tb in range(4):
                    for dh in range(2):
                        it = tb * 2 + dh
                        pb = it % 2
                        sc.wait("t", "V", pv_free[pb])
                        if tb == 0 and dh == 0:
                            sc.wait("t", f"xv{xbuf}", xw)
                            if grp == 0:
                                sc.wait("t", "wtv", wtw)
                        for f in range(16):
                            for cpos in (0, 512):
                                last = (f == 15 and cpos == 512)
                                fn = lambda E, pb=pb, f=f, c=cpos, xb2=xbuf, \
                                    tb=tb, dh=dh: \
                                    E.matmul(pv[pb][:, c:c + 512],
                                             xst[xb2][:, f, tb * 128:(tb + 1) * 128],
                                             wvT[:, f, dh * 1024 + c:dh * 1024 + c + 512],
                                             start=(f == 0), stop=(f == 15))
                                if last:
                                    mmref = sc.op("t", "T", fn)
                                else:
                                    sc.opn("t", fn)
                        lastmm = mmref
                        vb = it % 2
                        sc.wait("v", "T", mmref)
                        if grp == 0 and it == 0:
                            sc.wait("v", "bv", bvw)
                        sc.wait("v", f"vsp{vb}", vst_free[vb])
                        pv_free[pb] = sc.op(
                            "v", "V", lambda E, pb=pb, vb=vb, dh=dh:
                            E.tensor_add(vst[vb][:], pv[pb][:],
                                         bv_bc[:, dh * 1024:(dh + 1) * 1024]))
                        vref = sc.cnt["V"]
                        sc.wait("y", "V", vref)
                        row0 = (grp * 4 + tb) * 128
                        vst_free[vb] = sc.dma(
                            "y", f"vsp{vb}",
                            v_sp[row0:row0 + 128, dh * 1024:(dh + 1) * 1024],
                            vst[vb][:])
                xst_free[xbuf] = lastmm
            sc.barrier()
            sc.flush()

    if stop >= 5:
        phase_vproj()
    for cm in reversed(wpair_cm):
        cm.__exit__(None, None, None)

    y1nfm_cm = nc.sbuf_tensor("y1nfm", [128, 16, NTOK], BF16)
    y1nfm = y1nfm_cm.__enter__()
    woT_cm = nc.sbuf_tensor("woT", [128, 16, C], BF16)
    woT = woT_cm.__enter__()

    def phase_attn():
        KH = int(os.environ.get("KH", "16"))
        es = ExitStack()
        with es:
            Kh = [es.enter_context(nc.sbuf_tensor(f"Kh{i}", [128, T], BF16))
                  for i in range(2)]
            Qh = [es.enter_context(nc.sbuf_tensor(f"Qh{i}", [128, NTOK], BF16))
                  for i in range(2)]
            v2 = [es.enter_context(nc.sbuf_tensor(f"v2_{i}", [128, 32, 128], BF16))
                  for i in range(2)]
            masksT = es.enter_context(nc.sbuf_tensor("masksT", [128, NSLOT * 256], BF16))
            mkw = sc.dma("g", "mks", masksT[:], masksT_in)
            for f in range(16):
                sc.dma("g", "wto", woT[:, f, :], woT_in[f * 128:(f + 1) * 128, :])
            ptq = [es.enter_context(nc.sbuf_tensor(f"ptq{i}", [128, 1024], BF16))
                   for i in range(3)]
            sums_sb = [es.enter_context(nc.sbuf_tensor(f"sums_sb{i}", [128, 1024], F32))
                       for i in range(2)]
            lnt = es.enter_context(nc.sbuf_tensor("lnt", [128, 1024], F32))
            avout = [es.enter_context(nc.sbuf_tensor(f"avout{i}", [128, 1024], BF16))
                     for i in range(2)]
            sps = [es.enter_context(nc.psum_tensor(f"sps{i}", [128, 1024], F32))
                   for i in range(2)]
            avp = es.enter_context(nc.psum_tensor("avp", [128, 1024], F32))
            smtp = es.enter_context(nc.psum_tensor("smtp", [128, 1024], F32))
            kq_free = [0, 0]
            v2_free = [0, 0]
            spsB_free = [0, 0]
            ptq_free = [0, 0, 0]
            avp_free = 0
            smtp_evac = 0
            lnt_free = 0
            sums_free = [0, 0]
            avout_free = [0, 0]
            spsrot = [0]
            vw_cur = [0, 0]

            def stage_STAV(h, band, hb, kqw):
                nonlocal avp_free, smtp_evac, lnt_free
                bpar = band % 2
                tmax = 16 * band + 16
                hoff = 0
                vb2 = h % 2
                qbase = band * 1024
                erefs = {}

                def st_tile(t):
                    vbase = max(0, t // 2 - band * 8)
                    pb2 = spsrot[0] % 2
                    spsrot[0] += 1
                    sc.wait("t", "S", spsB_free[pb2])
                    c0 = vbase * 128
                    segs = [(c0, 512), (512, 1024)] if c0 < 512 else [(c0, 1024)]
                    jd = t // 2
                    jjd = jd - band * 8 if band * 8 <= jd < band * 8 + 8 else None
                    ref = None
                    for idx, (ca, cb) in enumerate(segs):
                        mask_here = jjd is not None and ca <= jjd * 128 < cb
                        last = (idx == len(segs) - 1)
                        smm = lambda E, pb2=pb2, ca=ca, cb=cb, t=t, hb=hb, \
                            stp=not mask_here: \
                            E.matmul(sps[pb2][:, ca:cb],
                                     Kh[hb][:, t * 128:(t + 1) * 128],
                                     Qh[hb][:, qbase + ca:qbase + cb],
                                     start=True, stop=stp)
                        if last and not mask_here:
                            ref = sc.op("t", "T", smm)
                        else:
                            sc.opn("t", smm)
                        if mask_here:
                            u = t % 2
                            fn = lambda E, pb2=pb2, jjd=jjd, jd=jd, u=u: \
                                E.matmul(sps[pb2][:, jjd * 128:(jjd + 1) * 128],
                                         ident[:],
                                         masksT[:, jd * 256 + u * 128:
                                                jd * 256 + (u + 1) * 128],
                                         start=False, stop=True)
                            if last:
                                ref = sc.op("t", "T", fn)
                            else:
                                sc.opn("t", fn)
                    return ref, pb2, vbase, c0

                def consume(t):
                    eref, rb, vbase, c0 = erefs[t]
                    sc.wait("t", "S", eref)
                    if t == 0:
                        sc.wait("t", "S", smtp_evac)
                    segs = [(c0, 512), (512, 1024)] if c0 < 512 else [(c0, 1024)]
                    for (ca, cb) in segs:
                        sc.opn("t", lambda E, ca=ca, cb=cb, rb=rb, t=t:
                               E.matmul(smtp[:, ca:cb], allones[:],
                                        ptq[rb][:, ca:cb],
                                        start=(t == 0), stop=False))
                    if t == 0:
                        sc.wait("t", "V", avp_free)
                    for si, (ca, cb) in enumerate(segs):
                        fn = lambda E, ca=ca, cb=cb, rb=rb, t=t, vb2=vb2, \
                            hoff=hoff: \
                            E.matmul(avp[:, ca:cb],
                                     v2[vb2][:, t, hoff:hoff + 128],
                                     ptq[rb][:, ca:cb],
                                     start=(t == 0), stop=False)
                        if si == len(segs) - 1:
                            ptq_free[rb] = sc.op("t", "T", fn)
                        else:
                            sc.opn("t", fn)

                if band == 0:
                    sc.wait("t", f"kq{hb}", kqw)
                    if h == 0:
                        sc.wait("t", "mks", mkw)
                    sc.wait("t", f"v2_{vb2}", vw_cur[vb2])
                for t in range(tmax):
                    mmref, pb2, vbase, c0 = st_tile(t)
                    rb = t % 3
                    sc.wait("s", "T", mmref)
                    sc.wait("s", "T", ptq_free[rb])
                    spsB_free[pb2] = sc.op(
                        "s", "S", lambda E, pb2=pb2, rb=rb, c0=c0:
                        E.activation(ptq[rb][:, c0:1024], sps[pb2][:, c0:1024],
                                     AF.Exp, bias=0.0, scale=1.0))
                    erefs[t] = (spsB_free[pb2], rb, vbase, c0)
                    if t >= 1:
                        consume(t - 1)
                consume(tmax - 1)
                sc.opn("t", lambda E:
                       E.matmul(avp[:, 0:512], allzeros[:], ptq[0][:, 0:512],
                                start=False, stop=True))
                sc.opn("t", lambda E:
                       E.matmul(avp[:, 512:1024], allzeros[:], ptq[0][:, 512:1024],
                                start=False, stop=True))
                sc.opn("t", lambda E:
                       E.matmul(smtp[:, 0:512], allzeros[:], ptq[0][:, 0:512],
                                start=False, stop=True))
                avref = sc.op("t", "T", lambda E:
                              E.matmul(smtp[:, 512:1024], allzeros[:],
                                       ptq[0][:, 512:1024], start=False, stop=True))
                v2_free[vb2] = avref
                sc.wait("s", "T", avref)
                sc.wait("s", "S", lnt_free)
                smtp_evac = sc.op("s", "S", lambda E:
                                  E.activation(lnt[:], smtp[:], AF.Ln))
                sc.wait("s", "S", smtp_evac)
                sc.wait("s", "V", sums_free[bpar])
                rcref = sc.op("s", "S", lambda E, bpar=bpar:
                              E.activation(sums_sb[bpar][:], lnt[:], AF.Exp,
                                           scale=-1.0))
                lnt_free = rcref
                ab = bpar
                sc.wait("v", "S", rcref)
                sc.wait("v", "T", avref)
                sc.wait("v", f"avo{ab}", avout_free[ab])
                avp_free = sc.op("v", "V", lambda E, ab=ab, bpar=bpar:
                                 E.tensor_tensor(avout[ab][:], avp[:],
                                                 sums_sb[bpar][:], ALU.mult))
                sums_free[bpar] = avp_free
                vref = sc.cnt["V"]
                sc.wait("y", "V", vref)
                avout_free[ab] = sc.dma(
                    "y", f"avo{ab}", av_sp[h, :, band * 1024:(band + 1) * 1024],
                    avout[ab][:])

            def head_loads(h):
                hb2 = h % 2
                sc.wait("y", "T", kq_free[hb2])
                sc.dma("y", f"kq{hb2}", Kh[hb2][:], k_sp[h])
                kqn = sc.dma("y", f"kq{hb2}", Qh[hb2][:], q_sp[h])
                sc.wait("y", "T", v2_free[hb2])
                vw_cur[hb2] = sc.dma("y", f"v2_{hb2}", v2[hb2][:],
                                     v_sp.rearrange("(tt p) d -> p tt d", p=128)
                                     [:, :, h * 128:(h + 1) * 128])
                return kqn

            kqw = head_loads(0)
            for h in range(KH):
                hb = h % 2
                stage_STAV(h, 0, hb, kqw)
                kqw_next = head_loads(h + 1) if h + 1 < KH else 0
                stage_STAV(h, 1, hb, kqw)
                kq_free[hb] = sc.cnt["T"]
                kqw = kqw_next
            sc.barrier()
            sc.flush()

    if stop >= 6:
        phase_attn()

    def phase_outproj():
        es = ExitStack()
        with es:
            wtw = sc.cnt.get("wto", 0)
            bo_bc = es.enter_context(nc.sbuf_tensor("bo_bc_sb", [128, C], F32))
            bow = sc.dma("y", "bo", bo_bc[:], bo_bc_in)
            avs = [es.enter_context(nc.sbuf_tensor("avs0", [128, 16, 512], BF16))]
            xot = [es.enter_context(nc.sbuf_tensor(f"xot{i}", [128, C], F32))
                   for i in range(2)]
            y1f = [es.enter_context(nc.sbuf_tensor(f"y1f{i}", [128, C], F32))
                   for i in range(2)]
            y1nst = [es.enter_context(nc.sbuf_tensor(f"y1nst{i}", [128, C], BF16))
                     for i in range(2)]
            bn6 = es.enter_context(nc.sbuf_tensor("bn6", [128, 48], F32))
            mv2 = es.enter_context(nc.sbuf_tensor("mv2", [128, 4], F32))
            sd1 = es.enter_context(nc.sbuf_tensor("sd1", [128, 2], F32))
            rstd = es.enter_context(nc.sbuf_tensor("rstd", [128, 2], F32))
            nmr = es.enter_context(nc.sbuf_tensor("nmr", [128, 2], F32))
            py1 = [es.enter_context(nc.psum_tensor(f"py1_{i}", [128, 1024], F32))
                   for i in range(2)]
            ttp = [es.enter_context(nc.psum_tensor(f"ttp3{i}", [128, 1024], BF16))
                   for i in range(2)]
            py_free = [0, 0]
            ttp_free = [0, 0]
            y1f_free = [0, 0]
            y1n_free = [0, 0]
            xo_free = [0, 0]
            avs_free = [0, 0]
            pend_tr = []
            for tb in range(16):
                g4 = tb // 4
                if tb % 4 == 0:
                    abuf = 0
                    sc.wait("y", "T", avs_free[abuf])
                    for d in range(16):
                        avw = sc.dma("y", f"avs{abuf}", avs[abuf][:, d, :],
                                     av_sp[d, :, g4 * 512:(g4 + 1) * 512])
                xbuf = tb % 2
                sc.wait("y", "V", xo_free[xbuf])
                xw = sc.dma("y", f"xo{xbuf}", xot[xbuf][:],
                            xown[tb * 128:(tb + 1) * 128, :])
                abuf = 0
                toff = (tb % 4) * 128
                lastmm = [0, 0]
                for fh in range(2):
                    pb = fh
                    sc.wait("t", "V", py_free[pb])
                    if tb % 4 == 0 and fh == 0:
                        sc.wait("t", f"avs{abuf}", avw)
                    if tb == 0 and fh == 0:
                        sc.wait("t", "wto", wtw)
                    for d in range(16):
                        for cpos in (0, 512):
                            last = (d == 15 and cpos == 512)
                            fn = lambda E, pb=pb, d=d, c=cpos, fh=fh, ab=abuf, \
                                toff=toff: \
                                E.matmul(py1[pb][:, c:c + 512],
                                         avs[ab][:, d, toff:toff + 128],
                                         woT[:, d, fh * 1024 + c:fh * 1024 + c + 512],
                                         start=(d == 0), stop=(d == 15))
                            if last:
                                lastmm[fh] = sc.op("t", "T", fn)
                            else:
                                sc.opn("t", fn)
                if tb % 4 == 3:
                    avs_free[abuf] = lastmm[1]
                if pend_tr:
                    stage_TR(*pend_tr.pop(0))
                tp = tb % 2
                for fh in range(2):
                    pb = fh
                    sc.wait("v", "T", lastmm[fh])
                    if tb == 0 and fh == 0:
                        sc.wait("v", "bo", bow)
                    sc.wait("v", f"xo{xbuf}", xw)
                    sc.wait("v", "S", y1f_free[tp])
                    a1ref = sc.op("v", "V", lambda E, pb=pb, fh=fh, tp=tp:
                                  E.tensor_add(y1f[tp][:, fh * 1024:(fh + 1) * 1024],
                                               py1[pb][:],
                                               bo_bc[:, fh * 1024:(fh + 1) * 1024]))
                    sc.wait("v", "V", a1ref)
                    py_free[pb] = sc.op(
                        "v", "V", lambda E, fh=fh, xbuf=xbuf, tp=tp:
                        E.tensor_add(y1f[tp][:, fh * 1024:(fh + 1) * 1024],
                                     y1f[tp][:, fh * 1024:(fh + 1) * 1024],
                                     xot[xbuf][:, fh * 1024:(fh + 1) * 1024]))
                xo_free[xbuf] = py_free[1]
                sc.wait("v", "V", py_free[1])
                for kb in range(3):
                    sc.opn("v", lambda E, kb=kb, tp=tp:
                           E.bn_stats(bn6[:, tp * 24 + kb * 6:tp * 24 + (kb + 1) * 6],
                                      y1f[tp][:, kb * 512:(kb + 1) * 512]))
                bsref = sc.op("v", "V", lambda E, tp=tp:
                              E.bn_stats(bn6[:, tp * 24 + 18:tp * 24 + 24],
                                         y1f[tp][:, 1536:2048]))
                sc.wait("v", "V", bsref)
                bnref = sc.op("v", "V", lambda E, tp=tp:
                              E.bn_aggr(mv2[:, tp * 2:tp * 2 + 2],
                                        bn6[:, tp * 24:(tp + 1) * 24]))
                sc.wait("s", "V", bnref)
                sdref = sc.op("s", "S", lambda E, tp=tp:
                              E.activation(sd1[:, tp:tp + 1],
                                           mv2[:, tp * 2 + 1:tp * 2 + 2], AF.Sqrt,
                                           bias=eps_sb[:]))
                sc.wait("v", "S", sdref)
                rcref = sc.op("v", "V", lambda E, tp=tp:
                              E.reciprocal(rstd[:, tp:tp + 1], sd1[:, tp:tp + 1]))
                sc.wait("v", "V", rcref)
                vref = sc.op("v", "V", lambda E, tp=tp:
                             E.tensor_scalar(nmr[:, tp:tp + 1],
                                             mv2[:, tp * 2:tp * 2 + 1],
                                             rstd[:, tp:tp + 1], -1.0,
                                             ALU.mult, ALU.mult))
                sc.wait("s", "V", vref)
                sc.wait("s", "T", y1n_free[tp])
                sc.wait("s", "y1sp", sc.cnt.get("y1sp", 0))
                y1f_free[tp] = sc.op("s", "S", lambda E, tp=tp:
                                     E.activation(y1nst[tp][:], y1f[tp][:],
                                                  AF.Identity,
                                                  bias=nmr[:, tp:tp + 1],
                                                  scale=rstd[:, tp:tp + 1]))
                sc.wait("y", "S", y1f_free[tp])
                sc.dma("y", "y1sp", y1tm_sp[tb], y1nst[tp][:])

                def stage_TR(tb, tp):
                    sc.wait("t", "S", y1f_free[tp])
                    for fg in range(2):
                        pb4 = fg
                        sc.wait("t", "S", ttp_free[pb4])
                        for f8 in range(7):
                            f = fg * 8 + f8
                            sc.opn("t", lambda E, pb4=pb4, f8=f8, f=f, tp=tp:
                                   E.transpose(ttp[pb4][:, f8 * 128:(f8 + 1) * 128],
                                               y1nst[tp][:, f * 128:(f + 1) * 128],
                                               ident[:]))
                        f = fg * 8 + 7
                        tref = sc.op("t", "T", lambda E, pb4=pb4, f=f, tp=tp:
                                     E.transpose(ttp[pb4][:, 7 * 128:8 * 128],
                                                 y1nst[tp][:, f * 128:(f + 1) * 128],
                                                 ident[:]))
                        y1n_free[tp] = tref
                        sc.wait("s", "T", tref)
                        ttp_free[pb4] = sc.op(
                            "s", "S", lambda E, pb4=pb4, fg=fg, tb=tb:
                            E.activation(
                                y1nfm[:, fg * 8:(fg + 1) * 8,
                                      tb * 128:(tb + 1) * 128],
                                ttp[pb4][:].rearrange("p (a b) -> p a b", a=8),
                                AF.Identity))
                pend_tr.append((tb, tp))
            while pend_tr:
                stage_TR(*pend_tr.pop(0))
            sc.barrier()
            sc.flush()

    if stop >= 7:
        phase_outproj()
    woT_cm.__exit__(None, None, None)


    def phase_gateup():
        es = ExitStack()
        with es:
            wg_sb = [es.enter_context(nc.sbuf_tensor(f"wg_sb{i}", [128, 16, 512], BF16))
                     for i in range(2)]
            wu_sb = [es.enter_context(nc.sbuf_tensor(f"wu_sb{i}", [128, 16, 512], BF16))
                     for i in range(2)]
            tg = es.enter_context(nc.sbuf_tensor("tg", [128, 1024], BF16))
            tu = es.enter_context(nc.sbuf_tensor("tu", [128, 1024], BF16))
            gsl = es.enter_context(nc.sbuf_tensor("gsl", [128, 1024], BF16))
            hst = [es.enter_context(nc.sbuf_tensor(f"hst{i}", [128, 1024], BF16))
                   for i in range(2)]
            sy_bc = es.enter_context(nc.sbuf_tensor("sy_bc", [128, C], F32))
            gps = es.enter_context(nc.psum_tensor("gps", [128, 1024], F32))
            ups = es.enter_context(nc.psum_tensor("ups", [128, 1024], F32))
            syp = es.enter_context(nc.psum_tensor("syp", [128, 1024], F32))
            syref = 0
            for c4 in range(2):
                sc.wait("t", "V", syref)
                for f in range(16):
                    for cpos in (0, 512):
                        last = (f == 15 and cpos == 512)
                        fn = lambda E, f=f, c=cpos, c4=c4: \
                            E.matmul(syp[:, c:c + 512], allones[:],
                                     y1nfm[:, f, c4 * 1024 + c:c4 * 1024 + c + 512],
                                     start=(f == 0), stop=(f == 15))
                        if last:
                            smr = sc.op("t", "T", fn)
                        else:
                            sc.opn("t", fn)
                sc.wait("v", "T", smr)
                syref = sc.op("v", "V", lambda E, c4=c4:
                              E.tensor_copy(sy_bc[:, c4 * 1024:(c4 + 1) * 1024],
                                            syp[:]))
            wfree = [0, 0]
            gps_free = 0
            ups_free = 0
            tg_free = 0
            tu_free = 0
            gsl_free = 0
            hst_free = [0, 0]
            for icg in range(16):
                buf = icg % 2
                sc.wait("g", "T", wfree[buf])
                for f in range(16):
                    sc.dma("g", f"gu{buf}", wg_sb[buf][:, f, :],
                           gwT[f * 128:(f + 1) * 128, icg * 512:(icg + 1) * 512])
                    ww = sc.dma("g", f"gu{buf}", wu_sb[buf][:, f, :],
                                uwT[f * 128:(f + 1) * 128, icg * 512:(icg + 1) * 512])
                for ici in range(4):
                    ic = icg * 4 + ici
                    for th in range(2):
                        sc.wait("t", "V", gps_free)
                        if ici == 0 and th == 0:
                            sc.wait("t", f"gu{buf}", ww)
                            if icg == 0:
                                sc.wait("t", "V", syref)
                        for f in range(16):
                            for cpos in (0, 512):
                                last = (f == 15 and cpos == 512)
                                fn = lambda E, f=f, c=cpos, th=th, buf=buf, ici=ici: \
                                    E.matmul(gps[:, c:c + 512],
                                             wg_sb[buf][:, f, ici * 128:(ici + 1) * 128],
                                             y1nfm[:, f, th * 1024 + c:th * 1024 + c + 512],
                                             start=(f == 0), stop=(f == 15))
                                if last:
                                    gmref = sc.op("t", "T", fn)
                                else:
                                    sc.opn("t", fn)
                        sc.wait("v", "T", gmref)
                        sc.wait("v", "S", tg_free)
                        gps_free = sc.op(
                            "v", "V", lambda E, th=th, ic=ic:
                            E.scalar_tensor_tensor(
                                tg[:], sy_bc[:, th * 1024:(th + 1) * 1024],
                                -128.0, gps[:], ALU.mult, ALU.add))
                        sc.wait("s", "V", gps_free)
                        sc.wait("s", "V", gsl_free)
                        tg_free = sc.op("s", "S", lambda E, ic=ic:
                                        E.activation(gsl[:], tg[:], AF.Silu,
                                                     scale=gsc_sb[:, ic:ic + 1]))
                        sc.wait("t", "V", ups_free)
                        for f in range(16):
                            for cpos in (0, 512):
                                last = (f == 15 and cpos == 512)
                                fn = lambda E, f=f, c=cpos, th=th, buf=buf, ici=ici: \
                                    E.matmul(ups[:, c:c + 512],
                                             wu_sb[buf][:, f, ici * 128:(ici + 1) * 128],
                                             y1nfm[:, f, th * 1024 + c:th * 1024 + c + 512],
                                             start=(f == 0), stop=(f == 15))
                                if last:
                                    umref = sc.op("t", "T", fn)
                                else:
                                    sc.opn("t", fn)
                        if ici == 3 and th == 1:
                            wfree[buf] = umref
                        hb = th % 2
                        sc.wait("v", "T", umref)
                        sc.wait("v", "V", tu_free)
                        ups_free = sc.op(
                            "v", "V", lambda E, th=th, ic=ic:
                            E.scalar_tensor_tensor(
                                tu[:], sy_bc[:, th * 1024:(th + 1) * 1024],
                                -128.0, ups[:], ALU.mult, ALU.add))
                        sc.wait("v", "V", ups_free)
                        sc.wait("v", "S", tg_free)
                        sc.wait("v", f"hsp{hb}", hst_free[hb])
                        tu_free = sc.op(
                            "v", "V", lambda E, hb=hb, ic=ic:
                            E.scalar_tensor_tensor(hst[hb][:], tu[:],
                                                   usc_sb[:, ic:ic + 1], gsl[:],
                                                   ALU.mult, ALU.mult))
                        gsl_free = tu_free
                        vref = sc.cnt["V"]
                        sc.wait("y", "V", vref)
                        sc.dma("y", f"hsp{hb}",
                               h_sp[2 * th, :, ic * 512:(ic + 1) * 512],
                               hst[hb][:, 0:512])
                        hst_free[hb] = sc.dma(
                            "y", f"hsp{hb}",
                            h_sp[2 * th + 1, :, ic * 512:(ic + 1) * 512],
                            hst[hb][:, 512:1024])
            sc.barrier()
            sc.flush()

    if stop >= 9:
        phase_gateup()

    def phase_down():
        es = ExitStack()
        with es:
            wdm = [es.enter_context(nc.sbuf_tensor(f"wdm{i}", [128, 32, 512], BF16))
                   for i in range(3)]
            hsb = [es.enter_context(nc.sbuf_tensor(f"hsb{i}", [128, 32, 512], BF16))
                   for i in range(2)]
            y1t = [es.enter_context(nc.sbuf_tensor(f"y1t{i}", [128, 512], BF16))
                   for i in range(4)]
            dsbc = es.enter_context(nc.sbuf_tensor("dsbc_sb", [128, C], F32))
            y2row = es.enter_context(nc.sbuf_tensor("y2row", [128, C], BF16))
            yfsl = es.enter_context(nc.sbuf_tensor("yfsl", [128, 512], F32))
            yfb = [es.enter_context(nc.sbuf_tensor(f"yfb{i}", [128, 512], BF16))
                   for i in range(2)]
            shcol = es.enter_context(nc.sbuf_tensor("shcol", [128, 16], F32))
            bnacc = es.enter_context(nc.sbuf_tensor("bnacc", [128, 16 * 24], F32))
            mv2 = es.enter_context(nc.sbuf_tensor("mv2b", [128, 2], F32))
            sd1 = es.enter_context(nc.sbuf_tensor("sd1b", [128, 1], F32))
            rstd = es.enter_context(nc.sbuf_tensor("rstdb", [128, 1], F32))
            nmr = es.enter_context(nc.sbuf_tensor("nmrb", [128, 1], F32))
            outst = [es.enter_context(nc.sbuf_tensor(f"outst{i}", [128, 1024], F32))
                     for i in range(2)]
            y2p = [es.enter_context(nc.psum_tensor(f"y2p{i}", [128, 512], F32))
                   for i in range(4)]
            shp = [es.enter_context(nc.psum_tensor(f"shp{i}", [128, 1], F32))
                   for i in range(4)]
            dsw = sc.dma("y", "dsb", dsbc[:], dsbc_in)

            wdm_freeT = [0, 0, 0]
            wload = [0, 0, 0]
            hsb_freeT = [0, 0]
            hload = [0, 0]
            y1t_free = [0, 0, 0, 0]
            y2p_freeV = [0, 0, 0, 0]
            yfb_free = [0, 0]
            y2row_free = 0
            outst_free = [0, 0]
            yfsl_free = 0

            def wd_load(wc):
                cg_, hh_ = wc // 2, wc % 2
                sl = wc % 3
                sc.wait("g", "T", wdm_freeT[sl])
                wload[sl] = sc.dma("g", f"wdl{sl}", wdm[sl][:],
                                   dwM[:, hh_ * 32:(hh_ + 1) * 32,
                                       cg_ * 512:(cg_ + 1) * 512])

            def h_load(hc):
                r = hc % 8
                g4_, hh_ = r // 2, r % 2
                sl = hc % 2
                sc.wait("y", "T", hsb_freeT[sl])
                hload[sl] = sc.dma(
                    "y", f"hsb{sl}", hsb[sl][:],
                    h_sp[g4_].rearrange("p (ic c) -> p ic c", ic=64)
                    [:, hh_ * 32:(hh_ + 1) * 32, :])

            wd_load(0)
            wd_load(1)
            h_load(0)
            h_load(1)
            for cg in range(4):
                for g4 in range(4):
                    ylw = [0, 0, 0, 0]
                    for tb4 in range(4):
                        tb = g4 * 4 + tb4
                        sc.wait("y", "V", y1t_free[tb4])
                        ylw[tb4] = sc.dma("y", f"y1t{tb4}", y1t[tb4][:],
                                          y1tm_sp[tb][:, cg * 512:(cg + 1) * 512])
                    mmref = [0, 0, 0, 0]
                    shmm = [0, 0, 0, 0]
                    for hh in range(2):
                        wsl = (cg * 2 + hh) % 3
                        hsl = (cg * 8 + g4 * 2 + hh) % 2
                        sc.wait("t", f"wdl{wsl}", wload[wsl])
                        sc.wait("t", f"hsb{hsl}", hload[hsl])
                        for tb4 in range(4):
                            tb = g4 * 4 + tb4
                            if hh == 0:
                                sc.wait("t", "V", y2p_freeV[tb4])
                            for i2 in range(32):
                                lastm = (hh == 1 and i2 == 31)
                                fn = lambda E, tb4=tb4, hsl=hsl, wsl=wsl, \
                                    i2=i2, hh=hh: \
                                    E.matmul(y2p[tb4][:],
                                             hsb[hsl][:, i2,
                                                      tb4 * 128:(tb4 + 1) * 128],
                                             wdm[wsl][:, i2, :],
                                             start=(hh == 0 and i2 == 0),
                                             stop=(hh == 1 and i2 == 31))
                                if lastm:
                                    mmref[tb4] = sc.op("t", "T", fn)
                                elif i2 == 31:
                                    sc.op("t", "T", fn)
                                else:
                                    sc.opn("t", fn)
                                if cg == 0:
                                    fs = lambda E, tb=tb, tb4=tb4, hsl=hsl, \
                                        i2=i2, hh=hh: \
                                        E.matmul(shp[tb4][:],
                                                 hsb[hsl][:, i2,
                                                          tb4 * 128:(tb4 + 1) * 128],
                                                 allones[:, 0:1],
                                                 start=(hh == 0 and i2 == 0),
                                                 stop=(hh == 1 and i2 == 31))
                                    if hh == 1 and i2 == 31:
                                        shmm[tb4] = sc.op("t", "T", fs)
                                    elif i2 == 31:
                                        sc.op("t", "T", fs)
                                    else:
                                        sc.opn("t", fs)
                        hsb_freeT[hsl] = sc.cnt["T"]
                        hc = cg * 8 + g4 * 2 + hh
                        if hc + 2 < 32:
                            h_load(hc + 2)
                        if g4 == 3:
                            wc = cg * 2 + hh
                            wdm_freeT[wsl] = sc.cnt["T"]
                            if wc + 2 < 8:
                                wd_load(wc + 2)
                    for tb4 in range(4):
                        tb = g4 * 4 + tb4
                        rbt = tb % 2
                        if cg == 0:
                            sc.wait("v", "T", shmm[tb4])
                            shv = sc.op("v", "V", lambda E, tb=tb, tb4=tb4:
                                        E.tensor_scalar_mul(shcol[:, tb:tb + 1],
                                                            shp[tb4][:],
                                                            -128.0))
                            sc.wait("v", "V", shv)
                        sc.wait("v", "T", mmref[tb4])
                        sc.wait("v", f"y1t{tb4}", ylw[tb4])
                        if cg == 0 and tb == 0:
                            sc.wait("v", "dsb", dsw)
                        sc.wait("v", "V", yfsl_free)
                        v1 = sc.op("v", "V", lambda E, tb4=tb4, cg=cg, tb=tb:
                                   E.scalar_tensor_tensor(
                                       yfsl[:], y2p[tb4][:], shcol[:, tb:tb + 1],
                                       dsbc[:, cg * 512:(cg + 1) * 512],
                                       ALU.add, ALU.mult))
                        y2p_freeV[tb4] = v1
                        sc.wait("v", "V", v1)
                        sc.wait("v", f"yspl{rbt}", yfb_free[rbt])
                        v2 = sc.op("v", "V", lambda E, rbt=rbt, tb4=tb4:
                                   E.tensor_add(yfb[rbt][:], yfsl[:],
                                                y1t[tb4][:]))
                        yfsl_free = v2
                        y1t_free[tb4] = v2
                        sc.wait("v", "V", v2)
                        v3 = sc.op("v", "V", lambda E, rbt=rbt, tb=tb, cg=cg:
                                   E.bn_stats(bnacc[:, tb * 24 + cg * 6:
                                                     tb * 24 + cg * 6 + 6],
                                              yfb[rbt][:]))
                        sc.wait("y", "V", v3)
                        yfb_free[rbt] = sc.dma(
                            "y", f"yspl{rbt}",
                            y2_sp[tb][:, cg * 512:(cg + 1) * 512], yfb[rbt][:])
                        if cg == 3:
                            sc.wait("v", "V", v3)
                            bnref = sc.op("v", "V", lambda E, tb=tb:
                                          E.bn_aggr(mv2[:],
                                                    bnacc[:, tb * 24:
                                                          (tb + 1) * 24]))
                            sc.wait("s", "V", bnref)
                            sdref = sc.op("s", "S", lambda E:
                                          E.activation(sd1[:], mv2[:, 1:2],
                                                       AF.Sqrt, bias=eps_sb[:]))
                            sc.wait("v", "S", sdref)
                            rcref = sc.op("v", "V", lambda E:
                                          E.reciprocal(rstd[:], sd1[:]))
                            sc.wait("v", "V", rcref)
                            vref = sc.op("v", "V", lambda E:
                                         E.tensor_scalar(nmr[:], mv2[:, 0:1],
                                                         rstd[:], -1.0,
                                                         ALU.mult, ALU.mult))
                            sc.wait("y", "S", y2row_free)
                            sc.wait("y", "yspl0", sc.cnt.get("yspl0", 0))
                            sc.wait("y", "yspl1", sc.cnt.get("yspl1", 0))
                            y2rw = sc.dma("y", "y2r", y2row[:], y2_sp[tb])
                            sc.wait("s", "y2r", y2rw)
                            sc.wait("s", "V", vref)
                            row0 = tb * 128
                            for half in range(2):
                                sc.wait("s", f"outd{half}", outst_free[half])
                                apref = sc.op(
                                    "s", "S", lambda E, half=half:
                                    E.activation(outst[half][:],
                                                 y2row[:, half * 1024:
                                                       (half + 1) * 1024],
                                                 AF.Identity, bias=nmr[:],
                                                 scale=rstd[:]))
                                if half == 1:
                                    y2row_free = apref
                                sc.wait("y", "S", apref)
                                outst_free[half] = sc.dma(
                                    "y", f"outd{half}",
                                    out[row0:row0 + 128,
                                        half * 1024:(half + 1) * 1024],
                                    outst[half][:])
            sc.wait("y", "outd0", sc.cnt.get("outd0", 0))
            sc.wait("y", "outd1", sc.cnt.get("outd1", 0))
            sc.barrier()
            sc.flush()

    y1nfm_cm.__exit__(None, None, None)
    if stop >= 10:
        phase_down()
    eso.close()
    return nc


_CACHE = {}
_LEAK = []


def _host_tables(par):
    th = np.float32(10000.0) ** (-(np.arange(DH, dtype=np.float32) / np.float32(DH)))
    pos_all = np.arange(T, dtype=np.float32)
    ang = pos_all[:, None] * th[None, :]
    cosk, sink = np.cos(ang).astype(np.float32), np.sin(ang).astype(np.float32)
    ckA = np.concatenate([cosk.T, sink.T], axis=0)
    ckB = np.concatenate([sink.T, cosk.T], axis=0)
    own_blocks = [2 * j + par for j in range(NSLOT)]
    pos_own = np.concatenate([np.arange(g * 128, (g + 1) * 128) for g in own_blocks])
    ango = pos_own[:, None].astype(np.float32) * th[None, :]
    cosq, sinq = np.cos(ango).astype(np.float32), np.sin(ango).astype(np.float32)
    s = np.float32(1.0 / math.sqrt(D))
    cqA = np.concatenate([cosq.T, sinq.T], axis=0) * s
    cqB = np.concatenate([sinq.T, cosq.T], axis=0) * s
    M = np.zeros((NSLOT, 128, 256), dtype=np.float32)
    kk = np.arange(256)[None, :]
    qq = np.arange(128)[:, None]
    for j in range(NSLOT):
        lim = qq + (0 if par == 0 else 128)
        M[j] = np.where(kk <= lim, 0.0, NEG)
    masks = M.transpose(1, 0, 2).reshape(128, NSLOT * 256)
    MT = np.zeros((NSLOT, 2, 128, 128), dtype=np.float32)
    for j in range(NSLOT):
        for u in range(2):
            MT[j, u] = M[j][:, u * 128:(u + 1) * 128].T
    masksT = MT.transpose(2, 0, 1, 3).reshape(128, NSLOT * 256)
    return ckA, ckB, cqA, cqB, masks, masksT, own_blocks


def kernel(L0_attn_x, L0_attn_w_qkv, L0_attn_b_qkv, L0_attn_w_out, L0_attn_b_out,
           L0_ffn_gate_w, L0_ffn_gate_scale, L0_ffn_up_w, L0_ffn_up_scale,
           L0_ffn_down_w, L0_ffn_down_scale):
    x = np.asarray(L0_attn_x, dtype=np.float32)
    wqkv = np.asarray(L0_attn_w_qkv, dtype=np.float32)
    bqkv = np.asarray(L0_attn_b_qkv, dtype=np.float32)
    wout = np.ascontiguousarray(np.asarray(L0_attn_w_out, dtype=np.float32))
    bout = np.asarray(L0_attn_b_out, dtype=np.float32)
    gwn = np.ascontiguousarray(np.asarray(L0_ffn_gate_w, dtype=np.int32))
    gscale = np.asarray(L0_ffn_gate_scale, dtype=np.float32)
    uwn = np.ascontiguousarray(np.asarray(L0_ffn_up_w, dtype=np.int32))
    uscale = np.asarray(L0_ffn_up_scale, dtype=np.float32)
    dwn = np.ascontiguousarray(np.asarray(L0_ffn_down_w, dtype=np.int32))
    dscale = np.asarray(L0_ffn_down_scale, dtype=np.float32)

    pair = np.concatenate([np.arange(0, 128, 2), np.arange(1, 128, 2)])
    perm = (np.arange(H)[:, None] * 128 + pair[None, :]).reshape(-1)
    wqT_h = np.ascontiguousarray(wqkv[0:C][perm].T)
    wkT_h = np.ascontiguousarray(wqkv[C:2 * C][perm].T)
    wvT_h = np.ascontiguousarray(wqkv[2 * C:3 * C].T)
    woT_h = np.ascontiguousarray(wout.T)
    bq_r = np.ascontiguousarray(bqkv[0:C][perm].reshape(H, 128).T)
    bk_r = np.ascontiguousarray(bqkv[C:2 * C][perm].reshape(H, 128).T)
    bv_bc = np.ascontiguousarray(np.broadcast_to(bqkv[2 * C:3 * C], (128, C)))
    bo_bc = np.ascontiguousarray(np.broadcast_to(bout, (128, C)))
    gwT_h = np.ascontiguousarray(gwn.T)
    uwT_h = np.ascontiguousarray(uwn.T)
    dwM_h = np.ascontiguousarray(dwn.T.reshape(64, 128, C).transpose(1, 0, 2))
    gsc = np.ascontiguousarray(gscale.reshape(64, 128).T)
    usc = np.ascontiguousarray(uscale.reshape(64, 128).T)
    dsbc_h = np.ascontiguousarray(np.broadcast_to(dscale, (128, C)))
    eye = np.eye(64, dtype=np.float32)
    m1 = np.concatenate([eye, -eye], axis=0)
    m2 = np.concatenate([eye, eye], axis=0)
    identity = np.eye(128, dtype=np.float32)

    _stop = int(os.environ.get("KSTOP", "99"))
    if "nc" not in _CACHE:
        _CACHE["nc"] = build_program(_stop)
    nc = _CACHE["nc"]

    tabs = {par: _host_tables(par) for par in range(2)}
    in_maps = []
    own_rows = {}
    for c in range(NCORE):
        b, par = c // 2, c % 2
        ckA, ckB, cqA, cqB, masks, masksT, own_blocks = tabs[par]
        rows = np.concatenate([np.arange(g * 128, (g + 1) * 128) for g in own_blocks])
        own_rows[c] = rows
        xo = np.ascontiguousarray(x[b][rows])
        in_maps.append({
            "xbT": np.ascontiguousarray(x[b].T),
            "xown": xo,
            "xownT": np.ascontiguousarray(xo.T),
            "wqT": wqT_h, "wkT": wkT_h, "wvT_in": wvT_h, "woT_in": woT_h,
            "bq_r": bq_r, "bk_r": bk_r, "bv_bc": bv_bc, "bo_bc": bo_bc,
            "gwT": gwT_h, "uwT": uwT_h, "dwM": dwM_h,
            "gsc": gsc, "usc": usc, "dsbc": dsbc_h,
            "ckA": np.ascontiguousarray(ckA), "ckB": np.ascontiguousarray(ckB),
            "cqA": np.ascontiguousarray(cqA), "cqB": np.ascontiguousarray(cqB),
            "m1_in": m1, "m2_in": m2, "ident_in": identity,
            "masks_in": np.ascontiguousarray(masks),
            "masksT_in": np.ascontiguousarray(masksT),
        })

    res = run_bass_kernel_spmd(nc, in_maps, core_ids=list(range(NCORE)))
    outf = np.empty((B, T, C), dtype=np.float32)
    for c in range(NCORE):
        b = c // 2
        outf[b][own_rows[c]] = np.asarray(res.results[c]["out"])
    return outf
